# version 08
# baseline (speedup 1.0000x reference)
import os
import contextlib
import numpy as np
import ml_dtypes
import concourse.bass as bass
import concourse.mybir as mybir
from concourse.bass_utils import run_bass_kernel_spmd

F32 = mybir.dt.float32
BF16 = mybir.dt.bfloat16
AF = mybir.ActivationFunctionType
ALU = mybir.AluOpType
AX = mybir.AxisListType

T = 2048
D = 1024
CH = 256
NCH = T // CH
NEG = -2.0 ** 32 + 1.0

DBG = int(os.environ.get("KDBG", "0"))
KRACE = int(os.environ.get("KRACE", "0"))
KPH = int(os.environ.get("KPH", "4"))

ENGINES = ("tensor", "vector", "scalar", "gpsimd", "sync")


class Emitter:
    def __init__(self, nc, stack):
        self.nc = nc
        self.q = {e: [] for e in ENGINES}
        self.chans = {}
        self.waited = {e: {} for e in ENGINES}
        self.stack = stack

    def chan(self, name, step=1):
        sem = self.stack.enter_context(self.nc.semaphore(name))
        self.chans[name] = [sem, 0, step]

    def tok(self, ch):
        return self.chans[ch][1]

    def emit(self, eng, fn, inc=None):
        tok = None
        if inc is not None:
            c = self.chans[inc]
            c[1] += c[2]
            tok = c[1]
        self.q[eng].append((fn, inc))
        return tok

    def wait(self, eng, ch, tok):
        if tok is None or tok <= 0:
            return
        w = self.waited[eng]
        if w.get(ch, 0) >= tok:
            return
        w[ch] = tok
        sem = self.chans[ch][0]
        self.q[eng].append((lambda e, s=sem, t=tok: e.wait_ge(s, t), "__wait__"))

    def snap(self):
        return {ch: c[1] for ch, c in self.chans.items()}

    def wait_snap(self, eng, snap):
        for ch, tok in snap.items():
            if ch.startswith("_"):
                continue
            self.wait(eng, ch, tok)

    def emit_s(self, eng, fn):
        ch = {"vector": "V", "scalar": "S", "tensor": "T", "gpsimd": "G"}[eng]
        tok = self.emit(eng, fn, inc=ch)
        self.wait(eng, ch, tok)
        return tok

    def barrier(self, engines=ENGINES):
        s = self.snap()
        for e in engines:
            self.wait_snap(e, s)

    def replay(self):
        qs = self.q
        chans = self.chans
        selfsems = {}
        if KRACE:
            for e in ENGINES:
                selfsems[e] = self.stack.enter_context(self.nc.semaphore("self_" + e))

        rcount = {ch: 0 for ch in chans}

        def mk(engname):
            def body(eng):
                total = [0]
                prev_chan = [None]
                for fn, inc in qs[engname]:
                    if inc == "__wait__":
                        fn(eng)
                        continue
                    if KRACE:
                        if total[0] > 0:
                            eng.wait_ge(selfsems[engname], total[0])
                        if prev_chan[0] is not None:
                            eng.wait_ge(chans[prev_chan[0]][0], rcount[prev_chan[0]])
                            prev_chan[0] = None
                    ins = fn(eng)
                    if inc is not None:
                        ins.then_inc(chans[inc][0], chans[inc][2])
                        rcount[inc] += chans[inc][2]
                    if KRACE:
                        if inc is None:
                            step = 16 if "DMA" in type(ins.ins).__name__ or "Dma" in type(ins.ins).__name__ else 1
                            ins.then_inc(selfsems[engname], step)
                            total[0] += step
                        else:
                            prev_chan[0] = inc
            return body
        with self.nc.Block() as block:
            block.tensor(mk("tensor"))
            block.vector(mk("vector"))
            block.scalar(mk("scalar"))
            block.gpsimd(mk("gpsimd"))
            block.sync(mk("sync"))


class Psum:
    def __init__(self, E, pt):
        self.E = E
        self.pt = pt
        self.state = [(None, 0)] * 8
        self.rr = 0
        self.reserved = set()

    def acquire(self, n=1):
        for _ in range(16):
            if self.rr + n > 8:
                self.rr = 0
            slots = list(range(self.rr, self.rr + n))
            if any(s in self.reserved for s in slots):
                self.rr = max(s for s in slots if s in self.reserved) + 1
                continue
            break
        else:
            raise RuntimeError("psum reserve deadlock")
        self.rr = (self.rr + n) % 8
        for s in slots:
            ch, tok = self.state[s]
            if ch is not None:
                self.E.wait("tensor", ch, tok)
        return slots

    def reserve(self, slots):
        self.reserved.update(slots)

    def unreserve(self, slots):
        self.reserved.difference_update(slots)

    def release(self, slots, ch, tok):
        for s in slots:
            self.state[s] = (ch, tok)


def bfv(ap):
    return ap.bitcast(BF16)


def build_nc():
    nc = bass.Bass("TRN2", target_bir_lowering=False, detect_race_conditions=bool(KRACE))
    stack = contextlib.ExitStack()

    def din(name, shape, dt=F32):
        return nc.dram_tensor(name, shape, dt, kind="ExternalInput").ap()

    def dint(name, shape, dt=F32):
        return nc.dram_tensor(name, shape, dt, kind="Internal").ap()

    x = din("x", [T, D])
    cw = din("cw", [128, 8, 4]); cb = din("cb", [128, 8])
    Afm = din("Afm", [128, 128]); Dfm = din("Dfm", [128, 8])
    bpwT = din("bpwT", [D, 16]); cpwT = din("cpwT", [D, 16])
    bcb2 = din("bcb2", [16, 2])
    gwT = din("gwT", [D, D]); owT = din("owT", [D, D])
    gb = din("gb", [128, 8]); ob = din("ob", [128, 8])
    lng = din("lng", [128, 8]); lnb = din("lnb", [128, 8])
    wr = din("wr", [D, 4])
    w1 = din("w1", [4, D, 4096])
    b1fm = din("b1fm", [128, 128])
    woB = din("woB", [8, 4096, 128])
    bofm = din("bofm", [128, 8])
    g2 = din("g2", [128, 8]); b2 = din("b2", [128, 8])
    wq = din("wq", [D, D]); wk = din("wk", [D, D]); wv = din("wv", [D, D])
    bqfm = din("bqfm", [128, 8]); bkfm = din("bkfm", [128, 8]); bvfm = din("bvfm", [128, 8])
    gamv = din("gamv", [128, 8]); betv = din("betv", [128, 8])
    g8T = din("g8T", [D, 8]); gb8 = din("gb8", [128, 8])
    W13 = din("W13", [2, 9, D, 2048])
    W2 = din("W2", [9, 2048, D])
    bvrep = din("bvrep", [128, 1024])
    identf = din("identf", [128, 128])
    trilT = din("trilT", [128, 128], BF16)
    ones1 = din("ones1", [1, 128])

    out = nc.dram_tensor("out", [T, D], F32, kind="ExternalOutput").ap()
    dbg = {}
    if DBG:
        for nm in ("d_a", "d_b", "d_c"):
            dbg[nm] = nc.dram_tensor(nm, [128, 16384], F32, kind="ExternalOutput").ap()
        dbg["d_r"] = nc.dram_tensor("d_r", [8, 2048], BF16, kind="ExternalOutput").ap()
        dbg["d_l"] = nc.dram_tensor("d_l", [128, 16384], BF16, kind="ExternalOutput").ap()
        dbg["d_g"] = nc.dram_tensor("d_g", [128, 4096], BF16, kind="ExternalOutput").ap()

    repscr = dint("repscr", [16, 512])
    maskscr = dint("maskscr", [4, 256], BF16)
    rowscr = dint("rowscr", [1, 2048])
    cfmscr = dint("cfmscr", [8, 2048], BF16)
    strmspill = dint("strmspill", [128, 16384])
    strm2d = dint("strm2d", [128, 16384])
    w1s = dint("w1s", [32, 128, 4096], BF16)
    wos = dint("wos", [8, 128, 4096], BF16)
    gws = dint("gws", [8, 128, 1024], BF16)
    ows = dint("ows", [8, 128, 1024], BF16)

    AR = 51712
    arena_t = stack.enter_context(nc.sbuf_tensor("arena", [128, AR], F32))
    consts = stack.enter_context(nc.sbuf_tensor("consts", [128, 832], F32))
    cbias = stack.enter_context(nc.sbuf_tensor("cbias", [128, 448], F32))
    pt = stack.enter_context(nc.psum_tensor("pt", [128, 4096], F32))

    E = Emitter(nc, stack)
    for ch in ("T", "V", "S", "G"):
        E.chan(ch, 1)
    for ch in ("DW", "DS", "DA"):
        E.chan(ch, 16)
    P = Psum(E, pt)

    def A(o, n):
        assert o + n <= AR, (o, n)
        return arena_t[:, o:o + n]

    c_identf = consts[:, 0:128]
    c_ones1 = consts[0:1, 128:256]
    c_onesb = bfv(consts[:, 256:257])[:, 0:1]
    c_tril = bfv(consts[:, 257:321])[:, 0:128]
    c_cw = consts[:, 322:354].rearrange("p (a b) -> p a b", a=8)
    c_cb = consts[:, 354:362]
    c_Aexp = consts[:, 362:490].rearrange("p (a b) -> p a b", a=8)
    c_Dfm = consts[:, 490:498]
    c_hst = consts[:, 498:626].rearrange("p (a b) -> p a b", a=8)
    c_onesf4 = consts[:, 626:630]
    c_ones64 = consts[64:65, 704:832]

    b_bpb = cbias[0:16, 0:1]; b_cpb = cbias[0:16, 1:2]
    b_gb = cbias[:, 2:10]; b_ob = cbias[:, 10:18]
    b_lng = cbias[:, 18:26]; b_lnb = cbias[:, 26:34]
    b_b1 = cbias[:, 34:162].rearrange("p (a b) -> p a b", a=4)
    b_bo = cbias[:, 162:170]
    b_g2 = cbias[:, 170:178]; b_b2 = cbias[:, 178:186]
    b_bq = cbias[:, 186:194]; b_bk = cbias[:, 194:202]; b_bv = cbias[:, 202:210]
    b_gam = cbias[:, 210:218]; b_bet = cbias[:, 218:226]
    b_gb8 = cbias[:, 226:234]

    bpw_t = stack.enter_context(nc.sbuf_tensor("bpw2", [128, 128], BF16))
    cpw_t = stack.enter_context(nc.sbuf_tensor("cpw2", [128, 128], BF16))
    wrf_t = stack.enter_context(nc.sbuf_tensor("wrf", [128, 32], F32))
    bpw = bpw_t[:, :].rearrange("p (a b) -> p a b", a=8)
    cpw = cpw_t[:, :].rearrange("p (a b) -> p a b", a=8)
    wrf = wrf_t[:, :].rearrange("p (a b) -> p a b", a=8)

    for dst, src in ((c_identf, identf), (c_ones1, ones1),
                     (c_cw, cw), (c_cb, cb), (c_Dfm, Dfm),
                     (cbias[0:16, 0:2], bcb2), (b_gb, gb), (b_ob, ob),
                     (b_lng, lng), (b_lnb, lnb), (b_b1, b1fm.rearrange("p (a b) -> p a b", a=4)),
                     (b_bo, bofm), (b_g2, g2), (b_b2, b2), (b_bq, bqfm), (b_bk, bkfm),
                     (b_bv, bvfm), (b_gam, gamv), (b_bet, betv), (b_gb8, gb8),
                     (consts[:, 362:490], Afm)):
        E.emit("sync", lambda e, dst=dst, src=src: e.dma_start(dst, src), inc="DS")
    E.emit("sync", lambda e: e.dma_start(c_tril, trilT), inc="DS")
    E.emit("sync", lambda e: e.dma_start(c_ones64, ones1), inc="DS")
    tDS0 = E.tok("DS")
    E.emit("gpsimd", lambda e: e.dma_start(bpw_t[:, :].rearrange("p (a b) -> p a b", a=8), bpwT.rearrange("(a p) b -> p a b", p=128)), inc="DW")
    E.emit("gpsimd", lambda e: e.dma_start(cpw_t[:, :].rearrange("p (a b) -> p a b", a=8), cpwT.rearrange("(a p) b -> p a b", p=128)), inc="DW")
    E.emit("sync", lambda e: e.dma_start(wrf_t[:, :].rearrange("p (a b) -> p a b", a=8), wr.rearrange("(a p) b -> p a b", p=128)), inc="DS")

    E.emit("vector", lambda e: e.memset(c_onesb, 1.0), inc="V")
    E.emit("vector", lambda e: e.memset(consts[:, 498:626], 0.0), inc="V")
    tok_o4 = E.emit("vector", lambda e: e.memset(c_onesf4, 1.0), inc="V")
    E.wait("scalar", "DS", tDS0)
    E.emit("scalar", lambda e: e.activation(consts[:, 362:490], consts[:, 362:490], AF.Exp), inc="S")
    tok_aexp = E.emit("scalar", lambda e: e.activation(consts[:, 362:490], consts[:, 362:490], AF.Exp, scale=-1.0), inc="S")

    o = 16384
    def alloc(n):
        nonlocal o
        r = o
        o += n
        assert o <= AR, o
        return r
    o_lnx1 = alloc(2072)
    o_lnxbf = alloc(1024)
    o_u = alloc(2048)
    o_ubf = alloc(1024)
    o_y = alloc(2048)
    o_Brep = alloc(2048)
    o_Crep = alloc(2048)
    o_scr = alloc(2048)
    o_tail = alloc(24)
    o_srow = alloc(1024)
    o_lnx2 = alloc(2048)
    o_lnx2bf = alloc(1024)
    o_inter = alloc(4096)
    o_gscr = alloc(256)
    o_zB = alloc(2048)
    o_zBbf = alloc(1024)
    o_w1r = alloc(4096)
    o_wor = alloc(4096)
    o_mrep = alloc(512)
    o_Bsc = alloc(64)

    strm = A(0, 16384).rearrange("p (a b) -> p a b", a=8)
    xst = A(o_lnx1, 1024)
    E.wait("sync", "DS", tDS0)
    for tt_ in range(16):
        tokd = E.emit("sync", lambda e, tt_=tt_: e.dma_start(xst, x[tt_ * 128:(tt_ + 1) * 128, :]), inc="DS")
        E.wait("tensor", "DS", tokd)
        for dd in range(8):
            slots = P.acquire(1)
            apo = pt[:, slots[0] * 512: slots[0] * 512 + 128]
            tokt = E.emit("tensor", lambda e, apo=apo, dd=dd: e.transpose(
                apo, xst[:, dd * 128:(dd + 1) * 128], c_identf), inc="T")
            E.wait("vector", "T", tokt)
            tokv = E.emit("vector", lambda e, apo=apo, dd=dd, tt_=tt_: e.tensor_copy(
                strm[:, dd, tt_ * 128:(tt_ + 1) * 128], apo), inc="V")
            P.release(slots, "V", tokv)
        E.wait("sync", "T", E.tok("T"))
    E.wait("vector", "T", E.tok("T"))
    E.wait("scalar", "T", E.tok("T"))


    def fm3(ap2d):
        return ap2d.rearrange("(a p) b -> p a b", p=128)

    def stage1(src, dst):
        na = src.shape[1]
        return E.emit("gpsimd", lambda e, src=src, dst=dst: e.dma_start(
            dst.rearrange("p (a b) -> p a b", a=na), src), inc="DW")

    for oo in range(8):
        stage1(fm3(gwT[:, oo * 128:(oo + 1) * 128]), gws[oo, :, :])
    for oo in range(8):
        tok_gwos = stage1(fm3(owT[:, oo * 128:(oo + 1) * 128]), ows[oo, :, :])
    w1_tok = {}
    for ee in range(4):
        for grp in range(8):
            w1_tok[ee * 8 + grp] = stage1(
                fm3(w1[ee, :, grp * 512:(grp + 1) * 512]), w1s[ee * 8 + grp, :, :])
    wo_tok = {}
    for oo in range(8):
        wo_tok[oo] = stage1(fm3(woB[oo, :, :]), wos[oo, :, :])

    lnx1 = A(o_lnx1, 2072).rearrange("p (a b) -> p a b", a=8)
    lnxbf = bfv(A(o_lnxbf, 1024)).rearrange("p (a b) -> p a b", a=8)
    ubuf = A(o_u, 2048).rearrange("p (a b) -> p a b", a=8)
    ubf = bfv(A(o_ubf, 1024)).rearrange("p (a b) -> p a b", a=8)
    ybuf = A(o_y, 2048).rearrange("p (a b) -> p a b", a=8)
    Brep = A(o_Brep, 2048).rearrange("p (a b) -> p a b", a=16)
    Crep = A(o_Crep, 2048).rearrange("p (a b) -> p a b", a=16)
    scr_uB = A(o_scr, 2048).rearrange("p (a b) -> p a b", a=16)
    scr_h = scr_uB
    scr_z = A(o_scr, 2048).rearrange("p (a b) -> p a b", a=8)
    tailb = A(o_tail, 24).rearrange("p (a b) -> p a b", a=8)
    srow = A(o_srow, 1024)
    bcs = arena_t[32:48, o_srow:o_srow + 512]
    mfm_st = bfv(arena_t[64:68, o_srow:o_srow + 128])
    lnx2 = A(o_lnx2, 2048).rearrange("p (a b) -> p a b", a=8)
    lnx2bf = bfv(A(o_lnx2bf, 1024)).rearrange("p (a b) -> p a b", a=8)
    interb = bfv(A(o_inter, 4096)).rearrange("p (a b) -> p a b", a=32)
    gscr = [bfv(A(o_gscr, 128)), bfv(A(o_gscr + 128, 128))]
    zB = A(o_zB, 2048).rearrange("p (a b) -> p a b", a=8)
    zBbf = bfv(A(o_zBbf, 1024)).rearrange("p (a b) -> p a b", a=8)
    mrep = bfv(A(o_mrep, 512)).rearrange("p (a b) -> p a b", a=4)
    lgsc = A(o_Bsc, 64)

    class Ring:
        def __init__(self, base, nslots, slotcols, eng="scalar", ch="DA"):
            self.off = [base + i * slotcols for i in range(nslots)]
            self.free = [(None, 0)] * nslots
            self.i = 0
            self.n = nslots
            self.eng = eng
            self.ch = ch

        def load(self, dma_fn):
            j = self.i
            self.i = (self.i + 1) % self.n
            ch, tok = self.free[j]
            if ch is not None:
                E.wait(self.eng, ch, tok)
            tokd = E.emit(self.eng, lambda e, off=self.off[j]: dma_fn(e, off), inc=self.ch)
            return j, self.off[j], tokd

        def consumed(self, j, ch, tok):
            self.free[j] = (ch, tok)

    w1ring = Ring(o_w1r, 2, 2048)
    woring = Ring(o_wor, 2, 2048)
    garing = Ring(o_Brep, 2, 512, eng="sync", ch="DS")

    def layer_norm(x_f32, ncols, eps, xbf_scr, sq_scr, out_f32=None, out_bf=None,
                   gcol=None, bcol=None, add_to=None, war_tok=None, srow_t=None):
        if srow_t is None:
            srow_t = srow
        if war_tok is not None:
            E.wait("vector", "T", war_tok)
            E.wait("scalar", "T", war_tok)
        nb = (2 * ncols + 511) // 512
        tokc = E.emit("vector", lambda e: e.tensor_copy(xbf_scr, x_f32), inc="V")
        E.wait("scalar", "V", tokc)
        toksq = E.emit("scalar", lambda e: e.activation(sq_scr, xbf_scr, AF.Square), inc="S")
        sl = P.acquire(nb)
        ps = pt[0:1, sl[0] * 512: sl[0] * 512 + 2 * ncols]
        E.wait("tensor", "V", tokc)
        E.wait("tensor", "S", toksq)
        tokt = None
        for dd in range(8):
            E.emit("tensor", lambda e, dd=dd: e.matmul(
                ps[:, 0:ncols], c_onesb, xbf_scr[:, dd, :], start=(dd == 0), stop=(dd == 7)))
        for dd in range(8):
            tokt = E.emit("tensor", lambda e, dd=dd: e.matmul(
                ps[:, ncols:2 * ncols], c_onesb, sq_scr[:, dd, :], start=(dd == 0), stop=(dd == 7)), inc="T")
        E.wait("vector", "T", tokt)
        mu = srow_t[0:1, 0:ncols]
        m2 = srow_t[0:1, ncols:2 * ncols]
        vpe = srow_t[0:1, 2 * ncols:3 * ncols]
        rstd = srow_t[0:1, 3 * ncols:4 * ncols]
        E.emit("vector", lambda e: e.tensor_scalar_mul(mu, ps[:, 0:ncols], 1.0 / D))
        E.emit("vector", lambda e: e.tensor_scalar_mul(m2, ps[:, ncols:2 * ncols], 1.0 / D))
        E.emit("vector", lambda e: e.tensor_mul(vpe, mu, mu))
        E.emit("vector", lambda e: e.scalar_tensor_tensor(vpe, vpe, -1.0, m2, ALU.mult, ALU.add))
        E.emit("vector", lambda e: e.tensor_scalar_add(vpe, vpe, float(eps)))
        tokv = E.emit("vector", lambda e: e.reciprocal(vpe, vpe), inc="V")
        P.release(sl, "V", tokv)
        E.wait("scalar", "V", tokv)
        toks = E.emit("scalar", lambda e: e.activation(rstd, vpe, AF.Sqrt), inc="S")
        E.wait("vector", "S", toks)
        tokn = E.emit("vector", lambda e: e.scalar_tensor_tensor(
            mu, mu, -1.0, rstd, ALU.mult, ALU.mult), inc="V")
        sl2 = P.acquire(nb)
        rp = pt[:, sl2[0] * 512: sl2[0] * 512 + 2 * ncols]
        E.wait("tensor", "V", tokn)
        onesrow = c_ones1 if srow_t.base_partition() == 0 else c_ones64
        E.emit("tensor", lambda e, onesrow=onesrow: e.matmul(
            rp[:, 0:ncols], onesrow, rstd, start=True, stop=True))
        tokt2 = E.emit("tensor", lambda e, onesrow=onesrow: e.matmul(
            rp[:, ncols:2 * ncols], onesrow, mu, start=True, stop=True), inc="T")
        E.wait("vector", "T", tokt2)
        rsr = rp[:, 0:ncols].unsqueeze(1).broadcast_to([128, 8, ncols])
        nmr = rp[:, ncols:2 * ncols].unsqueeze(1).broadcast_to([128, 8, ncols])
        if out_f32 is None and gcol is None and add_to is None:
            E.emit("vector", lambda e: e.tensor_mul(out_bf, x_f32, rsr))
            last = E.emit("vector", lambda e: e.tensor_add(out_bf, out_bf, nmr), inc="V")
            P.release(sl2, "V", last)
            return last
        tgt = out_f32 if out_f32 is not None else x_f32
        E.emit("vector", lambda e: e.tensor_mul(tgt, x_f32, rsr))
        last = E.emit("vector", lambda e: e.tensor_add(tgt, tgt, nmr), inc="V")
        if out_bf is not None:
            last = E.emit("vector", lambda e: e.tensor_copy(out_bf, tgt), inc="V")
        if gcol is not None:
            for dd in range(8):
                E.emit("vector", lambda e, dd=dd: e.tensor_scalar(
                    tgt[:, dd, :], tgt[:, dd, :], gcol[:, dd:dd + 1], bcol[:, dd:dd + 1],
                    op0=ALU.mult, op1=ALU.add))
            last = E.emit("vector", lambda e: e.tensor_add(add_to, add_to, tgt), inc="V")
        P.release(sl2, "V", last)
        return last

    snapA = [None]

    stA = {}

    def phaseA_head(c):
        if snapA[0] is None:
            pass
        else:
            for eng in ("vector", "scalar", "tensor", "sync"):
                E.wait_snap(eng, snapA[0])
        t0 = c * CH
        sc_strm = strm[:, :, t0:t0 + CH]
        stA["sc_strm"] = sc_strm
        lnx_body = lnx1[:, :, 3:3 + CH]
        layer_norm(sc_strm, CH, 1e-5, ubf, lnxbf, out_f32=lnx_body)
        if c == 0:
            E.emit("vector", lambda e: e.memset(lnx1[:, :, 0:3], 0.0))
        else:
            E.emit("vector", lambda e: e.tensor_copy(lnx1[:, :, 0:3], tailb))
        for dd in range(8):
            E.emit("vector", lambda e, dd=dd: e.tensor_scalar_mul(
                ubuf[:, dd, :], lnx1[:, dd, 0:CH], c_cw[:, dd, 0:1]))
            for j in (1, 2, 3):
                E.emit("vector", lambda e, dd=dd, j=j: e.scalar_tensor_tensor(
                    ubuf[:, dd, :], lnx1[:, dd, j:j + CH], c_cw[:, dd, j:j + 1],
                    ubuf[:, dd, :], ALU.mult, ALU.add))
        tokv = E.emit("vector", lambda e: e.tensor_copy(tailb, lnx1[:, :, CH:CH + 3]), inc="V")
        E.wait("scalar", "V", tokv)
        toks = None
        for dd in range(8):
            toks = E.emit("scalar", lambda e, dd=dd: e.activation(
                ubuf[:, dd, :], ubuf[:, dd, :], AF.Silu, bias=c_cb[:, dd:dd + 1]), inc="S")
        E.wait("vector", "S", toks)
        tokubf = E.emit("vector", lambda e: e.tensor_copy(ubf, ubuf), inc="V")
        toklbf = E.emit("vector", lambda e: e.tensor_copy(lnxbf, lnx_body), inc="V")
        sl = P.acquire(1)
        ps = pt[0:16, sl[0] * 512: sl[0] * 512 + 512]
        E.wait("tensor", "V", tokubf)
        tokt = None
        for dd in range(8):
            E.emit("tensor", lambda e, dd=dd, ps=ps: e.matmul(
                ps[:, 0:CH], bpw[:, dd, :], ubf[:, dd, :], start=(dd == 0), stop=(dd == 7)))
        for dd in range(8):
            tokt = E.emit("tensor", lambda e, dd=dd, ps=ps: e.matmul(
                ps[:, 256:256 + CH], cpw[:, dd, :], ubf[:, dd, :], start=(dd == 0), stop=(dd == 7)), inc="T")
        E.wait("vector", "T", tokt)
        E.emit("vector", lambda e, ps=ps: e.tensor_scalar_add(bcs[:, 0:CH], ps[:, 0:CH], b_bpb))
        toks2 = E.emit("vector", lambda e, ps=ps: e.tensor_scalar_add(
            bcs[:, 256:256 + CH], ps[:, 256:256 + CH], b_cpb), inc="V")
        P.release(sl, "V", toks2)
        E.wait("sync", "V", toks2)
        tokd = E.emit("sync", lambda e: e.dma_start(repscr[:, :], bcs), inc="DS")
        stA["tokd"] = tokd
        stA["toklbf"] = toklbf

    def phaseA_scan(c):
        tokd = stA["tokd"]
        lastred = None
        for half in (0, 1):
            hb = half * 128
            E.wait("sync", "DS", tokd)
            if lastred is not None:
                E.wait("sync", "V", lastred)
            E.emit("sync", lambda e, hb=hb: e.dma_start(
                Brep, bass.AP(repscr.tensor, hb, [[0, 128], [512, 16], [1, 128]])), inc="DS")
            tokC = E.emit("sync", lambda e, hb=hb: e.dma_start(
                Crep, bass.AP(repscr.tensor, 256 + hb, [[0, 128], [512, 16], [1, 128]])), inc="DS")
            E.wait("vector", "DS", tokC)
            for dd in range(8):
                ub_b = ubuf[:, dd, hb:hb + 128].unsqueeze(1).broadcast_to([128, 16, 128])
                E.emit("vector", lambda e, ub_b=ub_b: e.tensor_mul(scr_uB, ub_b, Brep))
                for n in range(16):
                    a0 = c_Aexp[:, dd, n:n + 1].broadcast_to([128, 128])
                    E.emit("vector", lambda e, a0=a0, n=n, dd=dd: e.tensor_tensor_scan(
                        scr_h[:, n, :], a0, scr_uB[:, n, :], c_hst[:, dd, n:n + 1],
                        ALU.mult, ALU.add))
                E.emit("vector", lambda e, dd=dd: e.tensor_copy(
                    c_hst[:, dd, :].unsqueeze(2), scr_h[:, :, 127:128]))
                E.emit("vector", lambda e: e.tensor_mul(scr_h, scr_h, Crep))
                lastred = E.emit("vector", lambda e, dd=dd, hb=hb: e.tensor_reduce(
                    ybuf[:, dd, hb:hb + 128].unsqueeze(2),
                    scr_h.rearrange("p a b -> p b a"), AX.X, ALU.add), inc="V")
        tok_ydu = None
        for dd in range(8):
            tok_ydu = E.emit("vector", lambda e, dd=dd: e.scalar_tensor_tensor(
                ybuf[:, dd, :], ubuf[:, dd, :], c_Dfm[:, dd:dd + 1], ybuf[:, dd, :],
                ALU.mult, ALU.add), inc="V")
        stA["tok_ydu"] = tok_ydu

    def phaseA_tail(c):
        sc_strm = stA["sc_strm"]
        tok_ydu = stA["tok_ydu"]
        toklbf = stA["toklbf"]
        if c == 0:
            E.wait("sync", "DW", tok_gwos)
        garing.i = 0
        garing.free = [("V", tok_ydu), ("V", tok_ydu)]
        E.wait("tensor", "V", toklbf)
        E.wait("scalar", "V", tok_ydu)
        toks = None
        for oo in range(8):
            j, offw, tokw = garing.load(lambda e, off, oo=oo: e.dma_start(
                bfv(A(off, 512)), gws[oo, :, :]))
            wap = bfv(A(offw, 512)).rearrange("p (a b) -> p a b", a=8)
            sl = P.acquire(1)
            psg = pt[:, sl[0] * 512: sl[0] * 512 + CH]
            E.wait("tensor", "DS", tokw)
            tokt = None
            for dd in range(8):
                tokt = E.emit("tensor", lambda e, dd=dd, wap=wap, psg=psg: e.matmul(
                    psg, wap[:, dd, :], lnxbf[:, dd, :], start=(dd == 0), stop=(dd == 7)), inc="T")
            E.wait("scalar", "T", tokt)
            toks = E.emit("scalar", lambda e, psg=psg, oo=oo: e.activation(
                scr_z[:, oo, :], psg, AF.Silu, bias=b_gb[:, oo:oo + 1]), inc="S")
            P.release(sl, "S", toks)
            garing.consumed(j, "T", tokt)
        E.wait("vector", "S", toks)
        E.emit("vector", lambda e: e.tensor_mul(ybuf, ybuf, scr_z))
        tokyb = E.emit("vector", lambda e: e.tensor_copy(ubf, ybuf), inc="V")
        E.wait("tensor", "V", tokyb)
        tokvz = None
        for oo in range(8):
            j, offw, tokw = garing.load(lambda e, off, oo=oo: e.dma_start(
                bfv(A(off, 512)), ows[oo, :, :]))
            wap = bfv(A(offw, 512)).rearrange("p (a b) -> p a b", a=8)
            sl = P.acquire(1)
            psg = pt[:, sl[0] * 512: sl[0] * 512 + CH]
            E.wait("tensor", "DS", tokw)
            tokt = None
            for dd in range(8):
                tokt = E.emit("tensor", lambda e, dd=dd, wap=wap, psg=psg: e.matmul(
                    psg, wap[:, dd, :], ubf[:, dd, :], start=(dd == 0), stop=(dd == 7)), inc="T")
            E.wait("vector", "T", tokt)
            tokvz = E.emit("vector", lambda e, psg=psg, oo=oo: e.scalar_tensor_tensor(
                lnx1[:, oo, 3:3 + CH], psg, b_ob[:, oo:oo + 1], lnx1[:, oo, 3:3 + CH],
                ALU.add, ALU.add), inc="V")
            P.release(sl, "V", tokvz)
            garing.consumed(j, "T", tokt)
        layer_norm(lnx1[:, :, 3:3 + CH], CH, 1e-5, lnxbf, ubf,
                   gcol=b_lng, bcol=b_lnb, add_to=sc_strm, war_tok=tokt)
        snapA[0] = E.snap()

    snapB = [None]
    snapBh = [None]

    def phaseB_head(c):
        if snapB[0] is not None:
            for eng in ("vector", "scalar", "tensor", "sync"):
                E.wait_snap(eng, snapB[0])
        t0 = c * CH
        sc_strm = strm[:, :, t0:t0 + CH]
        lnret = layer_norm(sc_strm, CH, 1e-5, lnx2bf, zBbf, out_f32=lnx2, out_bf=lnx2bf)
        E.wait("tensor", "V", lnret)
        tokm = None
        for ts_ in range(2):
            sl = P.acquire(1)
            psl = pt[:, sl[0] * 512: sl[0] * 512 + 8]
            tokt = None
            for dd in range(8):
                tokt = E.emit("tensor", lambda e, dd=dd, psl=psl, ts_=ts_: e.matmul(
                    psl[:, 0:4], lnx2[:, dd, ts_ * 128:(ts_ + 1) * 128], wrf[:, dd, :],
                    start=(dd == 0), stop=(dd == 7)), inc="T")
            E.wait("vector", "T", tokt)
            E.emit_s("vector", lambda e: e.memset(lgsc[:, 0:8], -1e30))
            E.emit_s("vector", lambda e, psl=psl: e.tensor_copy(lgsc[:, 0:4], psl[:, 0:4]))
            E.emit_s("vector", lambda e: e.reduce_max(lgsc[:, 8:9], lgsc[:, 0:8], AX.X))
            E.emit_s("vector", lambda e: e.tensor_tensor(
                lgsc[:, 16:24], lgsc[:, 0:8], lgsc[:, 8:9].broadcast_to([128, 8]), ALU.is_ge))
            E.emit_s("vector", lambda e: e.tensor_tensor_scan(
                lgsc[:, 24:32], c_onesf4[:, 0:1].broadcast_to([128, 8]), lgsc[:, 16:24],
                0.0, ALU.mult, ALU.add))
            E.emit_s("vector", lambda e: e.tensor_scalar(
                lgsc[:, 24:32], lgsc[:, 24:32], 1.0, None, op0=ALU.is_le))
            tokv = E.emit_s("vector", lambda e: e.tensor_mul(
                lgsc[:, 24:32], lgsc[:, 24:32], lgsc[:, 16:24]))
            P.release(sl, "V", tokv)
            sl2 = P.acquire(1)
            pst = pt[0:4, sl2[0] * 512: sl2[0] * 512 + 128]
            E.wait("tensor", "V", tokv)
            tokt2 = E.emit("tensor", lambda e, pst=pst: e.transpose(
                pst, lgsc[:, 24:28], c_identf), inc="T")
            E.wait("vector", "T", tokt2)
            tokm = E.emit("vector", lambda e, pst=pst, ts_=ts_: e.tensor_copy(
                mfm_st[0:4, ts_ * 128:(ts_ + 1) * 128], pst), inc="V")
            P.release(sl2, "V", tokm)
        E.wait("sync", "V", tokm)
        tokd = E.emit("sync", lambda e: e.dma_start(maskscr[:, :], mfm_st[0:4, 0:256]), inc="DS")
        E.wait("sync", "DS", tokd)
        tokd2 = E.emit("sync", lambda e: e.dma_start(
            mrep, bass.AP(maskscr.tensor, 0, [[0, 128], [CH, 4], [1, CH]])), inc="DS")
        snapBh[0] = E.snap()
        snapBh[0]["_tokd2"] = tokd2

    def phaseB_body(c):
        tokd2 = snapBh[0]["_tokd2"]
        first = snapB[0] is None
        if not first:
            for eng in ("vector", "scalar", "tensor", "sync", "gpsimd"):
                E.wait_snap(eng, snapB[0])
        for eng in ("vector", "scalar", "tensor", "sync"):
            E.wait_snap(eng, snapBh[0])
        t0 = c * CH
        sc_strm = strm[:, :, t0:t0 + CH]
        E.wait("vector", "DS", tokd2)
        E.wait("gpsimd", "DS", tokd2)
        tokcmb = None
        gtok = [None, None]
        for ee in range(4):
            for grp in range(8):
                if first:
                    E.wait("scalar", "DW", w1_tok[ee * 8 + grp])
                j, offw, tokw = w1ring.load(lambda e, off, ee=ee, grp=grp: e.dma_start(
                    bfv(A(off, 2048)), w1s[ee * 8 + grp, :, :]))
                wap = bfv(A(offw, 2048)).rearrange("p (a b) -> p a b", a=8)
                slots = P.acquire(4)
                E.wait("tensor", "DA", tokw)
                tokt = None
                for ii in range(4):
                    psn = pt[:, slots[ii] * 512: slots[ii] * 512 + CH]
                    for dd in range(8):
                        tokt = E.emit("tensor", lambda e, dd=dd, ii=ii, psn=psn, wap=wap: e.matmul(
                            psn, wap[:, dd, ii * 128:(ii + 1) * 128], lnx2bf[:, dd, :],
                            start=(dd == 0), stop=(dd == 7)), inc="T")
                w1ring.consumed(j, "T", tokt)
                E.wait("scalar", "T", tokt)
                for ii in range(4):
                    ic = grp * 4 + ii
                    psn = pt[:, slots[ii] * 512: slots[ii] * 512 + CH]
                    gb_ = gscr[ii % 2]
                    if gtok[ii % 2] is not None:
                        E.wait("scalar", "G", gtok[ii % 2])
                    toks = E.emit("scalar", lambda e, psn=psn, gb_=gb_, ee=ee, ic=ic: e.activation(
                        gb_[:, 0:CH], psn, AF.Gelu, bias=b_b1[:, ee, ic:ic + 1]), inc="S")
                    P.release([slots[ii]], "S", toks)
                    E.wait("gpsimd", "S", toks)
                    if ee == 0:
                        tokcmb = E.emit("gpsimd", lambda e, gb_=gb_, ic=ic: e.tensor_mul(
                            interb[:, ic, :], gb_[:, 0:CH], mrep[:, 0, :]), inc="G")
                    else:
                        E.emit("gpsimd", lambda e, gb_=gb_, ee=ee: e.tensor_mul(
                            gb_[:, 0:CH], gb_[:, 0:CH], mrep[:, ee, :]))
                        tokcmb = E.emit("gpsimd", lambda e, gb_=gb_, ic=ic: e.tensor_add(
                            interb[:, ic, :], interb[:, ic, :], gb_[:, 0:CH]), inc="G")
                    gtok[ii % 2] = tokcmb
        E.wait("tensor", "G", tokcmb)
        for oo in range(8):
            if first:
                E.wait("scalar", "DW", wo_tok[oo])
            j, offw, tokw = woring.load(lambda e, off, oo=oo: e.dma_start(
                bfv(A(off, 2048)), wos[oo, :, :]))
            wap = bfv(A(offw, 2048)).rearrange("p (a b) -> p a b", a=32)
            sl = P.acquire(1)
            psn = pt[:, sl[0] * 512: sl[0] * 512 + CH]
            E.wait("tensor", "DA", tokw)
            tokt = None
            for ii in range(32):
                tokt = E.emit("tensor", lambda e, ii=ii, psn=psn, wap=wap: e.matmul(
                    psn, wap[:, ii, :], interb[:, ii, :], start=(ii == 0), stop=(ii == 31)), inc="T")
            woring.consumed(j, "T", tokt)
            E.wait("vector", "T", tokt)
            tokvz = E.emit("vector", lambda e, psn=psn, oo=oo: e.scalar_tensor_tensor(
                zB[:, oo, :], psn, b_bo[:, oo:oo + 1], lnx2[:, oo, :],
                ALU.add, ALU.add), inc="V")
            P.release(sl, "V", tokvz)
        layer_norm(zB, CH, 1e-12, zBbf, lnx2bf, gcol=b_g2, bcol=b_b2,
                   add_to=sc_strm, war_tok=tokt)
        snapB[0] = E.snap()

    if KPH >= 2:
        phaseA_head(0)
        phaseA_scan(0)
        phaseA_tail(0)
        phaseB_head(0)
        for k in range(1, NCH):
            phaseA_head(k)
            phaseA_scan(k)
            phaseB_body(k - 1)
            phaseA_tail(k)
            phaseB_head(k)
        phaseB_body(NCH - 1)
    else:
        for k in range(NCH):
            phaseA_head(k)
            phaseA_scan(k)
            phaseA_tail(k)
    E.barrier()
    if DBG:
        E.emit("sync", lambda e: e.dma_start(dbg["d_a"], A(0, 16384)), inc="DS")
        E.barrier(("sync",))

    def phaseC():
        oc_K = 0
        oc_V = 8192
        oc_lnxout = 16512
        oc_lnxbf = 32896
        oc_qp = 41088
        oc_PT = 43136
        oc_qrep = 45184
        oc_rows = 47232
        oc_small = 49280
        oc_ring = 49536
        oc_srow = 50560
        Kb = bfv(A(oc_K, 8192)).rearrange("p (a b) -> p a b", a=8)
        Vb = bfv(A(oc_V, 8320)).rearrange("p (a b) -> p a b", a=16)
        lnxo = A(oc_lnxout, 16384).rearrange("p (a b) -> p a b", a=8)
        lnxb = bfv(A(oc_lnxbf, 8192)).rearrange("p (a b) -> p a b", a=8)
        qrep = A(oc_qrep, 2048)
        negrow = arena_t[64:65, oc_rows:oc_rows + 2048]
        negfm = A(oc_small, 16)
        biasfm = A(oc_small + 16, 16)
        mxcols = A(oc_small + 32, 48)
        ccol = A(oc_small + 80, 4)
        srowC = arena_t[64:65, oc_qrep:oc_qrep + 2048]
        wring = Ring(oc_ring, 2, 512, eng="gpsimd", ch="DW")

        for c4 in range(4):
            t0 = c4 * 512
            sc = strm[:, :, t0:t0 + 512]
            xbf_s = bfv(A(oc_qp, 2048)).rearrange("p (a b) -> p a b", a=8)
            sq_s = bfv(A(oc_PT, 2048)).rearrange("p (a b) -> p a b", a=8)
            layer_norm(sc, 512, 1e-5, xbf_s, sq_s,
                       out_f32=lnxo[:, :, t0:t0 + 512], out_bf=lnxb[:, :, t0:t0 + 512],
                       srow_t=srowC)
            sl = P.acquire(1)
            ps = pt[0:1, sl[0] * 512: sl[0] * 512 + 512]
            E.wait("tensor", "V", E.tok("V"))
            tokt = None
            for dd in range(8):
                tokt = E.emit("tensor", lambda e, dd=dd, ps=ps, t0=t0: e.matmul(
                    ps[:, 0:512], c_onesf4[:, 0:1], lnxo[:, dd, t0:t0 + 512],
                    start=(dd == 0), stop=(dd == 7)), inc="T")
            E.wait("vector", "T", tokt)
            E.emit_s("vector", lambda e, ps=ps, t0=t0: e.tensor_scalar(
                negrow[:, t0:t0 + 512], ps[:, 0:512], 0.0, None, op0=ALU.is_equal))
            tokv = E.emit_s("vector", lambda e, t0=t0: e.tensor_scalar(
                negrow[:, t0:t0 + 512], negrow[:, t0:t0 + 512], float(NEG), None,
                op0=ALU.mult))
            P.release(sl, "V", tokv)
        if DBG:
            E.barrier(("sync",))
            E.emit("sync", lambda e: e.dma_start(dbg["d_b"], A(16512, 16384)), inc="DS")
            E.barrier(("sync",))
        E.wait("sync", "V", E.tok("V"))
        tok_sp = None
        for c4 in range(4):
            tok_sp = E.emit("sync", lambda e, c4=c4: e.dma_start(
                strmspill[:, c4 * 4096:(c4 + 1) * 4096], A(c4 * 4096, 4096)), inc="DS")
        tokn = E.emit("sync", lambda e: e.dma_start(rowscr[0, :], negrow[0:1, :]), inc="DS")
        E.wait("sync", "DS", tokn)
        tokn2 = E.emit("sync", lambda e: e.dma_start(
            negfm, bass.AP(rowscr.tensor, 0, [[1, 128], [128, 16]]),
            allow_slow_non_contiguous=True), inc="DS")

        E.wait("scalar", "DS", tok_sp)
        toks = None
        for oo in range(8):
            j, offw, tokw = wring.load(lambda e, off, oo=oo: e.dma_start(
                bfv(A(off, 512)).rearrange("p (a b) -> p a b", a=8),
                wk[:, oo * 128:(oo + 1) * 128].rearrange("(a p) b -> p a b", p=128)))
            wap = bfv(A(offw, 512)).rearrange("p (a b) -> p a b", a=8)
            E.wait("tensor", "DW", tokw)
            tokt = None
            for pc in range(4):
                sl = P.acquire(1)
                psn = pt[:, sl[0] * 512: sl[0] * 512 + 512]
                for dd in range(8):
                    tokt = E.emit("tensor", lambda e, dd=dd, psn=psn, wap=wap, pc=pc: e.matmul(
                        psn, wap[:, dd, :], lnxb[:, dd, pc * 512:(pc + 1) * 512],
                        start=(dd == 0), stop=(dd == 7)), inc="T")
                E.wait("scalar", "T", tokt)
                toks = E.emit("scalar", lambda e, psn=psn, oo=oo, pc=pc: e.activation(
                    Kb[:, oo, pc * 512:(pc + 1) * 512], psn, AF.Relu,
                    bias=b_bk[:, oo:oo + 1]), inc="S")
                P.release(sl, "S", toks)
            wring.consumed(j, "T", tokt)

        E.wait("vector", "DS", tok_sp)
        tokm = None
        for tt_ in range(16):
            tokm = E.emit("vector", lambda e, tt_=tt_: e.memset(
                Vb[:, tt_, :].rearrange("p (a b) -> p a b", a=16)[:, :, 64:65], 1.0), inc="V")
        wvb = bfv(A(oc_qp, 4096)).rearrange("p (a b) -> p a b", a=8)
        tokbv = E.emit("sync", lambda e: e.dma_start(bvrep_b, bvrep), inc="DS")
        E.wait("vector", "DS", tokbv)
        tokwv = E.emit("gpsimd", lambda e: e.dma_start(
            wvb, wv.rearrange("(a p) b -> p a b", p=128)), inc="DW")
        E.wait("tensor", "DW", tokwv)
        for tt_ in range(16):
            sl = P.acquire(2)
            psn = pt[:, sl[0] * 512: sl[0] * 512 + 1024]
            tokt = None
            for kk in range(8):
                for hhalf in range(2):
                    tokt = E.emit("tensor", lambda e, kk=kk, tt_=tt_, psn=psn, hhalf=hhalf: e.matmul(
                        psn[:, hhalf * 512:(hhalf + 1) * 512],
                        lnxb[:, kk, tt_ * 128:(tt_ + 1) * 128],
                        wvb[:, kk, hhalf * 512:(hhalf + 1) * 512],
                        start=(kk == 0), stop=(kk == 7)), inc="T")
            E.wait("vector", "T", tokt)
            tokvv = E.emit("vector", lambda e, psn=psn: e.tensor_add(
                psn, psn, bvrep_b), inc="V")
            E.wait("scalar", "V", tokvv)
            if tokm is not None:
                E.wait("scalar", "V", tokm)
            dst = Vb[:, tt_, :].rearrange("p (a b) -> p a b", a=16)[:, :, 0:64]
            toksv = E.emit("scalar", lambda e, psn=psn, dst=dst: e.activation(
                dst, psn.rearrange("p (a b) -> p a b", a=16), AF.Relu), inc="S")
            P.release(sl, "S", toksv)
        E.wait("vector", "DS", tokn2)

        qread = [None, None]
        qlast = [None, None]
        mxc = [mxcols[:, 0:48], A(oc_small + 128, 48)]
        ccol2 = [ccol, A(oc_small + 176, 4)]
        biasfm2 = [biasfm, A(oc_small + 180, 16)]

        def emit_qpair(hp):
            par = hp % 2
            qbuf = bfv(A(oc_qp + par * 1024, 1024))
            if qread[par] is not None:
                E.wait("scalar", "T", qread[par])
            j, offw, tokw = wring.load(lambda e, off, hp=hp: e.dma_start(
                bfv(A(off, 512)).rearrange("p (a b) -> p a b", a=8),
                wq[:, hp * 128:(hp + 1) * 128].rearrange("(a p) b -> p a b", p=128)))
            wap = bfv(A(offw, 512)).rearrange("p (a b) -> p a b", a=8)
            E.wait("tensor", "DW", tokw)
            tokt = None
            toks = None
            for pc in range(4):
                sl = P.acquire(1)
                psn = pt[:, sl[0] * 512: sl[0] * 512 + 512]
                for dd in range(8):
                    tokt = E.emit("tensor", lambda e, dd=dd, psn=psn, wap=wap, pc=pc: e.matmul(
                        psn, wap[:, dd, :], lnxb[:, dd, pc * 512:(pc + 1) * 512],
                        start=(dd == 0), stop=(dd == 7)), inc="T")
                E.wait("scalar", "T", tokt)
                toks = E.emit("scalar", lambda e, psn=psn, qbuf=qbuf, pc=pc, hp=hp: e.activation(
                    qbuf[:, pc * 512:(pc + 1) * 512], psn, AF.Relu,
                    bias=b_bq[:, hp:hp + 1], scale=0.125), inc="S")
                P.release(sl, "S", toks)
            wring.consumed(j, "T", tokt)
            qlast[par] = toks

        def emit_pass1(h):
            hp, hh, par = h // 2, h % 2, (h // 2) % 2
            qbuf = bfv(A(oc_qp + par * 1024, 1024))
            E.wait("tensor", "S", qlast[par])
            nmx = 0
            tokv = None
            for qc in range(16):
                span = (qc + 1) * 128
                for pc in range((span + 511) // 512):
                    c0 = pc * 512
                    c1 = min(c0 + 512, span)
                    sl = P.acquire(1)
                    psn = pt[:, sl[0] * 512: sl[0] * 512 + (c1 - c0)]
                    tokt = E.emit("tensor", lambda e, psn=psn, qbuf=qbuf, hh=hh, hp=hp, qc=qc, c0=c0, c1=c1: e.matmul(
                        psn, qbuf[hh * 64:hh * 64 + 64, qc * 128:(qc + 1) * 128],
                        Kb[hh * 64:hh * 64 + 64, hp, c0:c1],
                        start=True, stop=True), inc="T")
                    E.wait("vector", "T", tokt)
                    tokv = E.emit("vector", lambda e, psn=psn, nmx=nmx, h=h: e.reduce_max(
                        mxc[h % 2][:, nmx:nmx + 1], psn, AX.X), inc="V")
                    P.release(sl, "V", tokv)
                    nmx += 1
            return nmx, tokv

        def emit_cchain(h, nmx, tokv):
            cc = ccol2[h % 2]
            bf_ = biasfm2[h % 2]
            E.wait("vector", "V", tokv)
            tokv = E.emit_s("vector", lambda e, nmx=nmx, h=h: e.reduce_max(
                cc[:, 0:1], mxc[h % 2][:, 0:nmx], AX.X))
            sl = P.acquire(1)
            pst = pt[0:1, sl[0] * 512: sl[0] * 512 + 128]
            E.wait("tensor", "V", tokv)
            tokt = E.emit("tensor", lambda e, pst=pst, cc=cc: e.transpose(
                pst, cc[:, 0:1], c_identf), inc="T")
            E.wait("vector", "T", tokt)
            tokv = E.emit_s("vector", lambda e, pst=pst, cc=cc: e.reduce_max(
                cc[0:1, 1:2], pst, AX.X))
            P.release(sl, "V", tokv)
            sl = P.acquire(1)
            psc = pt[:, sl[0] * 512: sl[0] * 512 + 128]
            E.wait("tensor", "V", tokv)
            tokt = E.emit("tensor", lambda e, psc=psc, cc=cc: e.matmul(
                psc[:, 0:1], c_ones1, cc[0:1, 1:2], start=True, stop=True), inc="T")
            E.wait("vector", "T", tokt)
            tokv = E.emit_s("vector", lambda e, psc=psc, bf_=bf_: e.tensor_tensor(
                bf_, negfm, psc[:, 0:1].broadcast_to([128, 16]), ALU.subtract))
            P.release(sl, "V", tokv)
            E.wait("scalar", "V", tokv)

        def emit_pass2_pv(h):
            hp, hh, par = h // 2, h % 2, (h // 2) % 2
            qbuf = bfv(A(oc_qp + par * 1024, 1024))
            bf_ = biasfm2[h % 2]
            pvsl = P.acquire(4)
            P.reserve(pvsl)
            pv = pt[0:65, pvsl[0] * 512: pvsl[0] * 512 + 2048]
            ptok = [None, None]
            tokpv = None
            for kc in range(16):
                q0 = kc * 128
                pc0 = q0 // 512
                pbuf = PTb[kc % 2]
                if ptok[kc % 2] is not None:
                    E.wait("scalar", "T", ptok[kc % 2])
                toks2 = None
                for pc in range(pc0, 4):
                    c0 = pc * 512
                    sl = P.acquire(1)
                    psn = pt[:, sl[0] * 512: sl[0] * 512 + 512]
                    tokt = E.emit("tensor", lambda e, psn=psn, qbuf=qbuf, hh=hh, hp=hp, kc=kc, c0=c0: e.matmul(
                        psn, Kb[hh * 64:hh * 64 + 64, hp, kc * 128:(kc + 1) * 128],
                        qbuf[hh * 64:hh * 64 + 64, c0:c0 + 512],
                        start=True, stop=True), inc="T")
                    E.wait("scalar", "T", tokt)
                    toks2 = E.emit("scalar", lambda e, psn=psn, pbuf=pbuf, c0=c0, kc=kc, bf_=bf_: e.activation(
                        pbuf[:, c0:c0 + 512], psn, AF.Exp, bias=bf_[:, kc:kc + 1]), inc="S")
                    P.release(sl, "S", toks2)
                E.wait("vector", "S", toks2)
                tokdv = E.emit("vector", lambda e, pbuf=pbuf, q0=q0: e.tensor_mul(
                    pbuf[:, q0:q0 + 128], pbuf[:, q0:q0 + 128], c_tril), inc="V")
                E.wait("tensor", "V", tokdv)
                for pc in range(pc0, 4):
                    c0 = max(pc * 512, q0)
                    c1 = (pc + 1) * 512
                    kc_last = min(15, (pc + 1) * 4 - 1)
                    tokpv = E.emit("tensor", lambda e, pv=pv, kc=kc, pbuf=pbuf, c0=c0, c1=c1, h=h, kl=kc_last: e.matmul(
                        pv[:, c0:c1],
                        Vb[:, kc, h * 65:(h + 1) * 65],
                        pbuf[:, c0:c1],
                        start=(kc == 0), stop=(kc == kl)), inc="T")
                ptok[kc % 2] = tokpv
            if hh == 1:
                qread[par] = tokpv
            return pvsl, pv, tokpv

        def emit_epilogue(h, pvsl, pv, tokpv):
            E.wait("vector", "T", tokpv)
            E.emit_s("vector", lambda e, pv=pv: e.scalar_tensor_tensor(
                srowC[0:1, 0:2048], negrow, float(1e38 / NEG), pv[64:65, :],
                ALU.mult, ALU.add))
            E.emit_s("vector", lambda e: e.tensor_scalar_max(
                srowC[0:1, 0:2048], srowC[0:1, 0:2048], 1e-35))
            tokv = E.emit_s("vector", lambda e: e.reciprocal(
                srowC[0:1, 0:2048], srowC[0:1, 0:2048]))
            E.wait("sync", "V", tokv)
            tokd = E.emit("sync", lambda e: e.dma_start(rowscr[0, :], srowC[0:1, 0:2048]), inc="DS")
            E.wait("sync", "DS", tokd)
            tokd2 = E.emit("sync", lambda e: e.dma_start(
                qrep[0:64, :], bass.AP(rowscr.tensor, 0, [[0, 64], [1, 2048]])), inc="DS")
            E.wait("vector", "DS", tokd2)
            tokv = E.emit("vector", lambda e, pv=pv: e.tensor_mul(
                qrep[0:64, :], qrep[0:64, :], pv[0:64, :]), inc="V")
            P.unreserve(pvsl)
            P.release(pvsl, "V", tokv)
            dst = lnxo[(h % 2) * 64:(h % 2) * 64 + 64, h // 2, :]
            E.wait("gpsimd", "V", tokv)
            tokg = E.emit("gpsimd", lambda e, dst=dst: e.dma_start(
                dst, qrep[0:64, :], accum_op=ALU.add), inc="DW")
            E.wait("sync", "DW", tokg)
            E.wait("vector", "DW", tokg)
            E.wait("tensor", "DW", tokg)

        emit_qpair(0)
        st_prev = None
        st_prev = emit_pass1(0)
        for h in range(16):
            emit_cchain(h, *st_prev)
            pvst = emit_pass2_pv(h)
            if h + 1 < 16:
                if (h + 1) % 2 == 0:
                    emit_qpair((h + 1) // 2)
                st_prev = emit_pass1(h + 1)
            emit_epilogue(h, *pvst)

        for c4 in range(4):
            t0 = c4 * 512
            xbf_s = bfv(A(oc_qp, 2048)).rearrange("p (a b) -> p a b", a=8)
            sq_s = bfv(A(oc_PT, 2048)).rearrange("p (a b) -> p a b", a=8)
            rl = A(oc_lnxbf, 4096).rearrange("p (a b) -> p a b", a=8)
            E.wait("sync", "V", E.tok("V"))
            E.wait("sync", "T", E.tok("T"))
            src_ap = bass.AP(strmspill.tensor, c4 * 512, [[16384, 128], [2048, 8], [1, 512]])
            dst_ap = bass.AP(strm2d.tensor, c4 * 512, [[16384, 128], [2048, 8], [1, 512]])
            tokd = E.emit("sync", lambda e, src_ap=src_ap: e.dma_start(rl, src_ap), inc="DS")
            E.wait("vector", "DS", tokd)
            lnret = layer_norm(lnxo[:, :, t0:t0 + 512], 512, 1e-8, xbf_s, sq_s,
                               gcol=b_gam, bcol=b_bet, add_to=rl, srow_t=srowC,
                               war_tok=E.tok("T"))
            E.wait("sync", "V", lnret)
            tokd = E.emit("sync", lambda e, dst_ap=dst_ap: e.dma_start(dst_ap, rl), inc="DS")
            E.wait("vector", "DS", tokd)

    PTb = [bfv(A(43136, 1024)), bfv(A(43136 + 1024, 1024))]
    bvrep_b = A(45184, 1024)

    if KPH >= 3:
        E.barrier()
        phaseC()
        E.barrier()

    def phaseD():
        od_lnxD = 0
        od_strm = 8192
        od_w13 = 24576
        od_w13b = 32768
        od_w2 = 40960
        od_w2b = 45056
        od_g = 49152
        od_cr = 50176
        od_sc = 51200
        lnxD = bfv(A(od_lnxD, 8192)).rearrange("p (a b) -> p a b", a=8)
        acc = A(od_strm, 16384).rearrange("p (a b) -> p a b", a=8)
        gbuf = bfv(A(od_g, 1024)).rearrange("p (a b) -> p a b", a=8)
        crep = bfv(A(od_cr, 1024))
        s1scr = bfv(A(od_sc, 128))
        h3scr = bfv(A(od_sc + 128, 128))
        lgD = A(od_sc + 256, 64)
        g8b = bfv(A(od_sc + 320, 32))
        gb8s = A(od_sc + 352, 8)
        cfm_st = bfv(A(od_g, 1024))[0:8, 0:2048]
        outst = A(od_cr, 1024)
        srowD = arena_t[0:1, od_w13 + 4096:od_w13 + 4096 + 2048]

        g8b3 = g8b[:, 0:64].rearrange("p (a b) -> p a b", a=8)
        tokg8 = E.emit("gpsimd", lambda e: e.dma_start(
            g8b3, g8T.rearrange("(a p) b -> p a b", p=128)), inc="DW")
        tokb8 = E.emit("sync", lambda e: e.dma_start(gb8s, gb8), inc="DS")
        tokrl = None
        for c4 in range(4):
            tokrl = E.emit("sync", lambda e, c4=c4: e.dma_start(
                A(od_strm + c4 * 4096, 4096), strm2d[:, c4 * 4096:(c4 + 1) * 4096]), inc="DS")
        E.wait("vector", "DS", tokrl)
        E.wait("tensor", "DS", tokrl)
        for c4 in range(4):
            t0 = c4 * 512
            xbf_s = bfv(A(od_w13, 2048)).rearrange("p (a b) -> p a b", a=8)
            sq_s = bfv(A(od_w13 + 2048, 2048)).rearrange("p (a b) -> p a b", a=8)
            layer_norm(acc[:, :, t0:t0 + 512], 512, 1e-5, xbf_s, sq_s,
                       out_bf=lnxD[:, :, t0:t0 + 512], srow_t=srowD)
        if DBG:
            E.barrier(("sync",))
            E.emit("sync", lambda e: e.dma_start(dbg["d_l"], bfv(A(od_lnxD, 8192))), inc="DS")
            E.barrier(("sync",))
        E.wait("tensor", "DW", tokg8)
        E.wait("vector", "DS", tokb8)
        g8v = g8b[:, 0:8].rearrange("p (a b) -> p a b", a=1) if False else None
        tokcf = None
        for tc in range(16):
            sl = P.acquire(1)
            psl = pt[:, sl[0] * 512: sl[0] * 512 + 8]
            E.wait("tensor", "V", E.tok("V"))
            tokt = None
            for dd in range(8):
                tokt = E.emit("tensor", lambda e, dd=dd, psl=psl, tc=tc: e.matmul(
                    psl[:, 0:8], lnxD[:, dd, tc * 128:(tc + 1) * 128],
                    g8b3[:, dd, :],
                    start=(dd == 0), stop=(dd == 7)), inc="T")
            E.wait("vector", "T", tokt)
            E.wait("scalar", "T", tokt)
            E.emit_s("vector", lambda e, psl=psl: e.reduce_max(
                lgD[:, 48:49], psl[:, 0:8], AX.X, negate=True))
            E.wait("scalar", "V", E.tok("V"))
            toke = E.emit("scalar", lambda e, psl=psl: e.activation(
                lgD[:, 0:8], psl[:, 0:8], AF.Exp, bias=lgD[:, 48:49],
                accum_out=lgD[:, 49:50]), inc="S")
            E.wait("vector", "S", toke)
            E.emit_s("vector", lambda e: e.reciprocal(lgD[:, 50:51], lgD[:, 49:50]))
            E.emit_s("vector", lambda e: e.tensor_scalar(
                lgD[:, 0:8], lgD[:, 0:8], lgD[:, 50:51], None, op0=ALU.mult))
            E.emit_s("vector", lambda e: e.tensor_add(lgD[:, 8:16], lgD[:, 0:8], gb8s))
            E.emit_s("vector", lambda e: e.reduce_max(lgD[:, 51:52], lgD[:, 8:16], AX.X))
            E.emit_s("vector", lambda e: e.tensor_tensor(
                lgD[:, 16:24], lgD[:, 8:16], lgD[:, 51:52].broadcast_to([128, 8]), ALU.is_ge))
            E.emit_s("vector", lambda e: e.tensor_tensor_scan(
                lgD[:, 24:32], c_onesf4[:, 0:1].broadcast_to([128, 8]), lgD[:, 16:24],
                0.0, ALU.mult, ALU.add))
            E.emit_s("vector", lambda e: e.tensor_scalar(
                lgD[:, 24:32], lgD[:, 24:32], 1.0, None, op0=ALU.is_le))
            E.emit_s("vector", lambda e: e.tensor_mul(lgD[:, 24:32], lgD[:, 24:32], lgD[:, 16:24]))
            E.emit_s("vector", lambda e: e.scalar_tensor_tensor(
                lgD[:, 8:16], lgD[:, 24:32], -1e30, lgD[:, 8:16], ALU.mult, ALU.add))
            E.emit_s("vector", lambda e: e.reduce_max(lgD[:, 51:52], lgD[:, 8:16], AX.X))
            E.emit_s("vector", lambda e: e.tensor_tensor(
                lgD[:, 16:24], lgD[:, 8:16], lgD[:, 51:52].broadcast_to([128, 8]), ALU.is_ge))
            E.emit_s("vector", lambda e: e.tensor_tensor_scan(
                lgD[:, 32:40], c_onesf4[:, 0:1].broadcast_to([128, 8]), lgD[:, 16:24],
                0.0, ALU.mult, ALU.add))
            E.emit_s("vector", lambda e: e.tensor_scalar(
                lgD[:, 32:40], lgD[:, 32:40], 1.0, None, op0=ALU.is_le))
            E.emit_s("vector", lambda e: e.tensor_mul(lgD[:, 32:40], lgD[:, 32:40], lgD[:, 16:24]))
            E.emit_s("vector", lambda e: e.tensor_mul(lgD[:, 40:48], lgD[:, 24:32], lgD[:, 0:8]))
            E.emit_s("vector", lambda e: e.reduce_sum(lgD[:, 52:53], lgD[:, 40:48], AX.X))
            E.emit_s("vector", lambda e: e.tensor_mul(lgD[:, 40:48], lgD[:, 32:40], lgD[:, 0:8]))
            E.emit_s("vector", lambda e: e.reduce_sum(lgD[:, 53:54], lgD[:, 40:48], AX.X))
            E.emit_s("vector", lambda e: e.tensor_scalar(
                lgD[:, 40:48], lgD[:, 24:32], lgD[:, 52:53], None, op0=ALU.mult))
            E.emit_s("vector", lambda e: e.scalar_tensor_tensor(
                lgD[:, 32:40], lgD[:, 32:40], lgD[:, 53:54], lgD[:, 40:48],
                ALU.mult, ALU.add))
            tokv = E.emit_s("vector", lambda e: e.engine_nop())
            P.release(sl, "V", tokv)
            sl2 = P.acquire(1)
            pst = pt[0:8, sl2[0] * 512: sl2[0] * 512 + 128]
            E.wait("tensor", "V", tokv)
            tokt2 = E.emit("tensor", lambda e, pst=pst: e.transpose(
                pst, lgD[:, 32:40], c_identf), inc="T")
            E.wait("vector", "T", tokt2)
            tokcf = E.emit("vector", lambda e, pst=pst, tc=tc: e.tensor_copy(
                cfm_st[0:8, tc * 128:(tc + 1) * 128], pst), inc="V")
            P.release(sl2, "V", tokcf)
        E.wait("sync", "V", tokcf)
        tokcd = E.emit("sync", lambda e: e.dma_start(cfmscr[:, :], cfm_st), inc="DS")
        E.wait("sync", "DS", tokcd)
        E.wait("vector", "DS", tokcd)
        E.wait("scalar", "DS", tokcd)
        if DBG:
            E.emit("sync", lambda e: e.dma_start(dbg["d_r"], cfm_st), inc="DS")
            E.barrier(("sync",))

        gbuf2 = bfv(A(od_g, 2048)).rearrange("p (a b) -> p a b", a=8)
        s1s = bfv(A(od_sc, 256))
        crepS = bfv(A(od_sc + 256, 256))

        def w13_dma(step):
            e8, half = step // 2, step % 2
            buf = od_w13 if step % 2 == 0 else od_w13b
            toks = []
            for m in range(2):
                dst = bfv(A(buf + m * 4096, 4096)).rearrange("p (a b) -> p a b", a=8)
                toks.append(E.emit("gpsimd", lambda e, m=m, e8=e8, half=half, dst=dst: e.dma_start(
                    dst, W13[m, e8, :, half * 1024:(half + 1) * 1024].rearrange(
                        "(a p) b -> p a b", p=128)), inc="DW"))
            buf2 = od_w2 if step % 2 == 0 else od_w2b
            dst2 = bfv(A(buf2, 4096)).rearrange("p (a b) -> p a b", a=8)
            tok2 = E.emit("gpsimd", lambda e, e8=e8, half=half, dst2=dst2: e.dma_start(
                dst2, W2[e8, half * 1024:(half + 1) * 1024, :].rearrange(
                    "(a p) b -> p a b", p=128)), inc="DW")
            return toks[-1], tok2

        E.wait("gpsimd", "T", E.tok("T"))
        E.wait("gpsimd", "V", E.tok("V"))
        wtoks = {0: w13_dma(0)}
        wfree = {}
        for e8 in range(9):
            shared = (e8 == 8)
            for half in range(2):
                step = e8 * 2 + half
                if step + 1 < 18:
                    if step - 1 in wfree:
                        E.wait("gpsimd", "T", wfree[step - 1])
                    wtoks[step + 1] = w13_dma(step + 1)
                tok13, tok2w = wtoks[step]
                buf = od_w13 if step % 2 == 0 else od_w13b
                buf2 = od_w2 if step % 2 == 0 else od_w2b
                w1v = bfv(A(buf, 4096)).rearrange("p (a b) -> p a b", a=8)
                w3v = bfv(A(buf + 4096, 4096)).rearrange("p (a b) -> p a b", a=8)
                w2v = bfv(A(buf2, 4096)).rearrange("p (a b) -> p a b", a=8)
                E.wait("tensor", "DW", tok13)
                lastT = None
                for sect in range(4):
                    s0 = sect * 512
                    if not shared:
                        E.wait("sync", "V", E.tok("V"))
                        tokcr = E.emit("sync", lambda e, e8=e8, s0=s0: e.dma_start(
                            crepS, bass.AP(cfmscr.tensor, e8 * 2048 + s0,
                                           [[0, 128], [1, 512]])), inc="DS")
                        E.wait("vector", "DS", tokcr)
                    gtokv = None
                    for ii in range(8):
                        sl1 = P.acquire(1)
                        ph1 = pt[:, sl1[0] * 512: sl1[0] * 512 + 512]
                        tokt = None
                        for kk in range(8):
                            tokt = E.emit("tensor", lambda e, ph1=ph1, w1v=w1v, kk=kk, ii=ii, s0=s0: e.matmul(
                                ph1, w1v[:, kk, ii * 128:(ii + 1) * 128],
                                lnxD[:, kk, s0:s0 + 512],
                                start=(kk == 0), stop=(kk == 7)), inc="T")
                        sl3 = P.acquire(1)
                        ph3 = pt[:, sl3[0] * 512: sl3[0] * 512 + 512]
                        tokt3 = None
                        for kk in range(8):
                            tokt3 = E.emit("tensor", lambda e, ph3=ph3, w3v=w3v, kk=kk, ii=ii, s0=s0: e.matmul(
                                ph3, w3v[:, kk, ii * 128:(ii + 1) * 128],
                                lnxD[:, kk, s0:s0 + 512],
                                start=(kk == 0), stop=(kk == 7)), inc="T")
                        E.wait("scalar", "T", tokt)
                        if gtokv is not None:
                            E.wait("scalar", "V", gtokv)
                        toks1 = E.emit("scalar", lambda e, ph1=ph1: e.activation(
                            s1s[:, 0:512], ph1, AF.Silu), inc="S")
                        P.release(sl1, "S", toks1)
                        E.wait("vector", "S", toks1)
                        if not shared:
                            E.emit("vector", lambda e: e.tensor_mul(
                                s1s[:, 0:512], s1s[:, 0:512], crepS[:, 0:512]))
                        E.wait("vector", "T", tokt3)
                        gtokv = E.emit("vector", lambda e, ph3=ph3, ii=ii: e.tensor_mul(
                            gbuf2[:, ii, :], s1s[:, 0:512], ph3), inc="V")
                        P.release(sl3, "V", gtokv)
                    if DBG and e8 == 0 and half == 0 and sect == 0:
                        E.barrier(("sync",))
                        E.emit("sync", lambda e: e.dma_start(dbg["d_g"], bfv(A(od_g, 2048))), inc="DS")
                        E.barrier(("sync",))
                    E.wait("tensor", "DW", tok2w)
                    E.wait("tensor", "V", gtokv)
                    for ddg in range(2):
                        osl = P.acquire(4)
                        P.reserve(osl)
                        ops = pt[:, osl[0] * 512: osl[0] * 512 + 2048]
                        tokw2 = None
                        for dd4 in range(4):
                            dd = ddg * 4 + dd4
                            for ii in range(8):
                                tokw2 = E.emit("tensor", lambda e, ops=ops, w2v=w2v, dd=dd, dd4=dd4, ii=ii: e.matmul(
                                    ops[:, dd4 * 512:dd4 * 512 + 512],
                                    w2v[:, ii, dd * 128:(dd + 1) * 128],
                                    gbuf2[:, ii, :],
                                    start=(ii == 0), stop=(ii == 7)), inc="T")
                        lastT = tokw2
                        E.wait("vector", "T", tokw2)
                        tokacc = E.emit("vector", lambda e, ops=ops, s0=s0, ddg=ddg: e.tensor_add(
                            acc[:, ddg * 4:(ddg + 1) * 4, s0:s0 + 512],
                            acc[:, ddg * 4:(ddg + 1) * 4, s0:s0 + 512],
                            ops.rearrange("p (a b) -> p a b", a=4)), inc="V")
                        P.unreserve(osl)
                        P.release(osl, "V", tokacc)
                wfree[step] = lastT
        E.wait("tensor", "V", E.tok("V"))
        outb2 = A(od_g, 1024)
        for tt_ in range(16):
            tokvo = None
            for dd in range(8):
                slo = P.acquire(1)
                apo = pt[:, slo[0] * 512: slo[0] * 512 + 128]
                tokto = E.emit("tensor", lambda e, apo=apo, dd=dd, tt_=tt_: e.transpose(
                    apo, acc[:, dd, tt_ * 128:(tt_ + 1) * 128], c_identf), inc="T")
                E.wait("vector", "T", tokto)
                tokvo = E.emit("vector", lambda e, apo=apo, dd=dd: e.tensor_copy(
                    outb2[:, dd * 128:(dd + 1) * 128], apo), inc="V")
                P.release(slo, "V", tokvo)
            E.wait("sync", "V", tokvo)
            tokdo = E.emit("sync", lambda e, tt_=tt_: e.dma_start(
                out[tt_ * 128:(tt_ + 1) * 128, :], outb2), inc="DS")
            E.wait("vector", "DS", tokdo)
            E.wait("tensor", "DS", tokdo)

    if KPH >= 4:
        E.barrier()
        phaseD()
        E.barrier()
    else:
        if KPH >= 3:
            for c4 in range(4):
                tokd = E.emit("sync", lambda e, c4=c4: e.dma_start(
                    A(c4 * 4096, 4096), strm2d[:, c4 * 4096:(c4 + 1) * 4096]), inc="DS")
            E.wait("sync", "DS", E.tok("DS"))
            E.wait("tensor", "DS", E.tok("DS"))
            E.wait("vector", "DS", E.tok("DS"))
            E.barrier()
        emit_out_from_strm(nc, E, P, pt, A, strm, c_identf, out)

    E.barrier()
    E.replay()
    stack.close()
    return nc


def emit_out_from_strm(nc, E, P, pt, A, strm, c_identf, out):
    o_ost = 16384
    ost = A(o_ost, 1024)
    for tt_ in range(16):
        tokv = None
        for dd in range(8):
            slots = P.acquire(1)
            apo = pt[:, slots[0] * 512: slots[0] * 512 + 128]
            tokt = E.emit("tensor", lambda e, apo=apo, dd=dd, tt_=tt_: e.transpose(
                apo, strm[:, dd, tt_ * 128:(tt_ + 1) * 128], c_identf), inc="T")
            E.wait("vector", "T", tokt)
            tokv = E.emit("vector", lambda e, apo=apo, dd=dd: e.tensor_copy(
                ost[:, dd * 128:(dd + 1) * 128], apo), inc="V")
            P.release(slots, "V", tokv)
        E.wait("sync", "V", tokv)
        tokd = E.emit("sync", lambda e, tt_=tt_: e.dma_start(
            out[tt_ * 128:(tt_ + 1) * 128, :], ost), inc="DS")
        E.wait("vector", "DS", tokd)
    E.wait("sync", "DS", E.tok("DS"))


def build_C(*a, **k):
    raise NotImplementedError


def build_D(*a, **k):
    raise NotImplementedError


_BF = ml_dtypes.bfloat16


def _fm(v):
    return np.ascontiguousarray(np.asarray(v, np.float32).reshape(8, 128).T)


_NC_CACHE = {}


def _get_nc():
    key = (KPH, DBG)
    if key not in _NC_CACHE:
        _NC_CACHE[key] = build_nc()
    return _NC_CACHE[key]


def kernel(**inp):
    f32 = np.float32
    g = {k: np.asarray(v) for k, v in inp.items()}
    im = {}
    im["cw"] = np.ascontiguousarray(
        g["L0_attn_conv_w"].reshape(1024, 4).reshape(8, 128, 4).transpose(1, 0, 2)).astype(f32)
    im["cb"] = _fm(g["L0_attn_conv_b"])
    im["Afm"] = np.ascontiguousarray(
        g["L0_attn_A"].reshape(8, 128, 16).transpose(1, 0, 2).reshape(128, 128)).astype(f32)
    im["Dfm"] = _fm(g["L0_attn_D"])
    im["bpwT"] = np.ascontiguousarray(g["L0_attn_Bp_w"].T).astype(f32)
    im["cpwT"] = np.ascontiguousarray(g["L0_attn_Cp_w"].T).astype(f32)
    im["bcb2"] = np.ascontiguousarray(
        np.stack([g["L0_attn_Bp_b"], g["L0_attn_Cp_b"]], axis=1)).astype(f32)
    im["gwT"] = np.ascontiguousarray(g["L0_attn_gate_w"].T).astype(f32)
    im["owT"] = np.ascontiguousarray(g["L0_attn_out_w"].T).astype(f32)
    im["gb"] = _fm(g["L0_attn_gate_b"])
    im["ob"] = _fm(g["L0_attn_out_b"])
    im["lng"] = _fm(g["L0_attn_ln_g"])
    im["lnb"] = _fm(g["L0_attn_ln_b"])
    im["wr"] = np.ascontiguousarray(g["L0_ffn_W_router"]).astype(f32)
    im["w1"] = np.ascontiguousarray(g["L0_ffn_W1"]).astype(f32)
    im["b1fm"] = np.ascontiguousarray(
        g["L0_ffn_b1"].reshape(4, 32, 128).transpose(2, 0, 1).reshape(128, 128)).astype(f32)
    im["woB"] = np.ascontiguousarray(
        g["L0_ffn_Wo"].reshape(4096, 8, 128).transpose(1, 0, 2)).astype(f32)
    im["bofm"] = _fm(g["L0_ffn_bo"])
    im["g2"] = _fm(g["L0_ffn_ln_gamma"])
    im["b2"] = _fm(g["L0_ffn_ln_beta"])
    im["wq"] = np.ascontiguousarray(g["L1_attn_Wq"]).astype(f32)
    im["wk"] = np.ascontiguousarray(g["L1_attn_Wk"]).astype(f32)
    im["wv"] = np.ascontiguousarray(g["L1_attn_Wv"]).astype(f32)
    im["bqfm"] = _fm(g["L1_attn_bq"] / 8.0) if False else _fm(np.asarray(g["L1_attn_bq"], f32) * 0.125)
    im["bkfm"] = _fm(g["L1_attn_bk"])
    im["bvfm"] = _fm(g["L1_attn_bv"])
    im["bvrep"] = np.ascontiguousarray(
        np.broadcast_to(np.asarray(g["L1_attn_bv"], f32), (128, 1024)))
    im["gamv"] = _fm(g["L1_attn_gamma"])
    im["betv"] = _fm(g["L1_attn_beta"])
    im["g8T"] = np.ascontiguousarray(g["L1_ffn_gate_w"].T).astype(f32)
    im["gb8"] = np.ascontiguousarray(
        np.broadcast_to(np.asarray(g["L1_ffn_gate_bias"], f32), (128, 8))).astype(f32)
    w1T = np.asarray(g["L1_ffn_w1"], f32).transpose(0, 2, 1)
    w3T = np.asarray(g["L1_ffn_w3"], f32).transpose(0, 2, 1)
    sw1T = np.asarray(g["L1_ffn_sw1"], f32).T[None]
    sw3T = np.asarray(g["L1_ffn_sw3"], f32).T[None]
    W13 = np.ascontiguousarray(np.stack([
        np.concatenate([w1T, sw1T], 0), np.concatenate([w3T, sw3T], 0)], 0))
    im["W13"] = W13
    w2T = np.asarray(g["L1_ffn_w2"], f32).transpose(0, 2, 1)
    sw2T = np.asarray(g["L1_ffn_sw2"], f32).T[None]
    im["W2"] = np.ascontiguousarray(np.concatenate([w2T, sw2T], 0))
    im["identf"] = np.eye(128, dtype=f32)
    tril = (np.arange(128)[None, :] >= np.arange(128)[:, None]).astype(_BF)
    im["trilT"] = np.ascontiguousarray(tril)
    im["ones1"] = np.ones((1, 128), f32)

    nc = _get_nc()
    xs = np.asarray(g["L0_attn_x"], f32)
    in_maps = [dict(im, x=np.ascontiguousarray(xs[b])) for b in range(8)]
    res = run_bass_kernel_spmd(nc, in_maps, core_ids=list(range(8)))
    outp = np.stack([np.asarray(r["out"]) for r in res.results], 0)
    if DBG:
        kernel._dbg = [{k2: np.asarray(v2) for k2, v2 in r.items() if k2 != "out"}
                       for r in res.results]
    return outp


# version 10
# speedup vs baseline: 1.0034x; 1.0034x over previous
import os
import contextlib
import numpy as np
import ml_dtypes
import concourse.bass as bass
import concourse.mybir as mybir
from concourse.bass_utils import run_bass_kernel_spmd

F32 = mybir.dt.float32
BF16 = mybir.dt.bfloat16
AF = mybir.ActivationFunctionType
ALU = mybir.AluOpType
AX = mybir.AxisListType

T = 2048
D = 1024
CH = 256
NCH = T // CH
NEG = -2.0 ** 32 + 1.0

DBG = int(os.environ.get("KDBG", "0"))
KRACE = int(os.environ.get("KRACE", "0"))
KPH = int(os.environ.get("KPH", "4"))

ENGINES = ("tensor", "vector", "scalar", "gpsimd", "sync")


class Emitter:
    def __init__(self, nc, stack):
        self.nc = nc
        self.q = {e: [] for e in ENGINES}
        self.chans = {}
        self.waited = {e: {} for e in ENGINES}
        self.stack = stack

    def chan(self, name, step=1):
        sem = self.stack.enter_context(self.nc.semaphore(name))
        self.chans[name] = [sem, 0, step]

    def tok(self, ch):
        return self.chans[ch][1]

    def emit(self, eng, fn, inc=None):
        tok = None
        if inc is not None:
            c = self.chans[inc]
            c[1] += c[2]
            tok = c[1]
        self.q[eng].append((fn, inc))
        return tok

    def wait(self, eng, ch, tok):
        if tok is None or tok <= 0:
            return
        w = self.waited[eng]
        if w.get(ch, 0) >= tok:
            return
        w[ch] = tok
        sem = self.chans[ch][0]
        self.q[eng].append((lambda e, s=sem, t=tok: e.wait_ge(s, t), "__wait__"))

    def snap(self):
        return {ch: c[1] for ch, c in self.chans.items()}

    def wait_snap(self, eng, snap):
        for ch, tok in snap.items():
            if ch.startswith("_"):
                continue
            self.wait(eng, ch, tok)

    def emit_s(self, eng, fn):
        ch = {"vector": "V", "scalar": "S", "tensor": "T", "gpsimd": "G"}[eng]
        tok = self.emit(eng, fn, inc=ch)
        self.wait(eng, ch, tok)
        return tok

    def barrier(self, engines=ENGINES):
        s = self.snap()
        for e in engines:
            self.wait_snap(e, s)

    def replay(self):
        qs = self.q
        chans = self.chans
        selfsems = {}
        if KRACE:
            for e in ENGINES:
                selfsems[e] = self.stack.enter_context(self.nc.semaphore("self_" + e))

        rcount = {ch: 0 for ch in chans}

        def mk(engname):
            def body(eng):
                total = [0]
                prev_chan = [None]
                for fn, inc in qs[engname]:
                    if inc == "__wait__":
                        fn(eng)
                        continue
                    if KRACE:
                        if total[0] > 0:
                            eng.wait_ge(selfsems[engname], total[0])
                        if prev_chan[0] is not None:
                            eng.wait_ge(chans[prev_chan[0]][0], rcount[prev_chan[0]])
                            prev_chan[0] = None
                    ins = fn(eng)
                    if inc is not None:
                        ins.then_inc(chans[inc][0], chans[inc][2])
                        rcount[inc] += chans[inc][2]
                    if KRACE:
                        if inc is None:
                            step = 16 if "DMA" in type(ins.ins).__name__ or "Dma" in type(ins.ins).__name__ else 1
                            ins.then_inc(selfsems[engname], step)
                            total[0] += step
                        else:
                            prev_chan[0] = inc
            return body
        with self.nc.Block() as block:
            block.tensor(mk("tensor"))
            block.vector(mk("vector"))
            block.scalar(mk("scalar"))
            block.gpsimd(mk("gpsimd"))
            block.sync(mk("sync"))


class Psum:
    def __init__(self, E, pt):
        self.E = E
        self.pt = pt
        self.state = [(None, 0)] * 8
        self.rr = 0
        self.reserved = set()

    def acquire(self, n=1):
        for _ in range(16):
            if self.rr + n > 8:
                self.rr = 0
            slots = list(range(self.rr, self.rr + n))
            if any(s in self.reserved for s in slots):
                self.rr = max(s for s in slots if s in self.reserved) + 1
                continue
            break
        else:
            raise RuntimeError("psum reserve deadlock")
        self.rr = (self.rr + n) % 8
        for s in slots:
            ch, tok = self.state[s]
            if ch is not None:
                self.E.wait("tensor", ch, tok)
        return slots

    def reserve(self, slots):
        self.reserved.update(slots)

    def unreserve(self, slots):
        self.reserved.difference_update(slots)

    def release(self, slots, ch, tok):
        for s in slots:
            self.state[s] = (ch, tok)


def bfv(ap):
    return ap.bitcast(BF16)


def build_nc():
    nc = bass.Bass("TRN2", target_bir_lowering=False, detect_race_conditions=bool(KRACE))
    stack = contextlib.ExitStack()

    def din(name, shape, dt=F32):
        return nc.dram_tensor(name, shape, dt, kind="ExternalInput").ap()

    def dint(name, shape, dt=F32):
        return nc.dram_tensor(name, shape, dt, kind="Internal").ap()

    x = din("x", [T, D])
    cw = din("cw", [128, 8, 4]); cb = din("cb", [128, 8])
    Afm = din("Afm", [128, 128]); Dfm = din("Dfm", [128, 8])
    bpwT = din("bpwT", [D, 16]); cpwT = din("cpwT", [D, 16])
    bcb2 = din("bcb2", [16, 2])
    gwT = din("gwT", [D, D]); owT = din("owT", [D, D])
    gb = din("gb", [128, 8]); ob = din("ob", [128, 8])
    lng = din("lng", [128, 8]); lnb = din("lnb", [128, 8])
    wr = din("wr", [D, 4])
    w1 = din("w1", [4, D, 4096])
    b1fm = din("b1fm", [128, 128])
    woB = din("woB", [8, 4096, 128])
    bofm = din("bofm", [128, 8])
    g2 = din("g2", [128, 8]); b2 = din("b2", [128, 8])
    wq = din("wq", [D, D]); wk = din("wk", [D, D]); wv = din("wv", [D, D])
    bqfm = din("bqfm", [128, 8]); bkfm = din("bkfm", [128, 8]); bvfm = din("bvfm", [128, 8])
    gamv = din("gamv", [128, 8]); betv = din("betv", [128, 8])
    g8T = din("g8T", [D, 8]); gb8 = din("gb8", [128, 8])
    W13 = din("W13", [2, 9, D, 2048])
    W2 = din("W2", [9, 2048, D])
    bvrep = din("bvrep", [128, 1024])
    identf = din("identf", [128, 128])
    trilT = din("trilT", [128, 128], BF16)
    ones1 = din("ones1", [1, 128])

    out = nc.dram_tensor("out", [T, D], F32, kind="ExternalOutput").ap()
    dbg = {}
    if DBG:
        for nm in ("d_a", "d_b", "d_c"):
            dbg[nm] = nc.dram_tensor(nm, [128, 16384], F32, kind="ExternalOutput").ap()
        dbg["d_r"] = nc.dram_tensor("d_r", [8, 2048], BF16, kind="ExternalOutput").ap()
        dbg["d_l"] = nc.dram_tensor("d_l", [128, 16384], BF16, kind="ExternalOutput").ap()
        dbg["d_g"] = nc.dram_tensor("d_g", [128, 4096], BF16, kind="ExternalOutput").ap()

    repscr = dint("repscr", [16, 512])
    maskscr = dint("maskscr", [4, 256], BF16)
    rowscr = dint("rowscr", [1, 2048])
    cfmscr = dint("cfmscr", [8, 2048], BF16)
    strmspill = dint("strmspill", [128, 16384])
    strm2d = dint("strm2d", [128, 16384])
    w1s = dint("w1s", [32, 128, 4096], BF16)
    wos = dint("wos", [8, 128, 4096], BF16)
    gws = dint("gws", [8, 128, 1024], BF16)
    ows = dint("ows", [8, 128, 1024], BF16)

    AR = 51712
    arena_t = stack.enter_context(nc.sbuf_tensor("arena", [128, AR], F32))
    consts = stack.enter_context(nc.sbuf_tensor("consts", [128, 832], F32))
    cbias = stack.enter_context(nc.sbuf_tensor("cbias", [128, 448], F32))
    pt = stack.enter_context(nc.psum_tensor("pt", [128, 4096], F32))

    E = Emitter(nc, stack)
    for ch in ("T", "V", "S", "G"):
        E.chan(ch, 1)
    for ch in ("DW", "DS", "DA"):
        E.chan(ch, 16)
    P = Psum(E, pt)

    def A(o, n):
        assert o + n <= AR, (o, n)
        return arena_t[:, o:o + n]

    c_identf = consts[:, 0:128]
    c_ones1 = consts[0:1, 128:256]
    c_onesb = bfv(consts[:, 256:257])[:, 0:1]
    c_tril = bfv(consts[:, 257:321])[:, 0:128]
    c_cw = consts[:, 322:354].rearrange("p (a b) -> p a b", a=8)
    c_cb = consts[:, 354:362]
    c_Aexp = consts[:, 362:490].rearrange("p (a b) -> p a b", a=8)
    c_Dfm = consts[:, 490:498]
    c_hst = consts[:, 498:626].rearrange("p (a b) -> p a b", a=8)
    c_onesf4 = consts[:, 626:630]
    c_ones64 = consts[64:65, 704:832]

    b_bpb = cbias[0:16, 0:1]; b_cpb = cbias[0:16, 1:2]
    b_gb = cbias[:, 2:10]; b_ob = cbias[:, 10:18]
    b_lng = cbias[:, 18:26]; b_lnb = cbias[:, 26:34]
    b_b1 = cbias[:, 34:162].rearrange("p (a b) -> p a b", a=4)
    b_bo = cbias[:, 162:170]
    b_g2 = cbias[:, 170:178]; b_b2 = cbias[:, 178:186]
    b_bq = cbias[:, 186:194]; b_bk = cbias[:, 194:202]; b_bv = cbias[:, 202:210]
    b_gam = cbias[:, 210:218]; b_bet = cbias[:, 218:226]
    b_gb8 = cbias[:, 226:234]

    bpw_t = stack.enter_context(nc.sbuf_tensor("bpw2", [128, 128], BF16))
    cpw_t = stack.enter_context(nc.sbuf_tensor("cpw2", [128, 128], BF16))
    wrf_t = stack.enter_context(nc.sbuf_tensor("wrf", [128, 32], F32))
    bpw = bpw_t[:, :].rearrange("p (a b) -> p a b", a=8)
    cpw = cpw_t[:, :].rearrange("p (a b) -> p a b", a=8)
    wrf = wrf_t[:, :].rearrange("p (a b) -> p a b", a=8)

    for dst, src in ((c_identf, identf), (c_ones1, ones1),
                     (c_cw, cw), (c_cb, cb), (c_Dfm, Dfm),
                     (cbias[0:16, 0:2], bcb2), (b_gb, gb), (b_ob, ob),
                     (b_lng, lng), (b_lnb, lnb), (b_b1, b1fm.rearrange("p (a b) -> p a b", a=4)),
                     (b_bo, bofm), (b_g2, g2), (b_b2, b2), (b_bq, bqfm), (b_bk, bkfm),
                     (b_bv, bvfm), (b_gam, gamv), (b_bet, betv), (b_gb8, gb8),
                     (consts[:, 362:490], Afm)):
        E.emit("sync", lambda e, dst=dst, src=src: e.dma_start(dst, src), inc="DS")
    E.emit("sync", lambda e: e.dma_start(c_tril, trilT), inc="DS")
    E.emit("sync", lambda e: e.dma_start(c_ones64, ones1), inc="DS")
    tDS0 = E.tok("DS")
    E.emit("gpsimd", lambda e: e.dma_start(bpw_t[:, :].rearrange("p (a b) -> p a b", a=8), bpwT.rearrange("(a p) b -> p a b", p=128)), inc="DW")
    E.emit("gpsimd", lambda e: e.dma_start(cpw_t[:, :].rearrange("p (a b) -> p a b", a=8), cpwT.rearrange("(a p) b -> p a b", p=128)), inc="DW")
    E.emit("sync", lambda e: e.dma_start(wrf_t[:, :].rearrange("p (a b) -> p a b", a=8), wr.rearrange("(a p) b -> p a b", p=128)), inc="DS")

    E.emit("vector", lambda e: e.memset(c_onesb, 1.0), inc="V")
    E.emit("vector", lambda e: e.memset(consts[:, 498:626], 0.0), inc="V")
    tok_o4 = E.emit("vector", lambda e: e.memset(c_onesf4, 1.0), inc="V")
    E.wait("scalar", "DS", tDS0)
    E.emit("scalar", lambda e: e.activation(consts[:, 362:490], consts[:, 362:490], AF.Exp), inc="S")
    tok_aexp = E.emit("scalar", lambda e: e.activation(consts[:, 362:490], consts[:, 362:490], AF.Exp, scale=-1.0), inc="S")

    o = 16384
    def alloc(n):
        nonlocal o
        r = o
        o += n
        assert o <= AR, o
        return r
    o_lnx1 = alloc(2072)
    o_lnxbf = alloc(1024)
    o_u = alloc(2048)
    o_ubf = alloc(1024)
    o_y = alloc(2048)
    o_Brep = alloc(2048)
    o_Crep = alloc(2048)
    o_scr = alloc(2048)
    o_tail = alloc(24)
    o_srow = alloc(1024)
    o_lnx2 = alloc(2048)
    o_lnx2bf = alloc(1024)
    o_inter = alloc(4096)
    o_gscr = alloc(256)
    o_zB = alloc(2048)
    o_zBbf = alloc(1024)
    o_w1r = alloc(4096)
    o_wor = alloc(4096)
    o_mrep = alloc(512)
    o_Bsc = alloc(64)

    strm = A(0, 16384).rearrange("p (a b) -> p a b", a=8)
    xst = A(o_lnx1, 1024)
    E.wait("sync", "DS", tDS0)
    for tt_ in range(16):
        tokd = E.emit("sync", lambda e, tt_=tt_: e.dma_start(xst, x[tt_ * 128:(tt_ + 1) * 128, :]), inc="DS")
        E.wait("tensor", "DS", tokd)
        for dd in range(8):
            slots = P.acquire(1)
            apo = pt[:, slots[0] * 512: slots[0] * 512 + 128]
            tokt = E.emit("tensor", lambda e, apo=apo, dd=dd: e.transpose(
                apo, xst[:, dd * 128:(dd + 1) * 128], c_identf), inc="T")
            E.wait("vector", "T", tokt)
            tokv = E.emit("vector", lambda e, apo=apo, dd=dd, tt_=tt_: e.tensor_copy(
                strm[:, dd, tt_ * 128:(tt_ + 1) * 128], apo), inc="V")
            P.release(slots, "V", tokv)
        E.wait("sync", "T", E.tok("T"))
    E.wait("vector", "T", E.tok("T"))
    E.wait("scalar", "T", E.tok("T"))


    def fm3(ap2d):
        return ap2d.rearrange("(a p) b -> p a b", p=128)

    def stage1(src, dst):
        na = src.shape[1]
        return E.emit("gpsimd", lambda e, src=src, dst=dst: e.dma_start(
            dst.rearrange("p (a b) -> p a b", a=na), src), inc="DW")

    for oo in range(8):
        stage1(fm3(gwT[:, oo * 128:(oo + 1) * 128]), gws[oo, :, :])
    for oo in range(8):
        tok_gwos = stage1(fm3(owT[:, oo * 128:(oo + 1) * 128]), ows[oo, :, :])
    w1_tok = {}
    for ee in range(4):
        for grp in range(8):
            w1_tok[ee * 8 + grp] = stage1(
                fm3(w1[ee, :, grp * 512:(grp + 1) * 512]), w1s[ee * 8 + grp, :, :])
    wo_tok = {}
    for oo in range(8):
        wo_tok[oo] = stage1(fm3(woB[oo, :, :]), wos[oo, :, :])

    lnx1 = A(o_lnx1, 2072).rearrange("p (a b) -> p a b", a=8)
    lnxbf = bfv(A(o_lnxbf, 1024)).rearrange("p (a b) -> p a b", a=8)
    ubuf = A(o_u, 2048).rearrange("p (a b) -> p a b", a=8)
    ubf = bfv(A(o_ubf, 1024)).rearrange("p (a b) -> p a b", a=8)
    ybuf = A(o_y, 2048).rearrange("p (a b) -> p a b", a=8)
    Brep = A(o_Brep, 2048).rearrange("p (a b) -> p a b", a=16)
    Crep = A(o_Crep, 2048).rearrange("p (a b) -> p a b", a=16)
    scr_uB = A(o_scr, 2048).rearrange("p (a b) -> p a b", a=16)
    scr_h = scr_uB
    scr_z = A(o_scr, 2048).rearrange("p (a b) -> p a b", a=8)
    tailb = A(o_tail, 24).rearrange("p (a b) -> p a b", a=8)
    srow = A(o_srow, 1024)
    bcs = arena_t[32:48, o_srow:o_srow + 512]
    mfm_st = bfv(arena_t[64:68, o_srow:o_srow + 128])
    lnx2 = A(o_lnx2, 2048).rearrange("p (a b) -> p a b", a=8)
    lnx2bf = bfv(A(o_lnx2bf, 1024)).rearrange("p (a b) -> p a b", a=8)
    interb = bfv(A(o_inter, 4096)).rearrange("p (a b) -> p a b", a=32)
    gscr = [bfv(A(o_gscr, 128)), bfv(A(o_gscr + 128, 128))]
    zB = A(o_zB, 2048).rearrange("p (a b) -> p a b", a=8)
    zBbf = bfv(A(o_zBbf, 1024)).rearrange("p (a b) -> p a b", a=8)
    mrep = bfv(A(o_mrep, 512)).rearrange("p (a b) -> p a b", a=4)
    lgsc = A(o_Bsc, 64)

    class Ring:
        def __init__(self, base, nslots, slotcols, eng="scalar", ch="DA"):
            self.off = [base + i * slotcols for i in range(nslots)]
            self.free = [(None, 0)] * nslots
            self.i = 0
            self.n = nslots
            self.eng = [eng] * nslots if isinstance(eng, str) else eng
            self.ch = [ch] * nslots if isinstance(ch, str) else ch

        def load(self, dma_fn):
            j = self.i
            self.i = (self.i + 1) % self.n
            ch, tok = self.free[j]
            if ch is not None:
                E.wait(self.eng[j], ch, tok)
            tokd = E.emit(self.eng[j], lambda e, off=self.off[j]: dma_fn(e, off), inc=self.ch[j])
            return j, self.off[j], tokd

        def consumed(self, j, ch, tok):
            self.free[j] = (ch, tok)

    w1ring = Ring(o_w1r, 2, 2048, eng=["scalar", "gpsimd"], ch=["DA", "DW"])
    woring = Ring(o_wor, 2, 2048, eng=["scalar", "gpsimd"], ch=["DA", "DW"])
    garing = Ring(o_Brep, 2, 512, eng="sync", ch="DS")

    def layer_norm(x_f32, ncols, eps, xbf_scr, sq_scr, out_f32=None, out_bf=None,
                   gcol=None, bcol=None, add_to=None, war_tok=None, srow_t=None):
        if srow_t is None:
            srow_t = srow
        if war_tok is not None:
            E.wait("vector", "T", war_tok)
            E.wait("scalar", "T", war_tok)
        nb = (2 * ncols + 511) // 512
        tokc = E.emit("vector", lambda e: e.tensor_copy(xbf_scr, x_f32), inc="V")
        E.wait("scalar", "V", tokc)
        toksq = E.emit("scalar", lambda e: e.activation(sq_scr, xbf_scr, AF.Square), inc="S")
        sl = P.acquire(nb)
        ps = pt[0:1, sl[0] * 512: sl[0] * 512 + 2 * ncols]
        E.wait("tensor", "V", tokc)
        E.wait("tensor", "S", toksq)
        tokt = None
        for dd in range(8):
            E.emit("tensor", lambda e, dd=dd: e.matmul(
                ps[:, 0:ncols], c_onesb, xbf_scr[:, dd, :], start=(dd == 0), stop=(dd == 7)))
        for dd in range(8):
            tokt = E.emit("tensor", lambda e, dd=dd: e.matmul(
                ps[:, ncols:2 * ncols], c_onesb, sq_scr[:, dd, :], start=(dd == 0), stop=(dd == 7)), inc="T")
        E.wait("vector", "T", tokt)
        mu = srow_t[0:1, 0:ncols]
        m2 = srow_t[0:1, ncols:2 * ncols]
        vpe = srow_t[0:1, 2 * ncols:3 * ncols]
        rstd = srow_t[0:1, 3 * ncols:4 * ncols]
        E.emit("vector", lambda e: e.tensor_scalar_mul(mu, ps[:, 0:ncols], 1.0 / D))
        E.emit("vector", lambda e: e.tensor_scalar_mul(m2, ps[:, ncols:2 * ncols], 1.0 / D))
        E.emit("vector", lambda e: e.tensor_mul(vpe, mu, mu))
        E.emit("vector", lambda e: e.scalar_tensor_tensor(vpe, vpe, -1.0, m2, ALU.mult, ALU.add))
        E.emit("vector", lambda e: e.tensor_scalar_add(vpe, vpe, float(eps)))
        tokv = E.emit("vector", lambda e: e.reciprocal(vpe, vpe), inc="V")
        P.release(sl, "V", tokv)
        E.wait("scalar", "V", tokv)
        toks = E.emit("scalar", lambda e: e.activation(rstd, vpe, AF.Sqrt), inc="S")
        E.wait("vector", "S", toks)
        tokn = E.emit("vector", lambda e: e.scalar_tensor_tensor(
            mu, mu, -1.0, rstd, ALU.mult, ALU.mult), inc="V")
        sl2 = P.acquire(nb)
        rp = pt[:, sl2[0] * 512: sl2[0] * 512 + 2 * ncols]
        E.wait("tensor", "V", tokn)
        onesrow = c_ones1 if srow_t.base_partition() == 0 else c_ones64
        E.emit("tensor", lambda e, onesrow=onesrow: e.matmul(
            rp[:, 0:ncols], onesrow, rstd, start=True, stop=True))
        tokt2 = E.emit("tensor", lambda e, onesrow=onesrow: e.matmul(
            rp[:, ncols:2 * ncols], onesrow, mu, start=True, stop=True), inc="T")
        E.wait("vector", "T", tokt2)
        rsr = rp[:, 0:ncols].unsqueeze(1).broadcast_to([128, 8, ncols])
        nmr = rp[:, ncols:2 * ncols].unsqueeze(1).broadcast_to([128, 8, ncols])
        if out_f32 is None and gcol is None and add_to is None:
            E.emit("vector", lambda e: e.tensor_mul(out_bf, x_f32, rsr))
            last = E.emit("vector", lambda e: e.tensor_add(out_bf, out_bf, nmr), inc="V")
            P.release(sl2, "V", last)
            return last
        tgt = out_f32 if out_f32 is not None else x_f32
        E.emit("vector", lambda e: e.tensor_mul(tgt, x_f32, rsr))
        last = E.emit("vector", lambda e: e.tensor_add(tgt, tgt, nmr), inc="V")
        if out_bf is not None:
            last = E.emit("vector", lambda e: e.tensor_copy(out_bf, tgt), inc="V")
        if gcol is not None:
            for dd in range(8):
                E.emit("vector", lambda e, dd=dd: e.tensor_scalar(
                    tgt[:, dd, :], tgt[:, dd, :], gcol[:, dd:dd + 1], bcol[:, dd:dd + 1],
                    op0=ALU.mult, op1=ALU.add))
            last = E.emit("vector", lambda e: e.tensor_add(add_to, add_to, tgt), inc="V")
        P.release(sl2, "V", last)
        return last

    snapA = [None]

    stA = {}

    def phaseA_head(c):
        if snapA[0] is None:
            pass
        else:
            for eng in ("vector", "scalar", "tensor", "sync"):
                E.wait_snap(eng, snapA[0])
        t0 = c * CH
        sc_strm = strm[:, :, t0:t0 + CH]
        stA["sc_strm"] = sc_strm
        lnx_body = lnx1[:, :, 3:3 + CH]
        layer_norm(sc_strm, CH, 1e-5, ubf, lnxbf, out_f32=lnx_body)
        if c == 0:
            E.emit("vector", lambda e: e.memset(lnx1[:, :, 0:3], 0.0))
        else:
            E.emit("vector", lambda e: e.tensor_copy(lnx1[:, :, 0:3], tailb))
        for dd in range(8):
            E.emit("vector", lambda e, dd=dd: e.tensor_scalar_mul(
                ubuf[:, dd, :], lnx1[:, dd, 0:CH], c_cw[:, dd, 0:1]))
            for j in (1, 2, 3):
                E.emit("vector", lambda e, dd=dd, j=j: e.scalar_tensor_tensor(
                    ubuf[:, dd, :], lnx1[:, dd, j:j + CH], c_cw[:, dd, j:j + 1],
                    ubuf[:, dd, :], ALU.mult, ALU.add))
        tokv = E.emit("vector", lambda e: e.tensor_copy(tailb, lnx1[:, :, CH:CH + 3]), inc="V")
        E.wait("scalar", "V", tokv)
        toks = None
        for dd in range(8):
            toks = E.emit("scalar", lambda e, dd=dd: e.activation(
                ubuf[:, dd, :], ubuf[:, dd, :], AF.Silu, bias=c_cb[:, dd:dd + 1]), inc="S")
        E.wait("vector", "S", toks)
        tokubf = E.emit("vector", lambda e: e.tensor_copy(ubf, ubuf), inc="V")
        toklbf = E.emit("vector", lambda e: e.tensor_copy(lnxbf, lnx_body), inc="V")
        sl = P.acquire(1)
        ps = pt[0:16, sl[0] * 512: sl[0] * 512 + 512]
        E.wait("tensor", "V", tokubf)
        tokt = None
        for dd in range(8):
            E.emit("tensor", lambda e, dd=dd, ps=ps: e.matmul(
                ps[:, 0:CH], bpw[:, dd, :], ubf[:, dd, :], start=(dd == 0), stop=(dd == 7)))
        for dd in range(8):
            tokt = E.emit("tensor", lambda e, dd=dd, ps=ps: e.matmul(
                ps[:, 256:256 + CH], cpw[:, dd, :], ubf[:, dd, :], start=(dd == 0), stop=(dd == 7)), inc="T")
        E.wait("vector", "T", tokt)
        E.emit("vector", lambda e, ps=ps: e.tensor_scalar_add(bcs[:, 0:CH], ps[:, 0:CH], b_bpb))
        toks2 = E.emit("vector", lambda e, ps=ps: e.tensor_scalar_add(
            bcs[:, 256:256 + CH], ps[:, 256:256 + CH], b_cpb), inc="V")
        P.release(sl, "V", toks2)
        E.wait("sync", "V", toks2)
        tokd = E.emit("sync", lambda e: e.dma_start(repscr[:, :], bcs), inc="DS")
        stA["tokd"] = tokd
        stA["toklbf"] = toklbf

    def phaseA_scan(c):
        tokd = stA["tokd"]
        lastred = None
        for half in (0, 1):
            hb = half * 128
            E.wait("sync", "DS", tokd)
            if lastred is not None:
                E.wait("sync", "V", lastred)
            E.emit("sync", lambda e, hb=hb: e.dma_start(
                Brep, bass.AP(repscr.tensor, hb, [[0, 128], [512, 16], [1, 128]])), inc="DS")
            tokC = E.emit("sync", lambda e, hb=hb: e.dma_start(
                Crep, bass.AP(repscr.tensor, 256 + hb, [[0, 128], [512, 16], [1, 128]])), inc="DS")
            E.wait("vector", "DS", tokC)
            for dd in range(8):
                ub_b = ubuf[:, dd, hb:hb + 128].unsqueeze(1).broadcast_to([128, 16, 128])
                E.emit("vector", lambda e, ub_b=ub_b: e.tensor_mul(scr_uB, ub_b, Brep))
                for n in range(16):
                    a0 = c_Aexp[:, dd, n:n + 1].broadcast_to([128, 128])
                    E.emit("vector", lambda e, a0=a0, n=n, dd=dd: e.tensor_tensor_scan(
                        scr_h[:, n, :], a0, scr_uB[:, n, :], c_hst[:, dd, n:n + 1],
                        ALU.mult, ALU.add))
                E.emit("vector", lambda e, dd=dd: e.tensor_copy(
                    c_hst[:, dd, :].unsqueeze(2), scr_h[:, :, 127:128]))
                E.emit("vector", lambda e: e.tensor_mul(scr_h, scr_h, Crep))
                lastred = E.emit("vector", lambda e, dd=dd, hb=hb: e.tensor_reduce(
                    ybuf[:, dd, hb:hb + 128].unsqueeze(2),
                    scr_h.rearrange("p a b -> p b a"), AX.X, ALU.add), inc="V")
        tok_ydu = None
        for dd in range(8):
            tok_ydu = E.emit("vector", lambda e, dd=dd: e.scalar_tensor_tensor(
                ybuf[:, dd, :], ubuf[:, dd, :], c_Dfm[:, dd:dd + 1], ybuf[:, dd, :],
                ALU.mult, ALU.add), inc="V")
        stA["tok_ydu"] = tok_ydu

    def phaseA_tail(c):
        sc_strm = stA["sc_strm"]
        tok_ydu = stA["tok_ydu"]
        toklbf = stA["toklbf"]
        if c == 0:
            E.wait("sync", "DW", tok_gwos)
        garing.i = 0
        garing.free = [("V", tok_ydu), ("V", tok_ydu)]
        E.wait("tensor", "V", toklbf)
        E.wait("scalar", "V", tok_ydu)
        toks = None
        for oo in range(8):
            j, offw, tokw = garing.load(lambda e, off, oo=oo: e.dma_start(
                bfv(A(off, 512)), gws[oo, :, :]))
            wap = bfv(A(offw, 512)).rearrange("p (a b) -> p a b", a=8)
            sl = P.acquire(1)
            psg = pt[:, sl[0] * 512: sl[0] * 512 + CH]
            E.wait("tensor", "DS", tokw)
            tokt = None
            for dd in range(8):
                tokt = E.emit("tensor", lambda e, dd=dd, wap=wap, psg=psg: e.matmul(
                    psg, wap[:, dd, :], lnxbf[:, dd, :], start=(dd == 0), stop=(dd == 7)), inc="T")
            E.wait("scalar", "T", tokt)
            toks = E.emit("scalar", lambda e, psg=psg, oo=oo: e.activation(
                scr_z[:, oo, :], psg, AF.Silu, bias=b_gb[:, oo:oo + 1]), inc="S")
            P.release(sl, "S", toks)
            garing.consumed(j, "T", tokt)
        E.wait("vector", "S", toks)
        E.emit("vector", lambda e: e.tensor_mul(ybuf, ybuf, scr_z))
        tokyb = E.emit("vector", lambda e: e.tensor_copy(ubf, ybuf), inc="V")
        E.wait("tensor", "V", tokyb)
        tokvz = None
        for oo in range(8):
            j, offw, tokw = garing.load(lambda e, off, oo=oo: e.dma_start(
                bfv(A(off, 512)), ows[oo, :, :]))
            wap = bfv(A(offw, 512)).rearrange("p (a b) -> p a b", a=8)
            sl = P.acquire(1)
            psg = pt[:, sl[0] * 512: sl[0] * 512 + CH]
            E.wait("tensor", "DS", tokw)
            tokt = None
            for dd in range(8):
                tokt = E.emit("tensor", lambda e, dd=dd, wap=wap, psg=psg: e.matmul(
                    psg, wap[:, dd, :], ubf[:, dd, :], start=(dd == 0), stop=(dd == 7)), inc="T")
            E.wait("vector", "T", tokt)
            tokvz = E.emit("vector", lambda e, psg=psg, oo=oo: e.scalar_tensor_tensor(
                lnx1[:, oo, 3:3 + CH], psg, b_ob[:, oo:oo + 1], lnx1[:, oo, 3:3 + CH],
                ALU.add, ALU.add), inc="V")
            P.release(sl, "V", tokvz)
            garing.consumed(j, "T", tokt)
        layer_norm(lnx1[:, :, 3:3 + CH], CH, 1e-5, lnxbf, ubf,
                   gcol=b_lng, bcol=b_lnb, add_to=sc_strm, war_tok=tokt)
        snapA[0] = E.snap()

    snapB = [None]
    snapBh = [None]

    def phaseB_head(c):
        if snapB[0] is not None:
            for eng in ("vector", "scalar", "tensor", "sync"):
                E.wait_snap(eng, snapB[0])
        t0 = c * CH
        sc_strm = strm[:, :, t0:t0 + CH]
        lnret = layer_norm(sc_strm, CH, 1e-5, lnx2bf, zBbf, out_f32=lnx2, out_bf=lnx2bf)
        E.wait("tensor", "V", lnret)
        tokm = None
        for ts_ in range(2):
            sl = P.acquire(1)
            psl = pt[:, sl[0] * 512: sl[0] * 512 + 8]
            tokt = None
            for dd in range(8):
                tokt = E.emit("tensor", lambda e, dd=dd, psl=psl, ts_=ts_: e.matmul(
                    psl[:, 0:4], lnx2[:, dd, ts_ * 128:(ts_ + 1) * 128], wrf[:, dd, :],
                    start=(dd == 0), stop=(dd == 7)), inc="T")
            E.wait("vector", "T", tokt)
            E.emit_s("vector", lambda e: e.memset(lgsc[:, 0:8], -1e30))
            E.emit_s("vector", lambda e, psl=psl: e.tensor_copy(lgsc[:, 0:4], psl[:, 0:4]))
            E.emit_s("vector", lambda e: e.reduce_max(lgsc[:, 8:9], lgsc[:, 0:8], AX.X))
            E.emit_s("vector", lambda e: e.tensor_tensor(
                lgsc[:, 16:24], lgsc[:, 0:8], lgsc[:, 8:9].broadcast_to([128, 8]), ALU.is_ge))
            E.emit_s("vector", lambda e: e.tensor_tensor_scan(
                lgsc[:, 24:32], c_onesf4[:, 0:1].broadcast_to([128, 8]), lgsc[:, 16:24],
                0.0, ALU.mult, ALU.add))
            E.emit_s("vector", lambda e: e.tensor_scalar(
                lgsc[:, 24:32], lgsc[:, 24:32], 1.0, None, op0=ALU.is_le))
            tokv = E.emit_s("vector", lambda e: e.tensor_mul(
                lgsc[:, 24:32], lgsc[:, 24:32], lgsc[:, 16:24]))
            P.release(sl, "V", tokv)
            sl2 = P.acquire(1)
            pst = pt[0:4, sl2[0] * 512: sl2[0] * 512 + 128]
            E.wait("tensor", "V", tokv)
            tokt2 = E.emit("tensor", lambda e, pst=pst: e.transpose(
                pst, lgsc[:, 24:28], c_identf), inc="T")
            E.wait("vector", "T", tokt2)
            tokm = E.emit("vector", lambda e, pst=pst, ts_=ts_: e.tensor_copy(
                mfm_st[0:4, ts_ * 128:(ts_ + 1) * 128], pst), inc="V")
            P.release(sl2, "V", tokm)
        E.wait("sync", "V", tokm)
        tokd = E.emit("sync", lambda e: e.dma_start(maskscr[:, :], mfm_st[0:4, 0:256]), inc="DS")
        E.wait("sync", "DS", tokd)
        tokd2 = E.emit("sync", lambda e: e.dma_start(
            mrep, bass.AP(maskscr.tensor, 0, [[0, 128], [CH, 4], [1, CH]])), inc="DS")
        snapBh[0] = E.snap()
        snapBh[0]["_tokd2"] = tokd2

    def phaseB_body(c):
        tokd2 = snapBh[0]["_tokd2"]
        first = snapB[0] is None
        if not first:
            for eng in ("vector", "scalar", "tensor", "sync", "gpsimd"):
                E.wait_snap(eng, snapB[0])
        for eng in ("vector", "scalar", "tensor", "sync"):
            E.wait_snap(eng, snapBh[0])
        t0 = c * CH
        sc_strm = strm[:, :, t0:t0 + CH]
        E.wait("vector", "DS", tokd2)
        E.wait("gpsimd", "DS", tokd2)
        tokcmb = None
        gtok = [None, None]
        for ee in range(4):
            for grp in range(8):
                if first:
                    E.wait("scalar", "DW", w1_tok[ee * 8 + grp])
                    E.wait("gpsimd", "DW", w1_tok[ee * 8 + grp])
                j, offw, tokw = w1ring.load(lambda e, off, ee=ee, grp=grp: e.dma_start(
                    bfv(A(off, 2048)), w1s[ee * 8 + grp, :, :]))
                wap = bfv(A(offw, 2048)).rearrange("p (a b) -> p a b", a=8)
                slots = P.acquire(4)
                E.wait("tensor", w1ring.ch[j], tokw)
                tokt = None
                for ii in range(4):
                    psn = pt[:, slots[ii] * 512: slots[ii] * 512 + CH]
                    for dd in range(8):
                        tokt = E.emit("tensor", lambda e, dd=dd, ii=ii, psn=psn, wap=wap: e.matmul(
                            psn, wap[:, dd, ii * 128:(ii + 1) * 128], lnx2bf[:, dd, :],
                            start=(dd == 0), stop=(dd == 7)), inc="T")
                w1ring.consumed(j, "T", tokt)
                E.wait("scalar", "T", tokt)
                for ii in range(4):
                    ic = grp * 4 + ii
                    psn = pt[:, slots[ii] * 512: slots[ii] * 512 + CH]
                    gb_ = gscr[ii % 2]
                    if gtok[ii % 2] is not None:
                        E.wait("scalar", "G", gtok[ii % 2])
                    toks = E.emit("scalar", lambda e, psn=psn, gb_=gb_, ee=ee, ic=ic: e.activation(
                        gb_[:, 0:CH], psn, AF.Gelu, bias=b_b1[:, ee, ic:ic + 1]), inc="S")
                    P.release([slots[ii]], "S", toks)
                    E.wait("gpsimd", "S", toks)
                    if ee == 0:
                        tokcmb = E.emit("gpsimd", lambda e, gb_=gb_, ic=ic: e.tensor_mul(
                            interb[:, ic, :], gb_[:, 0:CH], mrep[:, 0, :]), inc="G")
                    else:
                        E.emit("gpsimd", lambda e, gb_=gb_, ee=ee: e.tensor_mul(
                            gb_[:, 0:CH], gb_[:, 0:CH], mrep[:, ee, :]))
                        tokcmb = E.emit("gpsimd", lambda e, gb_=gb_, ic=ic: e.tensor_add(
                            interb[:, ic, :], interb[:, ic, :], gb_[:, 0:CH]), inc="G")
                    gtok[ii % 2] = tokcmb
        E.wait("tensor", "G", tokcmb)
        for oo in range(8):
            if first:
                E.wait("scalar", "DW", wo_tok[oo])
                E.wait("gpsimd", "DW", wo_tok[oo])
            j, offw, tokw = woring.load(lambda e, off, oo=oo: e.dma_start(
                bfv(A(off, 2048)), wos[oo, :, :]))
            wap = bfv(A(offw, 2048)).rearrange("p (a b) -> p a b", a=32)
            sl = P.acquire(1)
            psn = pt[:, sl[0] * 512: sl[0] * 512 + CH]
            E.wait("tensor", woring.ch[j], tokw)
            tokt = None
            for ii in range(32):
                tokt = E.emit("tensor", lambda e, ii=ii, psn=psn, wap=wap: e.matmul(
                    psn, wap[:, ii, :], interb[:, ii, :], start=(ii == 0), stop=(ii == 31)), inc="T")
            woring.consumed(j, "T", tokt)
            E.wait("vector", "T", tokt)
            tokvz = E.emit("vector", lambda e, psn=psn, oo=oo: e.scalar_tensor_tensor(
                zB[:, oo, :], psn, b_bo[:, oo:oo + 1], lnx2[:, oo, :],
                ALU.add, ALU.add), inc="V")
            P.release(sl, "V", tokvz)
        layer_norm(zB, CH, 1e-12, zBbf, lnx2bf, gcol=b_g2, bcol=b_b2,
                   add_to=sc_strm, war_tok=tokt)
        snapB[0] = E.snap()

    if KPH >= 2:
        phaseA_head(0)
        phaseA_scan(0)
        phaseA_tail(0)
        phaseB_head(0)
        for k in range(1, NCH):
            phaseA_head(k)
            phaseA_scan(k)
            phaseB_body(k - 1)
            phaseA_tail(k)
            phaseB_head(k)
        phaseB_body(NCH - 1)
    else:
        for k in range(NCH):
            phaseA_head(k)
            phaseA_scan(k)
            phaseA_tail(k)
    E.barrier()
    if DBG:
        E.emit("sync", lambda e: e.dma_start(dbg["d_a"], A(0, 16384)), inc="DS")
        E.barrier(("sync",))

    def phaseC():
        oc_K = 0
        oc_V = 8192
        oc_lnxout = 16512
        oc_lnxbf = 32896
        oc_qp = 41088
        oc_PT = 43136
        oc_qrep = 45184
        oc_rows = 47232
        oc_small = 49280
        oc_ring = 49536
        oc_srow = 50560
        Kb = bfv(A(oc_K, 8192)).rearrange("p (a b) -> p a b", a=8)
        Vb = bfv(A(oc_V, 8320)).rearrange("p (a b) -> p a b", a=16)
        lnxo = A(oc_lnxout, 16384).rearrange("p (a b) -> p a b", a=8)
        lnxb = bfv(A(oc_lnxbf, 8192)).rearrange("p (a b) -> p a b", a=8)
        qrep = A(oc_qrep, 2048)
        negrow = arena_t[64:65, oc_rows:oc_rows + 2048]
        negfm = A(oc_small, 16)
        biasfm = A(oc_small + 16, 16)
        mxcols = A(oc_small + 32, 48)
        ccol = A(oc_small + 80, 4)
        srowC = arena_t[64:65, oc_qrep:oc_qrep + 2048]
        wring = Ring(oc_ring, 2, 512, eng="gpsimd", ch="DW")

        for c4 in range(4):
            t0 = c4 * 512
            sc = strm[:, :, t0:t0 + 512]
            xbf_s = bfv(A(oc_qp, 2048)).rearrange("p (a b) -> p a b", a=8)
            sq_s = bfv(A(oc_PT, 2048)).rearrange("p (a b) -> p a b", a=8)
            layer_norm(sc, 512, 1e-5, xbf_s, sq_s,
                       out_f32=lnxo[:, :, t0:t0 + 512], out_bf=lnxb[:, :, t0:t0 + 512],
                       srow_t=srowC)
            sl = P.acquire(1)
            ps = pt[0:1, sl[0] * 512: sl[0] * 512 + 512]
            E.wait("tensor", "V", E.tok("V"))
            tokt = None
            for dd in range(8):
                tokt = E.emit("tensor", lambda e, dd=dd, ps=ps, t0=t0: e.matmul(
                    ps[:, 0:512], c_onesf4[:, 0:1], lnxo[:, dd, t0:t0 + 512],
                    start=(dd == 0), stop=(dd == 7)), inc="T")
            E.wait("vector", "T", tokt)
            E.emit_s("vector", lambda e, ps=ps, t0=t0: e.tensor_scalar(
                negrow[:, t0:t0 + 512], ps[:, 0:512], 0.0, None, op0=ALU.is_equal))
            tokv = E.emit_s("vector", lambda e, t0=t0: e.tensor_scalar(
                negrow[:, t0:t0 + 512], negrow[:, t0:t0 + 512], float(NEG), None,
                op0=ALU.mult))
            P.release(sl, "V", tokv)
        if DBG:
            E.barrier(("sync",))
            E.emit("sync", lambda e: e.dma_start(dbg["d_b"], A(16512, 16384)), inc="DS")
            E.barrier(("sync",))
        E.wait("sync", "V", E.tok("V"))
        tok_sp = None
        for c4 in range(4):
            tok_sp = E.emit("sync", lambda e, c4=c4: e.dma_start(
                strmspill[:, c4 * 4096:(c4 + 1) * 4096], A(c4 * 4096, 4096)), inc="DS")
        tokn = E.emit("sync", lambda e: e.dma_start(rowscr[0, :], negrow[0:1, :]), inc="DS")
        E.wait("sync", "DS", tokn)
        tokn2 = E.emit("sync", lambda e: e.dma_start(
            negfm, bass.AP(rowscr.tensor, 0, [[1, 128], [128, 16]]),
            allow_slow_non_contiguous=True), inc="DS")

        E.wait("scalar", "DS", tok_sp)
        toks = None
        for oo in range(8):
            j, offw, tokw = wring.load(lambda e, off, oo=oo: e.dma_start(
                bfv(A(off, 512)).rearrange("p (a b) -> p a b", a=8),
                wk[:, oo * 128:(oo + 1) * 128].rearrange("(a p) b -> p a b", p=128)))
            wap = bfv(A(offw, 512)).rearrange("p (a b) -> p a b", a=8)
            E.wait("tensor", "DW", tokw)
            tokt = None
            for pc in range(4):
                sl = P.acquire(1)
                psn = pt[:, sl[0] * 512: sl[0] * 512 + 512]
                for dd in range(8):
                    tokt = E.emit("tensor", lambda e, dd=dd, psn=psn, wap=wap, pc=pc: e.matmul(
                        psn, wap[:, dd, :], lnxb[:, dd, pc * 512:(pc + 1) * 512],
                        start=(dd == 0), stop=(dd == 7)), inc="T")
                E.wait("scalar", "T", tokt)
                toks = E.emit("scalar", lambda e, psn=psn, oo=oo, pc=pc: e.activation(
                    Kb[:, oo, pc * 512:(pc + 1) * 512], psn, AF.Relu,
                    bias=b_bk[:, oo:oo + 1]), inc="S")
                P.release(sl, "S", toks)
            wring.consumed(j, "T", tokt)

        E.wait("vector", "DS", tok_sp)
        tokm = None
        for tt_ in range(16):
            tokm = E.emit("vector", lambda e, tt_=tt_: e.memset(
                Vb[:, tt_, :].rearrange("p (a b) -> p a b", a=16)[:, :, 64:65], 1.0), inc="V")
        wvb = bfv(A(oc_qp, 4096)).rearrange("p (a b) -> p a b", a=8)
        tokbv = E.emit("sync", lambda e: e.dma_start(bvrep_b, bvrep), inc="DS")
        E.wait("vector", "DS", tokbv)
        tokwv = E.emit("gpsimd", lambda e: e.dma_start(
            wvb, wv.rearrange("(a p) b -> p a b", p=128)), inc="DW")
        E.wait("tensor", "DW", tokwv)
        for tt_ in range(16):
            sl = P.acquire(2)
            psn = pt[:, sl[0] * 512: sl[0] * 512 + 1024]
            tokt = None
            for kk in range(8):
                for hhalf in range(2):
                    tokt = E.emit("tensor", lambda e, kk=kk, tt_=tt_, psn=psn, hhalf=hhalf: e.matmul(
                        psn[:, hhalf * 512:(hhalf + 1) * 512],
                        lnxb[:, kk, tt_ * 128:(tt_ + 1) * 128],
                        wvb[:, kk, hhalf * 512:(hhalf + 1) * 512],
                        start=(kk == 0), stop=(kk == 7)), inc="T")
            E.wait("vector", "T", tokt)
            tokvv = E.emit("vector", lambda e, psn=psn: e.tensor_add(
                psn, psn, bvrep_b), inc="V")
            E.wait("scalar", "V", tokvv)
            if tokm is not None:
                E.wait("scalar", "V", tokm)
            dst = Vb[:, tt_, :].rearrange("p (a b) -> p a b", a=16)[:, :, 0:64]
            toksv = E.emit("scalar", lambda e, psn=psn, dst=dst: e.activation(
                dst, psn.rearrange("p (a b) -> p a b", a=16), AF.Relu), inc="S")
            P.release(sl, "S", toksv)
        E.wait("vector", "DS", tokn2)

        qread = [None, None]
        qlast = [None, None]
        mxc = [mxcols[:, 0:48], A(oc_small + 128, 48)]
        ccol2 = [ccol, A(oc_small + 176, 4)]
        biasfm2 = [biasfm, A(oc_small + 180, 16)]

        def emit_qpair(hp):
            par = hp % 2
            qbuf = bfv(A(oc_qp + par * 1024, 1024))
            if qread[par] is not None:
                E.wait("scalar", "T", qread[par])
            j, offw, tokw = wring.load(lambda e, off, hp=hp: e.dma_start(
                bfv(A(off, 512)).rearrange("p (a b) -> p a b", a=8),
                wq[:, hp * 128:(hp + 1) * 128].rearrange("(a p) b -> p a b", p=128)))
            wap = bfv(A(offw, 512)).rearrange("p (a b) -> p a b", a=8)
            E.wait("tensor", "DW", tokw)
            tokt = None
            toks = None
            for pc in range(4):
                sl = P.acquire(1)
                psn = pt[:, sl[0] * 512: sl[0] * 512 + 512]
                for dd in range(8):
                    tokt = E.emit("tensor", lambda e, dd=dd, psn=psn, wap=wap, pc=pc: e.matmul(
                        psn, wap[:, dd, :], lnxb[:, dd, pc * 512:(pc + 1) * 512],
                        start=(dd == 0), stop=(dd == 7)), inc="T")
                E.wait("scalar", "T", tokt)
                toks = E.emit("scalar", lambda e, psn=psn, qbuf=qbuf, pc=pc, hp=hp: e.activation(
                    qbuf[:, pc * 512:(pc + 1) * 512], psn, AF.Relu,
                    bias=b_bq[:, hp:hp + 1], scale=0.125), inc="S")
                P.release(sl, "S", toks)
            wring.consumed(j, "T", tokt)
            qlast[par] = toks

        def emit_pass1(h):
            hp, hh, par = h // 2, h % 2, (h // 2) % 2
            qbuf = bfv(A(oc_qp + par * 1024, 1024))
            E.wait("tensor", "S", qlast[par])
            nmx = 0
            tokv = None
            for qc in range(16):
                span = (qc + 1) * 128
                for pc in range((span + 511) // 512):
                    c0 = pc * 512
                    c1 = min(c0 + 512, span)
                    sl = P.acquire(1)
                    psn = pt[:, sl[0] * 512: sl[0] * 512 + (c1 - c0)]
                    tokt = E.emit("tensor", lambda e, psn=psn, qbuf=qbuf, hh=hh, hp=hp, qc=qc, c0=c0, c1=c1: e.matmul(
                        psn, qbuf[hh * 64:hh * 64 + 64, qc * 128:(qc + 1) * 128],
                        Kb[hh * 64:hh * 64 + 64, hp, c0:c1],
                        start=True, stop=True), inc="T")
                    E.wait("vector", "T", tokt)
                    tokv = E.emit("vector", lambda e, psn=psn, nmx=nmx, h=h: e.reduce_max(
                        mxc[h % 2][:, nmx:nmx + 1], psn, AX.X), inc="V")
                    P.release(sl, "V", tokv)
                    nmx += 1
            return nmx, tokv

        def emit_cchain(h, nmx, tokv):
            cc = ccol2[h % 2]
            bf_ = biasfm2[h % 2]
            E.wait("vector", "V", tokv)
            tokv = E.emit_s("vector", lambda e, nmx=nmx, h=h: e.reduce_max(
                cc[:, 0:1], mxc[h % 2][:, 0:nmx], AX.X))
            sl = P.acquire(1)
            pst = pt[0:1, sl[0] * 512: sl[0] * 512 + 128]
            E.wait("tensor", "V", tokv)
            tokt = E.emit("tensor", lambda e, pst=pst, cc=cc: e.transpose(
                pst, cc[:, 0:1], c_identf), inc="T")
            E.wait("vector", "T", tokt)
            tokv = E.emit_s("vector", lambda e, pst=pst, cc=cc: e.reduce_max(
                cc[0:1, 1:2], pst, AX.X))
            P.release(sl, "V", tokv)
            sl = P.acquire(1)
            psc = pt[:, sl[0] * 512: sl[0] * 512 + 128]
            E.wait("tensor", "V", tokv)
            tokt = E.emit("tensor", lambda e, psc=psc, cc=cc: e.matmul(
                psc[:, 0:1], c_ones1, cc[0:1, 1:2], start=True, stop=True), inc="T")
            E.wait("vector", "T", tokt)
            tokv = E.emit_s("vector", lambda e, psc=psc, bf_=bf_: e.tensor_tensor(
                bf_, negfm, psc[:, 0:1].broadcast_to([128, 16]), ALU.subtract))
            P.release(sl, "V", tokv)
            E.wait("scalar", "V", tokv)

        def emit_pass2_pv(h):
            hp, hh, par = h // 2, h % 2, (h // 2) % 2
            qbuf = bfv(A(oc_qp + par * 1024, 1024))
            bf_ = biasfm2[h % 2]
            pvsl = P.acquire(4)
            P.reserve(pvsl)
            pv = pt[0:65, pvsl[0] * 512: pvsl[0] * 512 + 2048]
            ptok = [None, None]
            tokpv = None
            for kc in range(16):
                q0 = kc * 128
                pc0 = q0 // 512
                pbuf = PTb[kc % 2]
                if ptok[kc % 2] is not None:
                    E.wait("scalar", "T", ptok[kc % 2])
                toks2 = None
                for pc in range(pc0, 4):
                    c0 = pc * 512
                    sl = P.acquire(1)
                    psn = pt[:, sl[0] * 512: sl[0] * 512 + 512]
                    tokt = E.emit("tensor", lambda e, psn=psn, qbuf=qbuf, hh=hh, hp=hp, kc=kc, c0=c0: e.matmul(
                        psn, Kb[hh * 64:hh * 64 + 64, hp, kc * 128:(kc + 1) * 128],
                        qbuf[hh * 64:hh * 64 + 64, c0:c0 + 512],
                        start=True, stop=True), inc="T")
                    E.wait("scalar", "T", tokt)
                    toks2 = E.emit("scalar", lambda e, psn=psn, pbuf=pbuf, c0=c0, kc=kc, bf_=bf_: e.activation(
                        pbuf[:, c0:c0 + 512], psn, AF.Exp, bias=bf_[:, kc:kc + 1]), inc="S")
                    P.release(sl, "S", toks2)
                E.wait("vector", "S", toks2)
                tokdv = E.emit("vector", lambda e, pbuf=pbuf, q0=q0: e.tensor_mul(
                    pbuf[:, q0:q0 + 128], pbuf[:, q0:q0 + 128], c_tril), inc="V")
                E.wait("tensor", "V", tokdv)
                for pc in range(pc0, 4):
                    c0 = max(pc * 512, q0)
                    c1 = (pc + 1) * 512
                    kc_last = min(15, (pc + 1) * 4 - 1)
                    tokpv = E.emit("tensor", lambda e, pv=pv, kc=kc, pbuf=pbuf, c0=c0, c1=c1, h=h, kl=kc_last: e.matmul(
                        pv[:, c0:c1],
                        Vb[:, kc, h * 65:(h + 1) * 65],
                        pbuf[:, c0:c1],
                        start=(kc == 0), stop=(kc == kl)), inc="T")
                ptok[kc % 2] = tokpv
            if hh == 1:
                qread[par] = tokpv
            return pvsl, pv, tokpv

        def emit_epilogue(h, pvsl, pv, tokpv):
            E.wait("vector", "T", tokpv)
            E.emit_s("vector", lambda e, pv=pv: e.scalar_tensor_tensor(
                srowC[0:1, 0:2048], negrow, float(1e38 / NEG), pv[64:65, :],
                ALU.mult, ALU.add))
            E.emit_s("vector", lambda e: e.tensor_scalar_max(
                srowC[0:1, 0:2048], srowC[0:1, 0:2048], 1e-35))
            tokv = E.emit_s("vector", lambda e: e.reciprocal(
                srowC[0:1, 0:2048], srowC[0:1, 0:2048]))
            E.wait("sync", "V", tokv)
            tokd = E.emit("sync", lambda e: e.dma_start(rowscr[0, :], srowC[0:1, 0:2048]), inc="DS")
            E.wait("sync", "DS", tokd)
            tokd2 = E.emit("sync", lambda e: e.dma_start(
                qrep[0:64, :], bass.AP(rowscr.tensor, 0, [[0, 64], [1, 2048]])), inc="DS")
            E.wait("vector", "DS", tokd2)
            tokv = E.emit("vector", lambda e, pv=pv: e.tensor_mul(
                qrep[0:64, :], qrep[0:64, :], pv[0:64, :]), inc="V")
            P.unreserve(pvsl)
            P.release(pvsl, "V", tokv)
            dst = lnxo[(h % 2) * 64:(h % 2) * 64 + 64, h // 2, :]
            E.wait("gpsimd", "V", tokv)
            tokg = E.emit("gpsimd", lambda e, dst=dst: e.dma_start(
                dst, qrep[0:64, :], accum_op=ALU.add), inc="DW")
            E.wait("sync", "DW", tokg)
            E.wait("vector", "DW", tokg)
            E.wait("tensor", "DW", tokg)

        emit_qpair(0)
        st_prev = None
        st_prev = emit_pass1(0)
        for h in range(16):
            emit_cchain(h, *st_prev)
            pvst = emit_pass2_pv(h)
            if h + 1 < 16:
                if (h + 1) % 2 == 0:
                    emit_qpair((h + 1) // 2)
                st_prev = emit_pass1(h + 1)
            emit_epilogue(h, *pvst)

        for c4 in range(4):
            t0 = c4 * 512
            xbf_s = bfv(A(oc_qp, 2048)).rearrange("p (a b) -> p a b", a=8)
            sq_s = bfv(A(oc_PT, 2048)).rearrange("p (a b) -> p a b", a=8)
            rl = A(oc_lnxbf, 4096).rearrange("p (a b) -> p a b", a=8)
            E.wait("sync", "V", E.tok("V"))
            E.wait("sync", "T", E.tok("T"))
            src_ap = bass.AP(strmspill.tensor, c4 * 512, [[16384, 128], [2048, 8], [1, 512]])
            dst_ap = bass.AP(strm2d.tensor, c4 * 512, [[16384, 128], [2048, 8], [1, 512]])
            tokd = E.emit("sync", lambda e, src_ap=src_ap: e.dma_start(rl, src_ap), inc="DS")
            E.wait("vector", "DS", tokd)
            lnret = layer_norm(lnxo[:, :, t0:t0 + 512], 512, 1e-8, xbf_s, sq_s,
                               gcol=b_gam, bcol=b_bet, add_to=rl, srow_t=srowC,
                               war_tok=E.tok("T"))
            E.wait("sync", "V", lnret)
            tokd = E.emit("sync", lambda e, dst_ap=dst_ap: e.dma_start(dst_ap, rl), inc="DS")
            E.wait("vector", "DS", tokd)

    PTb = [bfv(A(43136, 1024)), bfv(A(43136 + 1024, 1024))]
    bvrep_b = A(45184, 1024)

    if KPH >= 3:
        E.barrier()
        phaseC()
        E.barrier()

    def phaseD():
        od_lnxD = 0
        od_strm = 8192
        od_w13 = 24576
        od_w13b = 32768
        od_w2 = 40960
        od_w2b = 45056
        od_g = 49152
        od_cr = 50176
        od_sc = 51200
        lnxD = bfv(A(od_lnxD, 8192)).rearrange("p (a b) -> p a b", a=8)
        acc = A(od_strm, 16384).rearrange("p (a b) -> p a b", a=8)
        gbuf = bfv(A(od_g, 1024)).rearrange("p (a b) -> p a b", a=8)
        crep = bfv(A(od_cr, 1024))
        s1scr = bfv(A(od_sc, 128))
        h3scr = bfv(A(od_sc + 128, 128))
        lgD = A(od_sc + 256, 64)
        g8b = bfv(A(od_sc + 320, 32))
        gb8s = A(od_sc + 352, 8)
        cfm_st = bfv(A(od_g, 1024))[0:8, 0:2048]
        outst = A(od_cr, 1024)
        srowD = arena_t[0:1, od_w13 + 4096:od_w13 + 4096 + 2048]

        g8b3 = g8b[:, 0:64].rearrange("p (a b) -> p a b", a=8)
        tokg8 = E.emit("gpsimd", lambda e: e.dma_start(
            g8b3, g8T.rearrange("(a p) b -> p a b", p=128)), inc="DW")
        tokb8 = E.emit("sync", lambda e: e.dma_start(gb8s, gb8), inc="DS")
        tokrl = None
        for c4 in range(4):
            tokrl = E.emit("sync", lambda e, c4=c4: e.dma_start(
                A(od_strm + c4 * 4096, 4096), strm2d[:, c4 * 4096:(c4 + 1) * 4096]), inc="DS")
        E.wait("vector", "DS", tokrl)
        E.wait("tensor", "DS", tokrl)
        for c4 in range(4):
            t0 = c4 * 512
            xbf_s = bfv(A(od_w13, 2048)).rearrange("p (a b) -> p a b", a=8)
            sq_s = bfv(A(od_w13 + 2048, 2048)).rearrange("p (a b) -> p a b", a=8)
            layer_norm(acc[:, :, t0:t0 + 512], 512, 1e-5, xbf_s, sq_s,
                       out_bf=lnxD[:, :, t0:t0 + 512], srow_t=srowD)
        if DBG:
            E.barrier(("sync",))
            E.emit("sync", lambda e: e.dma_start(dbg["d_l"], bfv(A(od_lnxD, 8192))), inc="DS")
            E.barrier(("sync",))
        E.wait("tensor", "DW", tokg8)
        E.wait("vector", "DS", tokb8)
        g8v = g8b[:, 0:8].rearrange("p (a b) -> p a b", a=1) if False else None
        tokcf = None
        for tc in range(16):
            sl = P.acquire(1)
            psl = pt[:, sl[0] * 512: sl[0] * 512 + 8]
            E.wait("tensor", "V", E.tok("V"))
            tokt = None
            for dd in range(8):
                tokt = E.emit("tensor", lambda e, dd=dd, psl=psl, tc=tc: e.matmul(
                    psl[:, 0:8], lnxD[:, dd, tc * 128:(tc + 1) * 128],
                    g8b3[:, dd, :],
                    start=(dd == 0), stop=(dd == 7)), inc="T")
            E.wait("vector", "T", tokt)
            E.wait("scalar", "T", tokt)
            E.emit_s("vector", lambda e, psl=psl: e.reduce_max(
                lgD[:, 48:49], psl[:, 0:8], AX.X, negate=True))
            E.wait("scalar", "V", E.tok("V"))
            toke = E.emit("scalar", lambda e, psl=psl: e.activation(
                lgD[:, 0:8], psl[:, 0:8], AF.Exp, bias=lgD[:, 48:49],
                accum_out=lgD[:, 49:50]), inc="S")
            E.wait("vector", "S", toke)
            E.emit_s("vector", lambda e: e.reciprocal(lgD[:, 50:51], lgD[:, 49:50]))
            E.emit_s("vector", lambda e: e.tensor_scalar(
                lgD[:, 0:8], lgD[:, 0:8], lgD[:, 50:51], None, op0=ALU.mult))
            E.emit_s("vector", lambda e: e.tensor_add(lgD[:, 8:16], lgD[:, 0:8], gb8s))
            E.emit_s("vector", lambda e: e.reduce_max(lgD[:, 51:52], lgD[:, 8:16], AX.X))
            E.emit_s("vector", lambda e: e.tensor_tensor(
                lgD[:, 16:24], lgD[:, 8:16], lgD[:, 51:52].broadcast_to([128, 8]), ALU.is_ge))
            E.emit_s("vector", lambda e: e.tensor_tensor_scan(
                lgD[:, 24:32], c_onesf4[:, 0:1].broadcast_to([128, 8]), lgD[:, 16:24],
                0.0, ALU.mult, ALU.add))
            E.emit_s("vector", lambda e: e.tensor_scalar(
                lgD[:, 24:32], lgD[:, 24:32], 1.0, None, op0=ALU.is_le))
            E.emit_s("vector", lambda e: e.tensor_mul(lgD[:, 24:32], lgD[:, 24:32], lgD[:, 16:24]))
            E.emit_s("vector", lambda e: e.scalar_tensor_tensor(
                lgD[:, 8:16], lgD[:, 24:32], -1e30, lgD[:, 8:16], ALU.mult, ALU.add))
            E.emit_s("vector", lambda e: e.reduce_max(lgD[:, 51:52], lgD[:, 8:16], AX.X))
            E.emit_s("vector", lambda e: e.tensor_tensor(
                lgD[:, 16:24], lgD[:, 8:16], lgD[:, 51:52].broadcast_to([128, 8]), ALU.is_ge))
            E.emit_s("vector", lambda e: e.tensor_tensor_scan(
                lgD[:, 32:40], c_onesf4[:, 0:1].broadcast_to([128, 8]), lgD[:, 16:24],
                0.0, ALU.mult, ALU.add))
            E.emit_s("vector", lambda e: e.tensor_scalar(
                lgD[:, 32:40], lgD[:, 32:40], 1.0, None, op0=ALU.is_le))
            E.emit_s("vector", lambda e: e.tensor_mul(lgD[:, 32:40], lgD[:, 32:40], lgD[:, 16:24]))
            E.emit_s("vector", lambda e: e.tensor_mul(lgD[:, 40:48], lgD[:, 24:32], lgD[:, 0:8]))
            E.emit_s("vector", lambda e: e.reduce_sum(lgD[:, 52:53], lgD[:, 40:48], AX.X))
            E.emit_s("vector", lambda e: e.tensor_mul(lgD[:, 40:48], lgD[:, 32:40], lgD[:, 0:8]))
            E.emit_s("vector", lambda e: e.reduce_sum(lgD[:, 53:54], lgD[:, 40:48], AX.X))
            E.emit_s("vector", lambda e: e.tensor_scalar(
                lgD[:, 40:48], lgD[:, 24:32], lgD[:, 52:53], None, op0=ALU.mult))
            E.emit_s("vector", lambda e: e.scalar_tensor_tensor(
                lgD[:, 32:40], lgD[:, 32:40], lgD[:, 53:54], lgD[:, 40:48],
                ALU.mult, ALU.add))
            tokv = E.emit_s("vector", lambda e: e.engine_nop())
            P.release(sl, "V", tokv)
            sl2 = P.acquire(1)
            pst = pt[0:8, sl2[0] * 512: sl2[0] * 512 + 128]
            E.wait("tensor", "V", tokv)
            tokt2 = E.emit("tensor", lambda e, pst=pst: e.transpose(
                pst, lgD[:, 32:40], c_identf), inc="T")
            E.wait("vector", "T", tokt2)
            tokcf = E.emit("vector", lambda e, pst=pst, tc=tc: e.tensor_copy(
                cfm_st[0:8, tc * 128:(tc + 1) * 128], pst), inc="V")
            P.release(sl2, "V", tokcf)
        E.wait("sync", "V", tokcf)
        tokcd = E.emit("sync", lambda e: e.dma_start(cfmscr[:, :], cfm_st), inc="DS")
        E.wait("sync", "DS", tokcd)
        E.wait("vector", "DS", tokcd)
        E.wait("scalar", "DS", tokcd)
        if DBG:
            E.emit("sync", lambda e: e.dma_start(dbg["d_r"], cfm_st), inc="DS")
            E.barrier(("sync",))

        gbuf2 = bfv(A(od_g, 2048)).rearrange("p (a b) -> p a b", a=8)
        s1s = bfv(A(od_sc, 256))
        crepS = bfv(A(od_sc + 256, 256))

        def w13_dma(step):
            e8, half = step // 2, step % 2
            buf = od_w13 if step % 2 == 0 else od_w13b
            toks = []
            for m in range(2):
                dst = bfv(A(buf + m * 4096, 4096)).rearrange("p (a b) -> p a b", a=8)
                toks.append(E.emit("gpsimd", lambda e, m=m, e8=e8, half=half, dst=dst: e.dma_start(
                    dst, W13[m, e8, :, half * 1024:(half + 1) * 1024].rearrange(
                        "(a p) b -> p a b", p=128)), inc="DW"))
            buf2 = od_w2 if step % 2 == 0 else od_w2b
            dst2 = bfv(A(buf2, 4096)).rearrange("p (a b) -> p a b", a=8)
            tok2 = E.emit("gpsimd", lambda e, e8=e8, half=half, dst2=dst2: e.dma_start(
                dst2, W2[e8, half * 1024:(half + 1) * 1024, :].rearrange(
                    "(a p) b -> p a b", p=128)), inc="DW")
            return toks[-1], tok2

        E.wait("gpsimd", "T", E.tok("T"))
        E.wait("gpsimd", "V", E.tok("V"))
        wtoks = {0: w13_dma(0)}
        wfree = {}
        for e8 in range(9):
            shared = (e8 == 8)
            for half in range(2):
                step = e8 * 2 + half
                if step + 1 < 18:
                    if step - 1 in wfree:
                        E.wait("gpsimd", "T", wfree[step - 1])
                    wtoks[step + 1] = w13_dma(step + 1)
                tok13, tok2w = wtoks[step]
                buf = od_w13 if step % 2 == 0 else od_w13b
                buf2 = od_w2 if step % 2 == 0 else od_w2b
                w1v = bfv(A(buf, 4096)).rearrange("p (a b) -> p a b", a=8)
                w3v = bfv(A(buf + 4096, 4096)).rearrange("p (a b) -> p a b", a=8)
                w2v = bfv(A(buf2, 4096)).rearrange("p (a b) -> p a b", a=8)
                E.wait("tensor", "DW", tok13)
                lastT = None
                for sect in range(4):
                    s0 = sect * 512
                    if not shared:
                        E.wait("sync", "V", E.tok("V"))
                        tokcr = E.emit("sync", lambda e, e8=e8, s0=s0: e.dma_start(
                            crepS, bass.AP(cfmscr.tensor, e8 * 2048 + s0,
                                           [[0, 128], [1, 512]])), inc="DS")
                        E.wait("vector", "DS", tokcr)
                    gtokv = None
                    for ii in range(8):
                        sl1 = P.acquire(1)
                        ph1 = pt[:, sl1[0] * 512: sl1[0] * 512 + 512]
                        tokt = None
                        for kk in range(8):
                            tokt = E.emit("tensor", lambda e, ph1=ph1, w1v=w1v, kk=kk, ii=ii, s0=s0: e.matmul(
                                ph1, w1v[:, kk, ii * 128:(ii + 1) * 128],
                                lnxD[:, kk, s0:s0 + 512],
                                start=(kk == 0), stop=(kk == 7)), inc="T")
                        sl3 = P.acquire(1)
                        ph3 = pt[:, sl3[0] * 512: sl3[0] * 512 + 512]
                        tokt3 = None
                        for kk in range(8):
                            tokt3 = E.emit("tensor", lambda e, ph3=ph3, w3v=w3v, kk=kk, ii=ii, s0=s0: e.matmul(
                                ph3, w3v[:, kk, ii * 128:(ii + 1) * 128],
                                lnxD[:, kk, s0:s0 + 512],
                                start=(kk == 0), stop=(kk == 7)), inc="T")
                        E.wait("scalar", "T", tokt)
                        if gtokv is not None:
                            E.wait("scalar", "V", gtokv)
                        toks1 = E.emit("scalar", lambda e, ph1=ph1: e.activation(
                            s1s[:, 0:512], ph1, AF.Silu), inc="S")
                        P.release(sl1, "S", toks1)
                        E.wait("vector", "S", toks1)
                        if not shared:
                            E.emit("vector", lambda e: e.tensor_mul(
                                s1s[:, 0:512], s1s[:, 0:512], crepS[:, 0:512]))
                        E.wait("vector", "T", tokt3)
                        gtokv = E.emit("vector", lambda e, ph3=ph3, ii=ii: e.tensor_mul(
                            gbuf2[:, ii, :], s1s[:, 0:512], ph3), inc="V")
                        P.release(sl3, "V", gtokv)
                    if DBG and e8 == 0 and half == 0 and sect == 0:
                        E.barrier(("sync",))
                        E.emit("sync", lambda e: e.dma_start(dbg["d_g"], bfv(A(od_g, 2048))), inc="DS")
                        E.barrier(("sync",))
                    E.wait("tensor", "DW", tok2w)
                    E.wait("tensor", "V", gtokv)
                    for ddg in range(2):
                        osl = P.acquire(4)
                        P.reserve(osl)
                        ops = pt[:, osl[0] * 512: osl[0] * 512 + 2048]
                        tokw2 = None
                        for dd4 in range(4):
                            dd = ddg * 4 + dd4
                            for ii in range(8):
                                tokw2 = E.emit("tensor", lambda e, ops=ops, w2v=w2v, dd=dd, dd4=dd4, ii=ii: e.matmul(
                                    ops[:, dd4 * 512:dd4 * 512 + 512],
                                    w2v[:, ii, dd * 128:(dd + 1) * 128],
                                    gbuf2[:, ii, :],
                                    start=(ii == 0), stop=(ii == 7)), inc="T")
                        lastT = tokw2
                        E.wait("vector", "T", tokw2)
                        tokacc = E.emit("vector", lambda e, ops=ops, s0=s0, ddg=ddg: e.tensor_add(
                            acc[:, ddg * 4:(ddg + 1) * 4, s0:s0 + 512],
                            acc[:, ddg * 4:(ddg + 1) * 4, s0:s0 + 512],
                            ops.rearrange("p (a b) -> p a b", a=4)), inc="V")
                        P.unreserve(osl)
                        P.release(osl, "V", tokacc)
                wfree[step] = lastT
        E.wait("tensor", "V", E.tok("V"))
        outb2 = A(od_g, 1024)
        for tt_ in range(16):
            tokvo = None
            for dd in range(8):
                slo = P.acquire(1)
                apo = pt[:, slo[0] * 512: slo[0] * 512 + 128]
                tokto = E.emit("tensor", lambda e, apo=apo, dd=dd, tt_=tt_: e.transpose(
                    apo, acc[:, dd, tt_ * 128:(tt_ + 1) * 128], c_identf), inc="T")
                E.wait("vector", "T", tokto)
                tokvo = E.emit("vector", lambda e, apo=apo, dd=dd: e.tensor_copy(
                    outb2[:, dd * 128:(dd + 1) * 128], apo), inc="V")
                P.release(slo, "V", tokvo)
            E.wait("sync", "V", tokvo)
            tokdo = E.emit("sync", lambda e, tt_=tt_: e.dma_start(
                out[tt_ * 128:(tt_ + 1) * 128, :], outb2), inc="DS")
            E.wait("vector", "DS", tokdo)
            E.wait("tensor", "DS", tokdo)

    if KPH >= 4:
        E.barrier()
        phaseD()
        E.barrier()
    else:
        if KPH >= 3:
            for c4 in range(4):
                tokd = E.emit("sync", lambda e, c4=c4: e.dma_start(
                    A(c4 * 4096, 4096), strm2d[:, c4 * 4096:(c4 + 1) * 4096]), inc="DS")
            E.wait("sync", "DS", E.tok("DS"))
            E.wait("tensor", "DS", E.tok("DS"))
            E.wait("vector", "DS", E.tok("DS"))
            E.barrier()
        emit_out_from_strm(nc, E, P, pt, A, strm, c_identf, out)

    E.barrier()
    E.replay()
    stack.close()
    return nc


def emit_out_from_strm(nc, E, P, pt, A, strm, c_identf, out):
    o_ost = 16384
    ost = A(o_ost, 1024)
    for tt_ in range(16):
        tokv = None
        for dd in range(8):
            slots = P.acquire(1)
            apo = pt[:, slots[0] * 512: slots[0] * 512 + 128]
            tokt = E.emit("tensor", lambda e, apo=apo, dd=dd, tt_=tt_: e.transpose(
                apo, strm[:, dd, tt_ * 128:(tt_ + 1) * 128], c_identf), inc="T")
            E.wait("vector", "T", tokt)
            tokv = E.emit("vector", lambda e, apo=apo, dd=dd: e.tensor_copy(
                ost[:, dd * 128:(dd + 1) * 128], apo), inc="V")
            P.release(slots, "V", tokv)
        E.wait("sync", "V", tokv)
        tokd = E.emit("sync", lambda e, tt_=tt_: e.dma_start(
            out[tt_ * 128:(tt_ + 1) * 128, :], ost), inc="DS")
        E.wait("vector", "DS", tokd)
    E.wait("sync", "DS", E.tok("DS"))


def build_C(*a, **k):
    raise NotImplementedError


def build_D(*a, **k):
    raise NotImplementedError


_BF = ml_dtypes.bfloat16


def _fm(v):
    return np.ascontiguousarray(np.asarray(v, np.float32).reshape(8, 128).T)


_NC_CACHE = {}


def _get_nc():
    key = (KPH, DBG)
    if key not in _NC_CACHE:
        _NC_CACHE[key] = build_nc()
    return _NC_CACHE[key]


def kernel(**inp):
    f32 = np.float32
    g = {k: np.asarray(v) for k, v in inp.items()}
    im = {}
    im["cw"] = np.ascontiguousarray(
        g["L0_attn_conv_w"].reshape(1024, 4).reshape(8, 128, 4).transpose(1, 0, 2)).astype(f32)
    im["cb"] = _fm(g["L0_attn_conv_b"])
    im["Afm"] = np.ascontiguousarray(
        g["L0_attn_A"].reshape(8, 128, 16).transpose(1, 0, 2).reshape(128, 128)).astype(f32)
    im["Dfm"] = _fm(g["L0_attn_D"])
    im["bpwT"] = np.ascontiguousarray(g["L0_attn_Bp_w"].T).astype(f32)
    im["cpwT"] = np.ascontiguousarray(g["L0_attn_Cp_w"].T).astype(f32)
    im["bcb2"] = np.ascontiguousarray(
        np.stack([g["L0_attn_Bp_b"], g["L0_attn_Cp_b"]], axis=1)).astype(f32)
    im["gwT"] = np.ascontiguousarray(g["L0_attn_gate_w"].T).astype(f32)
    im["owT"] = np.ascontiguousarray(g["L0_attn_out_w"].T).astype(f32)
    im["gb"] = _fm(g["L0_attn_gate_b"])
    im["ob"] = _fm(g["L0_attn_out_b"])
    im["lng"] = _fm(g["L0_attn_ln_g"])
    im["lnb"] = _fm(g["L0_attn_ln_b"])
    im["wr"] = np.ascontiguousarray(g["L0_ffn_W_router"]).astype(f32)
    im["w1"] = np.ascontiguousarray(g["L0_ffn_W1"]).astype(f32)
    im["b1fm"] = np.ascontiguousarray(
        g["L0_ffn_b1"].reshape(4, 32, 128).transpose(2, 0, 1).reshape(128, 128)).astype(f32)
    im["woB"] = np.ascontiguousarray(
        g["L0_ffn_Wo"].reshape(4096, 8, 128).transpose(1, 0, 2)).astype(f32)
    im["bofm"] = _fm(g["L0_ffn_bo"])
    im["g2"] = _fm(g["L0_ffn_ln_gamma"])
    im["b2"] = _fm(g["L0_ffn_ln_beta"])
    im["wq"] = np.ascontiguousarray(g["L1_attn_Wq"]).astype(f32)
    im["wk"] = np.ascontiguousarray(g["L1_attn_Wk"]).astype(f32)
    im["wv"] = np.ascontiguousarray(g["L1_attn_Wv"]).astype(f32)
    im["bqfm"] = _fm(g["L1_attn_bq"] / 8.0) if False else _fm(np.asarray(g["L1_attn_bq"], f32) * 0.125)
    im["bkfm"] = _fm(g["L1_attn_bk"])
    im["bvfm"] = _fm(g["L1_attn_bv"])
    im["bvrep"] = np.ascontiguousarray(
        np.broadcast_to(np.asarray(g["L1_attn_bv"], f32), (128, 1024)))
    im["gamv"] = _fm(g["L1_attn_gamma"])
    im["betv"] = _fm(g["L1_attn_beta"])
    im["g8T"] = np.ascontiguousarray(g["L1_ffn_gate_w"].T).astype(f32)
    im["gb8"] = np.ascontiguousarray(
        np.broadcast_to(np.asarray(g["L1_ffn_gate_bias"], f32), (128, 8))).astype(f32)
    w1T = np.asarray(g["L1_ffn_w1"], f32).transpose(0, 2, 1)
    w3T = np.asarray(g["L1_ffn_w3"], f32).transpose(0, 2, 1)
    sw1T = np.asarray(g["L1_ffn_sw1"], f32).T[None]
    sw3T = np.asarray(g["L1_ffn_sw3"], f32).T[None]
    W13 = np.ascontiguousarray(np.stack([
        np.concatenate([w1T, sw1T], 0), np.concatenate([w3T, sw3T], 0)], 0))
    im["W13"] = W13
    w2T = np.asarray(g["L1_ffn_w2"], f32).transpose(0, 2, 1)
    sw2T = np.asarray(g["L1_ffn_sw2"], f32).T[None]
    im["W2"] = np.ascontiguousarray(np.concatenate([w2T, sw2T], 0))
    im["identf"] = np.eye(128, dtype=f32)
    tril = (np.arange(128)[None, :] >= np.arange(128)[:, None]).astype(_BF)
    im["trilT"] = np.ascontiguousarray(tril)
    im["ones1"] = np.ones((1, 128), f32)

    nc = _get_nc()
    xs = np.asarray(g["L0_attn_x"], f32)
    in_maps = [dict(im, x=np.ascontiguousarray(xs[b])) for b in range(8)]
    res = run_bass_kernel_spmd(nc, in_maps, core_ids=list(range(8)))
    outp = np.stack([np.asarray(r["out"]) for r in res.results], 0)
    if DBG:
        kernel._dbg = [{k2: np.asarray(v2) for k2, v2 in r.items() if k2 != "out"}
                       for r in res.results]
    return outp


# version 11
# speedup vs baseline: 1.0259x; 1.0224x over previous
import os
import contextlib
import numpy as np
import ml_dtypes
import concourse.bass as bass
import concourse.mybir as mybir
from concourse.bass_utils import run_bass_kernel_spmd

F32 = mybir.dt.float32
BF16 = mybir.dt.bfloat16
AF = mybir.ActivationFunctionType
ALU = mybir.AluOpType
AX = mybir.AxisListType

T = 2048
D = 1024
CH = 256
NCH = T // CH
NEG = -2.0 ** 32 + 1.0

DBG = int(os.environ.get("KDBG", "0"))
KRACE = int(os.environ.get("KRACE", "0"))
KPH = int(os.environ.get("KPH", "4"))

ENGINES = ("tensor", "vector", "scalar", "gpsimd", "sync")


class Emitter:
    def __init__(self, nc, stack):
        self.nc = nc
        self.q = {e: [] for e in ENGINES}
        self.chans = {}
        self.waited = {e: {} for e in ENGINES}
        self.stack = stack

    def chan(self, name, step=1):
        sem = self.stack.enter_context(self.nc.semaphore(name))
        self.chans[name] = [sem, 0, step]

    def tok(self, ch):
        return self.chans[ch][1]

    def emit(self, eng, fn, inc=None):
        tok = None
        if inc is not None:
            c = self.chans[inc]
            c[1] += c[2]
            tok = c[1]
        self.q[eng].append((fn, inc))
        return tok

    def wait(self, eng, ch, tok):
        if tok is None or tok <= 0:
            return
        w = self.waited[eng]
        if w.get(ch, 0) >= tok:
            return
        w[ch] = tok
        sem = self.chans[ch][0]
        self.q[eng].append((lambda e, s=sem, t=tok: e.wait_ge(s, t), "__wait__"))

    def snap(self):
        return {ch: c[1] for ch, c in self.chans.items()}

    def wait_snap(self, eng, snap):
        for ch, tok in snap.items():
            if ch.startswith("_"):
                continue
            self.wait(eng, ch, tok)

    def emit_s(self, eng, fn):
        ch = {"vector": "V", "scalar": "S", "tensor": "T", "gpsimd": "G"}[eng]
        tok = self.emit(eng, fn, inc=ch)
        self.wait(eng, ch, tok)
        return tok

    def barrier(self, engines=ENGINES):
        s = self.snap()
        for e in engines:
            self.wait_snap(e, s)

    def replay(self):
        qs = self.q
        chans = self.chans
        selfsems = {}
        if KRACE:
            for e in ENGINES:
                selfsems[e] = self.stack.enter_context(self.nc.semaphore("self_" + e))

        rcount = {ch: 0 for ch in chans}

        def mk(engname):
            def body(eng):
                total = [0]
                prev_chan = [None]
                for fn, inc in qs[engname]:
                    if inc == "__wait__":
                        fn(eng)
                        continue
                    if KRACE:
                        if total[0] > 0:
                            eng.wait_ge(selfsems[engname], total[0])
                        if prev_chan[0] is not None:
                            eng.wait_ge(chans[prev_chan[0]][0], rcount[prev_chan[0]])
                            prev_chan[0] = None
                    ins = fn(eng)
                    if inc is not None:
                        ins.then_inc(chans[inc][0], chans[inc][2])
                        rcount[inc] += chans[inc][2]
                    if KRACE:
                        if inc is None:
                            step = 16 if "DMA" in type(ins.ins).__name__ or "Dma" in type(ins.ins).__name__ else 1
                            ins.then_inc(selfsems[engname], step)
                            total[0] += step
                        else:
                            prev_chan[0] = inc
            return body
        with self.nc.Block() as block:
            block.tensor(mk("tensor"))
            block.vector(mk("vector"))
            block.scalar(mk("scalar"))
            block.gpsimd(mk("gpsimd"))
            block.sync(mk("sync"))


class Psum:
    def __init__(self, E, pt):
        self.E = E
        self.pt = pt
        self.state = [(None, 0)] * 8
        self.rr = 0
        self.reserved = set()

    def acquire(self, n=1):
        for _ in range(16):
            if self.rr + n > 8:
                self.rr = 0
            slots = list(range(self.rr, self.rr + n))
            if any(s in self.reserved for s in slots):
                self.rr = max(s for s in slots if s in self.reserved) + 1
                continue
            break
        else:
            raise RuntimeError("psum reserve deadlock")
        self.rr = (self.rr + n) % 8
        for s in slots:
            ch, tok = self.state[s]
            if ch is not None:
                self.E.wait("tensor", ch, tok)
        return slots

    def reserve(self, slots):
        self.reserved.update(slots)

    def unreserve(self, slots):
        self.reserved.difference_update(slots)

    def release(self, slots, ch, tok):
        for s in slots:
            self.state[s] = (ch, tok)


def bfv(ap):
    return ap.bitcast(BF16)


def build_nc():
    nc = bass.Bass("TRN2", target_bir_lowering=False, detect_race_conditions=bool(KRACE))
    stack = contextlib.ExitStack()

    def din(name, shape, dt=F32):
        return nc.dram_tensor(name, shape, dt, kind="ExternalInput").ap()

    def dint(name, shape, dt=F32):
        return nc.dram_tensor(name, shape, dt, kind="Internal").ap()

    x = din("x", [T, D])
    cw = din("cw", [128, 8, 4]); cb = din("cb", [128, 8])
    Afm = din("Afm", [128, 128]); Dfm = din("Dfm", [128, 8])
    bpwT = din("bpwT", [D, 16]); cpwT = din("cpwT", [D, 16])
    bcb2 = din("bcb2", [16, 2])
    gwT = din("gwT", [D, D]); owT = din("owT", [D, D])
    gb = din("gb", [128, 8]); ob = din("ob", [128, 8])
    lng = din("lng", [128, 8]); lnb = din("lnb", [128, 8])
    wr = din("wr", [D, 4])
    w1 = din("w1", [4, D, 4096])
    b1fm = din("b1fm", [128, 128])
    woB = din("woB", [8, 4096, 128])
    bofm = din("bofm", [128, 8])
    g2 = din("g2", [128, 8]); b2 = din("b2", [128, 8])
    wq = din("wq", [D, D]); wk = din("wk", [D, D]); wv = din("wv", [D, D])
    bqfm = din("bqfm", [128, 8]); bkfm = din("bkfm", [128, 8]); bvfm = din("bvfm", [128, 8])
    gamv = din("gamv", [128, 8]); betv = din("betv", [128, 8])
    g8T = din("g8T", [D, 8]); gb8 = din("gb8", [128, 8])
    W13 = din("W13", [2, 9, D, 2048])
    W2 = din("W2", [9, 2048, D])
    bvrep = din("bvrep", [128, 1024])
    identf = din("identf", [128, 128])
    trilT = din("trilT", [128, 128], BF16)
    ones1 = din("ones1", [1, 128])

    out = nc.dram_tensor("out", [T, D], F32, kind="ExternalOutput").ap()
    dbg = {}
    if DBG:
        for nm in ("d_a", "d_b", "d_c"):
            dbg[nm] = nc.dram_tensor(nm, [128, 16384], F32, kind="ExternalOutput").ap()
        dbg["d_r"] = nc.dram_tensor("d_r", [8, 2048], BF16, kind="ExternalOutput").ap()
        dbg["d_l"] = nc.dram_tensor("d_l", [128, 16384], BF16, kind="ExternalOutput").ap()
        dbg["d_g"] = nc.dram_tensor("d_g", [128, 4096], BF16, kind="ExternalOutput").ap()

    repscr = dint("repscr", [16, 512])
    maskscr = dint("maskscr", [4, 256], BF16)
    rowscr = dint("rowscr", [1, 2048])
    cfmscr = dint("cfmscr", [8, 2048], BF16)
    strmspill = dint("strmspill", [128, 16384])
    strm2d = dint("strm2d", [128, 16384])
    w1s = dint("w1s", [32, 128, 4096], BF16)
    wos = dint("wos", [8, 128, 4096], BF16)
    gws = dint("gws", [8, 128, 1024], BF16)
    ows = dint("ows", [8, 128, 1024], BF16)

    AR = 51712
    arena_t = stack.enter_context(nc.sbuf_tensor("arena", [128, AR], F32))
    consts = stack.enter_context(nc.sbuf_tensor("consts", [128, 832], F32))
    cbias = stack.enter_context(nc.sbuf_tensor("cbias", [128, 448], F32))
    pt = stack.enter_context(nc.psum_tensor("pt", [128, 4096], F32))

    E = Emitter(nc, stack)
    for ch in ("T", "V", "S", "G"):
        E.chan(ch, 1)
    for ch in ("DW", "DS", "DA"):
        E.chan(ch, 16)
    P = Psum(E, pt)

    def A(o, n):
        assert o + n <= AR, (o, n)
        return arena_t[:, o:o + n]

    c_identf = consts[:, 0:128]
    c_ones1 = consts[0:1, 128:256]
    c_onesb = bfv(consts[:, 256:257])[:, 0:1]
    c_tril = bfv(consts[:, 257:321])[:, 0:128]
    c_cw = consts[:, 322:354].rearrange("p (a b) -> p a b", a=8)
    c_cb = consts[:, 354:362]
    c_Aexp = consts[:, 362:490].rearrange("p (a b) -> p a b", a=8)
    c_Dfm = consts[:, 490:498]
    c_hst = consts[:, 498:626].rearrange("p (a b) -> p a b", a=8)
    c_onesf4 = consts[:, 626:630]
    c_ones64 = consts[64:65, 704:832]

    b_bpb = cbias[0:16, 0:1]; b_cpb = cbias[0:16, 1:2]
    b_gb = cbias[:, 2:10]; b_ob = cbias[:, 10:18]
    b_lng = cbias[:, 18:26]; b_lnb = cbias[:, 26:34]
    b_b1 = cbias[:, 34:162].rearrange("p (a b) -> p a b", a=4)
    b_bo = cbias[:, 162:170]
    b_g2 = cbias[:, 170:178]; b_b2 = cbias[:, 178:186]
    b_bq = cbias[:, 186:194]; b_bk = cbias[:, 194:202]; b_bv = cbias[:, 202:210]
    b_gam = cbias[:, 210:218]; b_bet = cbias[:, 218:226]
    b_gb8 = cbias[:, 226:234]

    bpw_t = stack.enter_context(nc.sbuf_tensor("bpw2", [128, 128], BF16))
    cpw_t = stack.enter_context(nc.sbuf_tensor("cpw2", [128, 128], BF16))
    wrf_t = stack.enter_context(nc.sbuf_tensor("wrf", [128, 32], F32))
    bpw = bpw_t[:, :].rearrange("p (a b) -> p a b", a=8)
    cpw = cpw_t[:, :].rearrange("p (a b) -> p a b", a=8)
    wrf = wrf_t[:, :].rearrange("p (a b) -> p a b", a=8)

    for dst, src in ((c_identf, identf), (c_ones1, ones1),
                     (c_cw, cw), (c_cb, cb), (c_Dfm, Dfm),
                     (cbias[0:16, 0:2], bcb2), (b_gb, gb), (b_ob, ob),
                     (b_lng, lng), (b_lnb, lnb), (b_b1, b1fm.rearrange("p (a b) -> p a b", a=4)),
                     (b_bo, bofm), (b_g2, g2), (b_b2, b2), (b_bq, bqfm), (b_bk, bkfm),
                     (b_bv, bvfm), (b_gam, gamv), (b_bet, betv), (b_gb8, gb8),
                     (consts[:, 362:490], Afm)):
        E.emit("sync", lambda e, dst=dst, src=src: e.dma_start(dst, src), inc="DS")
    E.emit("sync", lambda e: e.dma_start(c_tril, trilT), inc="DS")
    E.emit("sync", lambda e: e.dma_start(c_ones64, ones1), inc="DS")
    tDS0 = E.tok("DS")
    E.emit("gpsimd", lambda e: e.dma_start(bpw_t[:, :].rearrange("p (a b) -> p a b", a=8), bpwT.rearrange("(a p) b -> p a b", p=128)), inc="DW")
    E.emit("gpsimd", lambda e: e.dma_start(cpw_t[:, :].rearrange("p (a b) -> p a b", a=8), cpwT.rearrange("(a p) b -> p a b", p=128)), inc="DW")
    E.emit("sync", lambda e: e.dma_start(wrf_t[:, :].rearrange("p (a b) -> p a b", a=8), wr.rearrange("(a p) b -> p a b", p=128)), inc="DS")

    E.emit("vector", lambda e: e.memset(c_onesb, 1.0), inc="V")
    E.emit("vector", lambda e: e.memset(consts[:, 498:626], 0.0), inc="V")
    tok_o4 = E.emit("vector", lambda e: e.memset(c_onesf4, 1.0), inc="V")
    E.wait("scalar", "DS", tDS0)
    E.emit("scalar", lambda e: e.activation(consts[:, 362:490], consts[:, 362:490], AF.Exp), inc="S")
    tok_aexp = E.emit("scalar", lambda e: e.activation(consts[:, 362:490], consts[:, 362:490], AF.Exp, scale=-1.0), inc="S")

    o = 16384
    def alloc(n):
        nonlocal o
        r = o
        o += n
        assert o <= AR, o
        return r
    o_lnx1 = alloc(2072)
    o_lnxbf = alloc(1024)
    o_u = alloc(2048)
    o_ubf = alloc(1024)
    o_y = alloc(2048)
    o_Brep = alloc(2048)
    o_Crep = alloc(2048)
    o_scr = alloc(2048)
    o_tail = alloc(24)
    o_srow = alloc(1024)
    o_lnx2 = alloc(2048)
    o_lnx2bf = alloc(1024)
    o_inter = alloc(4096)
    o_gscr = alloc(256)
    o_zB = alloc(2048)
    o_zBbf = alloc(1024)
    o_w1r = alloc(4096)
    o_wor = alloc(4096)
    o_mrep = alloc(512)
    o_Bsc = alloc(64)

    strm = A(0, 16384).rearrange("p (a b) -> p a b", a=8)
    xst = A(o_lnx1, 1024)
    E.wait("sync", "DS", tDS0)
    for tt_ in range(16):
        tokd = E.emit("sync", lambda e, tt_=tt_: e.dma_start(xst, x[tt_ * 128:(tt_ + 1) * 128, :]), inc="DS")
        E.wait("tensor", "DS", tokd)
        for dd in range(8):
            slots = P.acquire(1)
            apo = pt[:, slots[0] * 512: slots[0] * 512 + 128]
            tokt = E.emit("tensor", lambda e, apo=apo, dd=dd: e.transpose(
                apo, xst[:, dd * 128:(dd + 1) * 128], c_identf), inc="T")
            E.wait("vector", "T", tokt)
            tokv = E.emit("vector", lambda e, apo=apo, dd=dd, tt_=tt_: e.tensor_copy(
                strm[:, dd, tt_ * 128:(tt_ + 1) * 128], apo), inc="V")
            P.release(slots, "V", tokv)
        E.wait("sync", "T", E.tok("T"))
    E.wait("vector", "T", E.tok("T"))
    E.wait("scalar", "T", E.tok("T"))


    def fm3(ap2d):
        return ap2d.rearrange("(a p) b -> p a b", p=128)

    def stage1(src, dst):
        na = src.shape[1]
        return E.emit("gpsimd", lambda e, src=src, dst=dst: e.dma_start(
            dst.rearrange("p (a b) -> p a b", a=na), src), inc="DW")

    for oo in range(8):
        stage1(fm3(gwT[:, oo * 128:(oo + 1) * 128]), gws[oo, :, :])
    for oo in range(8):
        tok_gwos = stage1(fm3(owT[:, oo * 128:(oo + 1) * 128]), ows[oo, :, :])
    w1_tok = {}
    for ee in range(4):
        for grp in range(8):
            w1_tok[ee * 8 + grp] = stage1(
                fm3(w1[ee, :, grp * 512:(grp + 1) * 512]), w1s[ee * 8 + grp, :, :])
    wo_tok = {}
    for oo in range(8):
        wo_tok[oo] = stage1(fm3(woB[oo, :, :]), wos[oo, :, :])

    lnx1 = A(o_lnx1, 2072).rearrange("p (a b) -> p a b", a=8)
    lnxbf = bfv(A(o_lnxbf, 1024)).rearrange("p (a b) -> p a b", a=8)
    ubuf = A(o_u, 2048).rearrange("p (a b) -> p a b", a=8)
    ubf = bfv(A(o_ubf, 1024)).rearrange("p (a b) -> p a b", a=8)
    ybuf = A(o_y, 2048).rearrange("p (a b) -> p a b", a=8)
    Brep = A(o_Brep, 2048).rearrange("p (a b) -> p a b", a=16)
    Crep = A(o_Crep, 2048).rearrange("p (a b) -> p a b", a=16)
    scr_uB = A(o_scr, 2048).rearrange("p (a b) -> p a b", a=16)
    scr_h = scr_uB
    scr_z = A(o_scr, 2048).rearrange("p (a b) -> p a b", a=8)
    tailb = A(o_tail, 24).rearrange("p (a b) -> p a b", a=8)
    srow = A(o_srow, 1024)
    bcs = arena_t[32:48, o_srow:o_srow + 512]
    mfm_st = bfv(arena_t[64:68, o_srow:o_srow + 128])
    lnx2 = A(o_lnx2, 2048).rearrange("p (a b) -> p a b", a=8)
    lnx2bf = bfv(A(o_lnx2bf, 1024)).rearrange("p (a b) -> p a b", a=8)
    interb = bfv(A(o_inter, 4096)).rearrange("p (a b) -> p a b", a=32)
    gscr = [bfv(A(o_gscr, 128)), bfv(A(o_gscr + 128, 128))]
    zB = A(o_zB, 2048).rearrange("p (a b) -> p a b", a=8)
    zBbf = bfv(A(o_zBbf, 1024)).rearrange("p (a b) -> p a b", a=8)
    mrep = bfv(A(o_mrep, 512)).rearrange("p (a b) -> p a b", a=4)
    lgsc = A(o_Bsc, 64)

    class Ring:
        def __init__(self, base, nslots, slotcols, eng="scalar", ch="DA"):
            self.off = [base + i * slotcols for i in range(nslots)]
            self.free = [(None, 0)] * nslots
            self.i = 0
            self.n = nslots
            self.eng = [eng] * nslots if isinstance(eng, str) else eng
            self.ch = [ch] * nslots if isinstance(ch, str) else ch

        def load(self, dma_fn):
            j = self.i
            self.i = (self.i + 1) % self.n
            ch, tok = self.free[j]
            if ch is not None:
                E.wait(self.eng[j], ch, tok)
            tokd = E.emit(self.eng[j], lambda e, off=self.off[j]: dma_fn(e, off), inc=self.ch[j])
            return j, self.off[j], tokd

        def consumed(self, j, ch, tok):
            self.free[j] = (ch, tok)

    w1ring = Ring(o_w1r, 2, 2048)
    woring = Ring(o_wor, 2, 2048)
    garing = Ring(o_Brep, 2, 512, eng="sync", ch="DS")

    def layer_norm(x_f32, ncols, eps, xbf_scr, sq_scr, out_f32=None, out_bf=None,
                   gcol=None, bcol=None, add_to=None, war_tok=None, srow_t=None):
        if srow_t is None:
            srow_t = srow
        if war_tok is not None:
            E.wait("vector", "T", war_tok)
            E.wait("scalar", "T", war_tok)
        nb = (2 * ncols + 511) // 512
        tokc = E.emit("vector", lambda e: e.tensor_copy(xbf_scr, x_f32), inc="V")
        E.wait("scalar", "V", tokc)
        toksq = E.emit("scalar", lambda e: e.activation(sq_scr, xbf_scr, AF.Square), inc="S")
        sl = P.acquire(nb)
        ps = pt[0:1, sl[0] * 512: sl[0] * 512 + 2 * ncols]
        E.wait("tensor", "V", tokc)
        E.wait("tensor", "S", toksq)
        tokt = None
        for dd in range(8):
            E.emit("tensor", lambda e, dd=dd: e.matmul(
                ps[:, 0:ncols], c_onesb, xbf_scr[:, dd, :], start=(dd == 0), stop=(dd == 7)))
        for dd in range(8):
            tokt = E.emit("tensor", lambda e, dd=dd: e.matmul(
                ps[:, ncols:2 * ncols], c_onesb, sq_scr[:, dd, :], start=(dd == 0), stop=(dd == 7)), inc="T")
        E.wait("vector", "T", tokt)
        mu = srow_t[0:1, 0:ncols]
        m2 = srow_t[0:1, ncols:2 * ncols]
        vpe = srow_t[0:1, 2 * ncols:3 * ncols]
        rstd = srow_t[0:1, 3 * ncols:4 * ncols]
        E.emit("vector", lambda e: e.tensor_scalar_mul(mu, ps[:, 0:ncols], 1.0 / D))
        E.emit("vector", lambda e: e.tensor_scalar_mul(m2, ps[:, ncols:2 * ncols], 1.0 / D))
        E.emit("vector", lambda e: e.tensor_mul(vpe, mu, mu))
        E.emit("vector", lambda e: e.scalar_tensor_tensor(vpe, vpe, -1.0, m2, ALU.mult, ALU.add))
        E.emit("vector", lambda e: e.tensor_scalar_add(vpe, vpe, float(eps)))
        tokv = E.emit("vector", lambda e: e.reciprocal(vpe, vpe), inc="V")
        P.release(sl, "V", tokv)
        E.wait("scalar", "V", tokv)
        toks = E.emit("scalar", lambda e: e.activation(rstd, vpe, AF.Sqrt), inc="S")
        E.wait("vector", "S", toks)
        tokn = E.emit("vector", lambda e: e.scalar_tensor_tensor(
            mu, mu, -1.0, rstd, ALU.mult, ALU.mult), inc="V")
        sl2 = P.acquire(nb)
        rp = pt[:, sl2[0] * 512: sl2[0] * 512 + 2 * ncols]
        E.wait("tensor", "V", tokn)
        onesrow = c_ones1 if srow_t.base_partition() == 0 else c_ones64
        E.emit("tensor", lambda e, onesrow=onesrow: e.matmul(
            rp[:, 0:ncols], onesrow, rstd, start=True, stop=True))
        tokt2 = E.emit("tensor", lambda e, onesrow=onesrow: e.matmul(
            rp[:, ncols:2 * ncols], onesrow, mu, start=True, stop=True), inc="T")
        E.wait("vector", "T", tokt2)
        rsr = rp[:, 0:ncols].unsqueeze(1).broadcast_to([128, 8, ncols])
        nmr = rp[:, ncols:2 * ncols].unsqueeze(1).broadcast_to([128, 8, ncols])
        if out_f32 is None and gcol is None and add_to is None:
            E.emit("vector", lambda e: e.tensor_mul(out_bf, x_f32, rsr))
            last = E.emit("vector", lambda e: e.tensor_add(out_bf, out_bf, nmr), inc="V")
            P.release(sl2, "V", last)
            return last
        tgt = out_f32 if out_f32 is not None else x_f32
        E.emit("vector", lambda e: e.tensor_mul(tgt, x_f32, rsr))
        last = E.emit("vector", lambda e: e.tensor_add(tgt, tgt, nmr), inc="V")
        if out_bf is not None:
            last = E.emit("vector", lambda e: e.tensor_copy(out_bf, tgt), inc="V")
        if gcol is not None:
            for dd in range(8):
                E.emit("vector", lambda e, dd=dd: e.tensor_scalar(
                    tgt[:, dd, :], tgt[:, dd, :], gcol[:, dd:dd + 1], bcol[:, dd:dd + 1],
                    op0=ALU.mult, op1=ALU.add))
            last = E.emit("vector", lambda e: e.tensor_add(add_to, add_to, tgt), inc="V")
        P.release(sl2, "V", last)
        return last

    snapA = [None]

    stA = {}

    def phaseA_head(c):
        if snapA[0] is None:
            pass
        else:
            for eng in ("vector", "scalar", "tensor", "sync"):
                E.wait_snap(eng, snapA[0])
        t0 = c * CH
        sc_strm = strm[:, :, t0:t0 + CH]
        stA["sc_strm"] = sc_strm
        lnx_body = lnx1[:, :, 3:3 + CH]
        layer_norm(sc_strm, CH, 1e-5, ubf, lnxbf, out_f32=lnx_body)
        if c == 0:
            E.emit("vector", lambda e: e.memset(lnx1[:, :, 0:3], 0.0))
        else:
            E.emit("vector", lambda e: e.tensor_copy(lnx1[:, :, 0:3], tailb))
        for dd in range(8):
            E.emit("vector", lambda e, dd=dd: e.tensor_scalar_mul(
                ubuf[:, dd, :], lnx1[:, dd, 0:CH], c_cw[:, dd, 0:1]))
            for j in (1, 2, 3):
                E.emit("vector", lambda e, dd=dd, j=j: e.scalar_tensor_tensor(
                    ubuf[:, dd, :], lnx1[:, dd, j:j + CH], c_cw[:, dd, j:j + 1],
                    ubuf[:, dd, :], ALU.mult, ALU.add))
        tokv = E.emit("vector", lambda e: e.tensor_copy(tailb, lnx1[:, :, CH:CH + 3]), inc="V")
        E.wait("scalar", "V", tokv)
        toks = None
        for dd in range(8):
            toks = E.emit("scalar", lambda e, dd=dd: e.activation(
                ubuf[:, dd, :], ubuf[:, dd, :], AF.Silu, bias=c_cb[:, dd:dd + 1]), inc="S")
        E.wait("vector", "S", toks)
        tokubf = E.emit("vector", lambda e: e.tensor_copy(ubf, ubuf), inc="V")
        toklbf = E.emit("vector", lambda e: e.tensor_copy(lnxbf, lnx_body), inc="V")
        sl = P.acquire(1)
        ps = pt[0:16, sl[0] * 512: sl[0] * 512 + 512]
        E.wait("tensor", "V", tokubf)
        tokt = None
        for dd in range(8):
            E.emit("tensor", lambda e, dd=dd, ps=ps: e.matmul(
                ps[:, 0:CH], bpw[:, dd, :], ubf[:, dd, :], start=(dd == 0), stop=(dd == 7)))
        for dd in range(8):
            tokt = E.emit("tensor", lambda e, dd=dd, ps=ps: e.matmul(
                ps[:, 256:256 + CH], cpw[:, dd, :], ubf[:, dd, :], start=(dd == 0), stop=(dd == 7)), inc="T")
        E.wait("vector", "T", tokt)
        E.emit("vector", lambda e, ps=ps: e.tensor_scalar_add(bcs[:, 0:CH], ps[:, 0:CH], b_bpb))
        toks2 = E.emit("vector", lambda e, ps=ps: e.tensor_scalar_add(
            bcs[:, 256:256 + CH], ps[:, 256:256 + CH], b_cpb), inc="V")
        P.release(sl, "V", toks2)
        E.wait("sync", "V", toks2)
        tokd = E.emit("sync", lambda e: e.dma_start(repscr[:, :], bcs), inc="DS")
        stA["tokd"] = tokd
        stA["toklbf"] = toklbf

    def phaseA_scan(c):
        tokd = stA["tokd"]
        lastred = None
        for half in (0, 1):
            hb = half * 128
            E.wait("sync", "DS", tokd)
            if lastred is not None:
                E.wait("sync", "V", lastred)
            E.emit("sync", lambda e, hb=hb: e.dma_start(
                Brep, bass.AP(repscr.tensor, hb, [[0, 128], [512, 16], [1, 128]])), inc="DS")
            tokC = E.emit("sync", lambda e, hb=hb: e.dma_start(
                Crep, bass.AP(repscr.tensor, 256 + hb, [[0, 128], [512, 16], [1, 128]])), inc="DS")
            E.wait("vector", "DS", tokC)
            for dd in range(8):
                ub_b = ubuf[:, dd, hb:hb + 128].unsqueeze(1).broadcast_to([128, 16, 128])
                E.emit("vector", lambda e, ub_b=ub_b: e.tensor_mul(scr_uB, ub_b, Brep))
                for n in range(16):
                    a0 = c_Aexp[:, dd, n:n + 1].broadcast_to([128, 128])
                    E.emit("vector", lambda e, a0=a0, n=n, dd=dd: e.tensor_tensor_scan(
                        scr_h[:, n, :], a0, scr_uB[:, n, :], c_hst[:, dd, n:n + 1],
                        ALU.mult, ALU.add))
                E.emit("vector", lambda e, dd=dd: e.tensor_copy(
                    c_hst[:, dd, :].unsqueeze(2), scr_h[:, :, 127:128]))
                E.emit("vector", lambda e: e.tensor_mul(scr_h, scr_h, Crep))
                lastred = E.emit("vector", lambda e, dd=dd, hb=hb: e.tensor_reduce(
                    ybuf[:, dd, hb:hb + 128].unsqueeze(2),
                    scr_h.rearrange("p a b -> p b a"), AX.X, ALU.add), inc="V")
        tok_ydu = None
        for dd in range(8):
            tok_ydu = E.emit("vector", lambda e, dd=dd: e.scalar_tensor_tensor(
                ybuf[:, dd, :], ubuf[:, dd, :], c_Dfm[:, dd:dd + 1], ybuf[:, dd, :],
                ALU.mult, ALU.add), inc="V")
        stA["tok_ydu"] = tok_ydu

    def phaseA_tail(c):
        sc_strm = stA["sc_strm"]
        tok_ydu = stA["tok_ydu"]
        toklbf = stA["toklbf"]
        if c == 0:
            E.wait("sync", "DW", tok_gwos)
        garing.i = 0
        garing.free = [("V", tok_ydu), ("V", tok_ydu)]
        E.wait("tensor", "V", toklbf)
        E.wait("scalar", "V", tok_ydu)
        toks = None
        for oo in range(8):
            j, offw, tokw = garing.load(lambda e, off, oo=oo: e.dma_start(
                bfv(A(off, 512)), gws[oo, :, :]))
            wap = bfv(A(offw, 512)).rearrange("p (a b) -> p a b", a=8)
            sl = P.acquire(1)
            psg = pt[:, sl[0] * 512: sl[0] * 512 + CH]
            E.wait("tensor", "DS", tokw)
            tokt = None
            for dd in range(8):
                tokt = E.emit("tensor", lambda e, dd=dd, wap=wap, psg=psg: e.matmul(
                    psg, wap[:, dd, :], lnxbf[:, dd, :], start=(dd == 0), stop=(dd == 7)), inc="T")
            E.wait("scalar", "T", tokt)
            toks = E.emit("scalar", lambda e, psg=psg, oo=oo: e.activation(
                scr_z[:, oo, :], psg, AF.Silu, bias=b_gb[:, oo:oo + 1]), inc="S")
            P.release(sl, "S", toks)
            garing.consumed(j, "T", tokt)
        E.wait("vector", "S", toks)
        E.emit("vector", lambda e: e.tensor_mul(ybuf, ybuf, scr_z))
        tokyb = E.emit("vector", lambda e: e.tensor_copy(ubf, ybuf), inc="V")
        E.wait("tensor", "V", tokyb)
        tokvz = None
        for oo in range(8):
            j, offw, tokw = garing.load(lambda e, off, oo=oo: e.dma_start(
                bfv(A(off, 512)), ows[oo, :, :]))
            wap = bfv(A(offw, 512)).rearrange("p (a b) -> p a b", a=8)
            sl = P.acquire(1)
            psg = pt[:, sl[0] * 512: sl[0] * 512 + CH]
            E.wait("tensor", "DS", tokw)
            tokt = None
            for dd in range(8):
                tokt = E.emit("tensor", lambda e, dd=dd, wap=wap, psg=psg: e.matmul(
                    psg, wap[:, dd, :], ubf[:, dd, :], start=(dd == 0), stop=(dd == 7)), inc="T")
            E.wait("vector", "T", tokt)
            tokvz = E.emit("vector", lambda e, psg=psg, oo=oo: e.scalar_tensor_tensor(
                lnx1[:, oo, 3:3 + CH], psg, b_ob[:, oo:oo + 1], lnx1[:, oo, 3:3 + CH],
                ALU.add, ALU.add), inc="V")
            P.release(sl, "V", tokvz)
            garing.consumed(j, "T", tokt)
        layer_norm(lnx1[:, :, 3:3 + CH], CH, 1e-5, lnxbf, ubf,
                   gcol=b_lng, bcol=b_lnb, add_to=sc_strm, war_tok=tokt)
        snapA[0] = E.snap()

    snapB = [None]
    snapBh = [None]

    def phaseB_head(c):
        if snapB[0] is not None:
            for eng in ("vector", "scalar", "tensor", "sync"):
                E.wait_snap(eng, snapB[0])
        t0 = c * CH
        sc_strm = strm[:, :, t0:t0 + CH]
        lnret = layer_norm(sc_strm, CH, 1e-5, lnx2bf, zBbf, out_f32=lnx2, out_bf=lnx2bf)
        E.wait("tensor", "V", lnret)
        tokm = None
        for ts_ in range(2):
            sl = P.acquire(1)
            psl = pt[:, sl[0] * 512: sl[0] * 512 + 8]
            tokt = None
            for dd in range(8):
                tokt = E.emit("tensor", lambda e, dd=dd, psl=psl, ts_=ts_: e.matmul(
                    psl[:, 0:4], lnx2[:, dd, ts_ * 128:(ts_ + 1) * 128], wrf[:, dd, :],
                    start=(dd == 0), stop=(dd == 7)), inc="T")
            E.wait("vector", "T", tokt)
            E.emit_s("vector", lambda e: e.memset(lgsc[:, 0:8], -1e30))
            E.emit_s("vector", lambda e, psl=psl: e.tensor_copy(lgsc[:, 0:4], psl[:, 0:4]))
            E.emit_s("vector", lambda e: e.reduce_max(lgsc[:, 8:9], lgsc[:, 0:8], AX.X))
            E.emit_s("vector", lambda e: e.tensor_tensor(
                lgsc[:, 16:24], lgsc[:, 0:8], lgsc[:, 8:9].broadcast_to([128, 8]), ALU.is_ge))
            E.emit_s("vector", lambda e: e.tensor_tensor_scan(
                lgsc[:, 24:32], c_onesf4[:, 0:1].broadcast_to([128, 8]), lgsc[:, 16:24],
                0.0, ALU.mult, ALU.add))
            E.emit_s("vector", lambda e: e.tensor_scalar(
                lgsc[:, 24:32], lgsc[:, 24:32], 1.0, None, op0=ALU.is_le))
            tokv = E.emit_s("vector", lambda e: e.tensor_mul(
                lgsc[:, 24:32], lgsc[:, 24:32], lgsc[:, 16:24]))
            P.release(sl, "V", tokv)
            sl2 = P.acquire(1)
            pst = pt[0:4, sl2[0] * 512: sl2[0] * 512 + 128]
            E.wait("tensor", "V", tokv)
            tokt2 = E.emit("tensor", lambda e, pst=pst: e.transpose(
                pst, lgsc[:, 24:28], c_identf), inc="T")
            E.wait("vector", "T", tokt2)
            tokm = E.emit("vector", lambda e, pst=pst, ts_=ts_: e.tensor_copy(
                mfm_st[0:4, ts_ * 128:(ts_ + 1) * 128], pst), inc="V")
            P.release(sl2, "V", tokm)
        E.wait("sync", "V", tokm)
        tokd = E.emit("sync", lambda e: e.dma_start(maskscr[:, :], mfm_st[0:4, 0:256]), inc="DS")
        E.wait("sync", "DS", tokd)
        tokd2 = E.emit("sync", lambda e: e.dma_start(
            mrep, bass.AP(maskscr.tensor, 0, [[0, 128], [CH, 4], [1, CH]])), inc="DS")
        snapBh[0] = E.snap()
        snapBh[0]["_tokd2"] = tokd2

    def phaseB_body(c):
        tokd2 = snapBh[0]["_tokd2"]
        first = snapB[0] is None
        if not first:
            for eng in ("vector", "scalar", "tensor", "sync", "gpsimd"):
                E.wait_snap(eng, snapB[0])
        for eng in ("vector", "scalar", "tensor", "sync"):
            E.wait_snap(eng, snapBh[0])
        t0 = c * CH
        sc_strm = strm[:, :, t0:t0 + CH]
        E.wait("vector", "DS", tokd2)
        E.wait("gpsimd", "DS", tokd2)
        tokcmb = None
        gtok = [None, None]
        for ee in range(4):
            for grp in range(8):
                if first:
                    E.wait("scalar", "DW", w1_tok[ee * 8 + grp])
                    E.wait("gpsimd", "DW", w1_tok[ee * 8 + grp])
                j, offw, tokw = w1ring.load(lambda e, off, ee=ee, grp=grp: e.dma_start(
                    bfv(A(off, 2048)), w1s[ee * 8 + grp, :, :]))
                wap = bfv(A(offw, 2048)).rearrange("p (a b) -> p a b", a=8)
                slots = P.acquire(4)
                E.wait("tensor", w1ring.ch[j], tokw)
                tokt = None
                for ii in range(4):
                    psn = pt[:, slots[ii] * 512: slots[ii] * 512 + CH]
                    for dd in range(8):
                        tokt = E.emit("tensor", lambda e, dd=dd, ii=ii, psn=psn, wap=wap: e.matmul(
                            psn, wap[:, dd, ii * 128:(ii + 1) * 128], lnx2bf[:, dd, :],
                            start=(dd == 0), stop=(dd == 7)), inc="T")
                w1ring.consumed(j, "T", tokt)
                E.wait("scalar", "T", tokt)
                for ii in range(4):
                    ic = grp * 4 + ii
                    psn = pt[:, slots[ii] * 512: slots[ii] * 512 + CH]
                    gb_ = gscr[ii % 2]
                    if gtok[ii % 2] is not None:
                        E.wait("scalar", "G", gtok[ii % 2])
                    toks = E.emit("scalar", lambda e, psn=psn, gb_=gb_, ee=ee, ic=ic: e.activation(
                        gb_[:, 0:CH], psn, AF.Gelu, bias=b_b1[:, ee, ic:ic + 1]), inc="S")
                    P.release([slots[ii]], "S", toks)
                    E.wait("gpsimd", "S", toks)
                    if ee == 0:
                        tokcmb = E.emit("gpsimd", lambda e, gb_=gb_, ic=ic: e.tensor_mul(
                            interb[:, ic, :], gb_[:, 0:CH], mrep[:, 0, :]), inc="G")
                    else:
                        E.emit("gpsimd", lambda e, gb_=gb_, ee=ee: e.tensor_mul(
                            gb_[:, 0:CH], gb_[:, 0:CH], mrep[:, ee, :]))
                        tokcmb = E.emit("gpsimd", lambda e, gb_=gb_, ic=ic: e.tensor_add(
                            interb[:, ic, :], interb[:, ic, :], gb_[:, 0:CH]), inc="G")
                    gtok[ii % 2] = tokcmb
        E.wait("tensor", "G", tokcmb)
        for oo in range(8):
            if first:
                E.wait("scalar", "DW", wo_tok[oo])
                E.wait("gpsimd", "DW", wo_tok[oo])
            j, offw, tokw = woring.load(lambda e, off, oo=oo: e.dma_start(
                bfv(A(off, 2048)), wos[oo, :, :]))
            wap = bfv(A(offw, 2048)).rearrange("p (a b) -> p a b", a=32)
            sl = P.acquire(1)
            psn = pt[:, sl[0] * 512: sl[0] * 512 + CH]
            E.wait("tensor", woring.ch[j], tokw)
            tokt = None
            for ii in range(32):
                tokt = E.emit("tensor", lambda e, ii=ii, psn=psn, wap=wap: e.matmul(
                    psn, wap[:, ii, :], interb[:, ii, :], start=(ii == 0), stop=(ii == 31)), inc="T")
            woring.consumed(j, "T", tokt)
            E.wait("vector", "T", tokt)
            tokvz = E.emit("vector", lambda e, psn=psn, oo=oo: e.scalar_tensor_tensor(
                zB[:, oo, :], psn, b_bo[:, oo:oo + 1], lnx2[:, oo, :],
                ALU.add, ALU.add), inc="V")
            P.release(sl, "V", tokvz)
        layer_norm(zB, CH, 1e-12, zBbf, lnx2bf, gcol=b_g2, bcol=b_b2,
                   add_to=sc_strm, war_tok=tokt)
        snapB[0] = E.snap()

    if KPH >= 2:
        phaseA_head(0)
        phaseA_scan(0)
        phaseA_tail(0)
        phaseB_head(0)
        for k in range(1, NCH):
            phaseA_head(k)
            phaseA_scan(k)
            phaseB_body(k - 1)
            phaseA_tail(k)
            phaseB_head(k)
        phaseB_body(NCH - 1)
    else:
        for k in range(NCH):
            phaseA_head(k)
            phaseA_scan(k)
            phaseA_tail(k)
    E.barrier()
    if DBG:
        E.emit("sync", lambda e: e.dma_start(dbg["d_a"], A(0, 16384)), inc="DS")
        E.barrier(("sync",))

    def phaseC():
        oc_K = 0
        oc_V = 8192
        oc_lnxout = 16512
        oc_lnxbf = 32896
        oc_qp = 41088
        oc_PT = 43136
        oc_qrep = 45184
        oc_rows = 47232
        oc_small = 49280
        oc_ring = 49536
        oc_srow = 50560
        Kb = bfv(A(oc_K, 8192)).rearrange("p (a b) -> p a b", a=8)
        Vb = bfv(A(oc_V, 8320)).rearrange("p (a b) -> p a b", a=16)
        lnxo = A(oc_lnxout, 16384).rearrange("p (a b) -> p a b", a=8)
        lnxb = bfv(A(oc_lnxbf, 8192)).rearrange("p (a b) -> p a b", a=8)
        qrep = A(oc_qrep, 2048)
        negrow = arena_t[64:65, oc_rows:oc_rows + 2048]
        negfm = A(oc_small, 16)
        biasfm = A(oc_small + 16, 16)
        mxcols = A(oc_small + 32, 48)
        ccol = A(oc_small + 80, 4)
        srowC = arena_t[64:65, oc_qrep:oc_qrep + 2048]
        wring = Ring(oc_ring, 2, 512, eng="gpsimd", ch="DW")

        for c4 in range(4):
            t0 = c4 * 512
            sc = strm[:, :, t0:t0 + 512]
            xbf_s = bfv(A(oc_qp, 2048)).rearrange("p (a b) -> p a b", a=8)
            sq_s = bfv(A(oc_PT, 2048)).rearrange("p (a b) -> p a b", a=8)
            layer_norm(sc, 512, 1e-5, xbf_s, sq_s,
                       out_f32=lnxo[:, :, t0:t0 + 512], out_bf=lnxb[:, :, t0:t0 + 512],
                       srow_t=srowC)
            sl = P.acquire(1)
            ps = pt[0:1, sl[0] * 512: sl[0] * 512 + 512]
            E.wait("tensor", "V", E.tok("V"))
            tokt = None
            for dd in range(8):
                tokt = E.emit("tensor", lambda e, dd=dd, ps=ps, t0=t0: e.matmul(
                    ps[:, 0:512], c_onesf4[:, 0:1], lnxo[:, dd, t0:t0 + 512],
                    start=(dd == 0), stop=(dd == 7)), inc="T")
            E.wait("vector", "T", tokt)
            E.emit_s("vector", lambda e, ps=ps, t0=t0: e.tensor_scalar(
                negrow[:, t0:t0 + 512], ps[:, 0:512], 0.0, None, op0=ALU.is_equal))
            tokv = E.emit_s("vector", lambda e, t0=t0: e.tensor_scalar(
                negrow[:, t0:t0 + 512], negrow[:, t0:t0 + 512], float(NEG), None,
                op0=ALU.mult))
            P.release(sl, "V", tokv)
        if DBG:
            E.barrier(("sync",))
            E.emit("sync", lambda e: e.dma_start(dbg["d_b"], A(16512, 16384)), inc="DS")
            E.barrier(("sync",))
        E.wait("sync", "V", E.tok("V"))
        tok_sp = None
        for c4 in range(4):
            tok_sp = E.emit("sync", lambda e, c4=c4: e.dma_start(
                strmspill[:, c4 * 4096:(c4 + 1) * 4096], A(c4 * 4096, 4096)), inc="DS")
        tokn = E.emit("sync", lambda e: e.dma_start(rowscr[0, :], negrow[0:1, :]), inc="DS")
        E.wait("sync", "DS", tokn)
        tokn2 = E.emit("sync", lambda e: e.dma_start(
            negfm, bass.AP(rowscr.tensor, 0, [[1, 128], [128, 16]]),
            allow_slow_non_contiguous=True), inc="DS")

        E.wait("scalar", "DS", tok_sp)
        toks = None
        for oo in range(8):
            j, offw, tokw = wring.load(lambda e, off, oo=oo: e.dma_start(
                bfv(A(off, 512)).rearrange("p (a b) -> p a b", a=8),
                wk[:, oo * 128:(oo + 1) * 128].rearrange("(a p) b -> p a b", p=128)))
            wap = bfv(A(offw, 512)).rearrange("p (a b) -> p a b", a=8)
            E.wait("tensor", "DW", tokw)
            tokt = None
            for pc in range(4):
                sl = P.acquire(1)
                psn = pt[:, sl[0] * 512: sl[0] * 512 + 512]
                for dd in range(8):
                    tokt = E.emit("tensor", lambda e, dd=dd, psn=psn, wap=wap, pc=pc: e.matmul(
                        psn, wap[:, dd, :], lnxb[:, dd, pc * 512:(pc + 1) * 512],
                        start=(dd == 0), stop=(dd == 7)), inc="T")
                E.wait("scalar", "T", tokt)
                toks = E.emit("scalar", lambda e, psn=psn, oo=oo, pc=pc: e.activation(
                    Kb[:, oo, pc * 512:(pc + 1) * 512], psn, AF.Relu,
                    bias=b_bk[:, oo:oo + 1]), inc="S")
                P.release(sl, "S", toks)
            wring.consumed(j, "T", tokt)

        E.wait("vector", "DS", tok_sp)
        tokm = None
        for tt_ in range(16):
            tokm = E.emit("vector", lambda e, tt_=tt_: e.memset(
                Vb[:, tt_, :].rearrange("p (a b) -> p a b", a=16)[:, :, 64:65], 1.0), inc="V")
        wvb = bfv(A(oc_qp, 4096)).rearrange("p (a b) -> p a b", a=8)
        tokbv = E.emit("sync", lambda e: e.dma_start(bvrep_b, bvrep), inc="DS")
        E.wait("vector", "DS", tokbv)
        tokwv = E.emit("gpsimd", lambda e: e.dma_start(
            wvb, wv.rearrange("(a p) b -> p a b", p=128)), inc="DW")
        E.wait("tensor", "DW", tokwv)
        for tt_ in range(16):
            sl = P.acquire(2)
            psn = pt[:, sl[0] * 512: sl[0] * 512 + 1024]
            tokt = None
            for kk in range(8):
                for hhalf in range(2):
                    tokt = E.emit("tensor", lambda e, kk=kk, tt_=tt_, psn=psn, hhalf=hhalf: e.matmul(
                        psn[:, hhalf * 512:(hhalf + 1) * 512],
                        lnxb[:, kk, tt_ * 128:(tt_ + 1) * 128],
                        wvb[:, kk, hhalf * 512:(hhalf + 1) * 512],
                        start=(kk == 0), stop=(kk == 7)), inc="T")
            E.wait("vector", "T", tokt)
            tokvv = E.emit("vector", lambda e, psn=psn: e.tensor_add(
                psn, psn, bvrep_b), inc="V")
            E.wait("scalar", "V", tokvv)
            if tokm is not None:
                E.wait("scalar", "V", tokm)
            dst = Vb[:, tt_, :].rearrange("p (a b) -> p a b", a=16)[:, :, 0:64]
            toksv = E.emit("scalar", lambda e, psn=psn, dst=dst: e.activation(
                dst, psn.rearrange("p (a b) -> p a b", a=16), AF.Relu), inc="S")
            P.release(sl, "S", toksv)
        E.wait("vector", "DS", tokn2)

        qread = [None, None]
        qlast = [None, None]
        mxc = [mxcols[:, 0:48], A(oc_small + 128, 48)]
        ccol2 = [ccol, A(oc_small + 176, 4)]
        biasfm2 = [biasfm, A(oc_small + 180, 16)]

        def emit_qpair(hp):
            par = hp % 2
            qbuf = bfv(A(oc_qp + par * 1024, 1024))
            if qread[par] is not None:
                E.wait("scalar", "T", qread[par])
            j, offw, tokw = wring.load(lambda e, off, hp=hp: e.dma_start(
                bfv(A(off, 512)).rearrange("p (a b) -> p a b", a=8),
                wq[:, hp * 128:(hp + 1) * 128].rearrange("(a p) b -> p a b", p=128)))
            wap = bfv(A(offw, 512)).rearrange("p (a b) -> p a b", a=8)
            E.wait("tensor", "DW", tokw)
            tokt = None
            toks = None
            for pc in range(4):
                sl = P.acquire(1)
                psn = pt[:, sl[0] * 512: sl[0] * 512 + 512]
                for dd in range(8):
                    tokt = E.emit("tensor", lambda e, dd=dd, psn=psn, wap=wap, pc=pc: e.matmul(
                        psn, wap[:, dd, :], lnxb[:, dd, pc * 512:(pc + 1) * 512],
                        start=(dd == 0), stop=(dd == 7)), inc="T")
                E.wait("scalar", "T", tokt)
                toks = E.emit("scalar", lambda e, psn=psn, qbuf=qbuf, pc=pc, hp=hp: e.activation(
                    qbuf[:, pc * 512:(pc + 1) * 512], psn, AF.Relu,
                    bias=b_bq[:, hp:hp + 1], scale=0.125), inc="S")
                P.release(sl, "S", toks)
            wring.consumed(j, "T", tokt)
            qlast[par] = toks

        def emit_pass1(h):
            hp, hh, par = h // 2, h % 2, (h // 2) % 2
            qbuf = bfv(A(oc_qp + par * 1024, 1024))
            E.wait("tensor", "S", qlast[par])
            nmx = 0
            tokv = None
            for qc in range(16):
                span = (qc + 1) * 128
                for pc in range((span + 511) // 512):
                    c0 = pc * 512
                    c1 = min(c0 + 512, span)
                    sl = P.acquire(1)
                    psn = pt[:, sl[0] * 512: sl[0] * 512 + (c1 - c0)]
                    tokt = E.emit("tensor", lambda e, psn=psn, qbuf=qbuf, hh=hh, hp=hp, qc=qc, c0=c0, c1=c1: e.matmul(
                        psn, qbuf[hh * 64:hh * 64 + 64, qc * 128:(qc + 1) * 128],
                        Kb[hh * 64:hh * 64 + 64, hp, c0:c1],
                        start=True, stop=True), inc="T")
                    E.wait("vector", "T", tokt)
                    tokv = E.emit("vector", lambda e, psn=psn, nmx=nmx, h=h: e.reduce_max(
                        mxc[h % 2][:, nmx:nmx + 1], psn, AX.X), inc="V")
                    P.release(sl, "V", tokv)
                    nmx += 1
            return nmx, tokv

        def emit_cchain(h, nmx, tokv):
            cc = ccol2[h % 2]
            bf_ = biasfm2[h % 2]
            E.wait("vector", "V", tokv)
            tokv = E.emit_s("vector", lambda e, nmx=nmx, h=h: e.reduce_max(
                cc[:, 0:1], mxc[h % 2][:, 0:nmx], AX.X))
            sl = P.acquire(1)
            pst = pt[0:1, sl[0] * 512: sl[0] * 512 + 128]
            E.wait("tensor", "V", tokv)
            tokt = E.emit("tensor", lambda e, pst=pst, cc=cc: e.transpose(
                pst, cc[:, 0:1], c_identf), inc="T")
            E.wait("vector", "T", tokt)
            tokv = E.emit_s("vector", lambda e, pst=pst, cc=cc: e.reduce_max(
                cc[0:1, 1:2], pst, AX.X))
            P.release(sl, "V", tokv)
            sl = P.acquire(1)
            psc = pt[:, sl[0] * 512: sl[0] * 512 + 128]
            E.wait("tensor", "V", tokv)
            tokt = E.emit("tensor", lambda e, psc=psc, cc=cc: e.matmul(
                psc[:, 0:1], c_ones1, cc[0:1, 1:2], start=True, stop=True), inc="T")
            E.wait("vector", "T", tokt)
            tokv = E.emit_s("vector", lambda e, psc=psc, bf_=bf_: e.tensor_tensor(
                bf_, negfm, psc[:, 0:1].broadcast_to([128, 16]), ALU.subtract))
            P.release(sl, "V", tokv)
            E.wait("scalar", "V", tokv)

        def emit_pass2_pv(h):
            hp, hh, par = h // 2, h % 2, (h // 2) % 2
            qbuf = bfv(A(oc_qp + par * 1024, 1024))
            bf_ = biasfm2[h % 2]
            pvsl = P.acquire(4)
            P.reserve(pvsl)
            pv = pt[0:65, pvsl[0] * 512: pvsl[0] * 512 + 2048]
            ptok = [None, None]
            tokpv = None
            for kc in range(16):
                q0 = kc * 128
                pc0 = q0 // 512
                pbuf = PTb[kc % 2]
                if ptok[kc % 2] is not None:
                    E.wait("scalar", "T", ptok[kc % 2])
                toks2 = None
                for pc in range(pc0, 4):
                    c0 = pc * 512
                    sl = P.acquire(1)
                    psn = pt[:, sl[0] * 512: sl[0] * 512 + 512]
                    tokt = E.emit("tensor", lambda e, psn=psn, qbuf=qbuf, hh=hh, hp=hp, kc=kc, c0=c0: e.matmul(
                        psn, Kb[hh * 64:hh * 64 + 64, hp, kc * 128:(kc + 1) * 128],
                        qbuf[hh * 64:hh * 64 + 64, c0:c0 + 512],
                        start=True, stop=True), inc="T")
                    E.wait("scalar", "T", tokt)
                    toks2 = E.emit("scalar", lambda e, psn=psn, pbuf=pbuf, c0=c0, kc=kc, bf_=bf_: e.activation(
                        pbuf[:, c0:c0 + 512], psn, AF.Exp, bias=bf_[:, kc:kc + 1]), inc="S")
                    P.release(sl, "S", toks2)
                E.wait("vector", "S", toks2)
                tokdv = E.emit("vector", lambda e, pbuf=pbuf, q0=q0: e.tensor_mul(
                    pbuf[:, q0:q0 + 128], pbuf[:, q0:q0 + 128], c_tril), inc="V")
                E.wait("tensor", "V", tokdv)
                for pc in range(pc0, 4):
                    c0 = max(pc * 512, q0)
                    c1 = (pc + 1) * 512
                    kc_last = min(15, (pc + 1) * 4 - 1)
                    tokpv = E.emit("tensor", lambda e, pv=pv, kc=kc, pbuf=pbuf, c0=c0, c1=c1, h=h, kl=kc_last: e.matmul(
                        pv[:, c0:c1],
                        Vb[:, kc, h * 65:(h + 1) * 65],
                        pbuf[:, c0:c1],
                        start=(kc == 0), stop=(kc == kl)), inc="T")
                ptok[kc % 2] = tokpv
            if hh == 1:
                qread[par] = tokpv
            return pvsl, pv, tokpv

        def emit_epilogue(h, pvsl, pv, tokpv):
            E.wait("vector", "T", tokpv)
            E.emit_s("vector", lambda e, pv=pv: e.scalar_tensor_tensor(
                srowC[0:1, 0:2048], negrow, float(1e38 / NEG), pv[64:65, :],
                ALU.mult, ALU.add))
            E.emit_s("vector", lambda e: e.tensor_scalar_max(
                srowC[0:1, 0:2048], srowC[0:1, 0:2048], 1e-35))
            tokv = E.emit_s("vector", lambda e: e.reciprocal(
                srowC[0:1, 0:2048], srowC[0:1, 0:2048]))
            E.wait("sync", "V", tokv)
            tokd = E.emit("sync", lambda e: e.dma_start(rowscr[0, :], srowC[0:1, 0:2048]), inc="DS")
            E.wait("sync", "DS", tokd)
            tokd2 = E.emit("sync", lambda e: e.dma_start(
                qrep[0:64, :], bass.AP(rowscr.tensor, 0, [[0, 64], [1, 2048]])), inc="DS")
            E.wait("vector", "DS", tokd2)
            tokv = E.emit("vector", lambda e, pv=pv: e.tensor_mul(
                qrep[0:64, :], qrep[0:64, :], pv[0:64, :]), inc="V")
            P.unreserve(pvsl)
            P.release(pvsl, "V", tokv)
            dst = lnxo[(h % 2) * 64:(h % 2) * 64 + 64, h // 2, :]
            E.wait("gpsimd", "V", tokv)
            tokg = E.emit("gpsimd", lambda e, dst=dst: e.dma_start(
                dst, qrep[0:64, :], accum_op=ALU.add), inc="DW")
            E.wait("sync", "DW", tokg)
            E.wait("vector", "DW", tokg)
            E.wait("tensor", "DW", tokg)

        emit_qpair(0)
        st_prev = None
        st_prev = emit_pass1(0)
        for h in range(16):
            emit_cchain(h, *st_prev)
            pvst = emit_pass2_pv(h)
            if h + 1 < 16:
                if (h + 1) % 2 == 0:
                    emit_qpair((h + 1) // 2)
                st_prev = emit_pass1(h + 1)
            emit_epilogue(h, *pvst)

        for c4 in range(4):
            t0 = c4 * 512
            xbf_s = bfv(A(oc_qp, 2048)).rearrange("p (a b) -> p a b", a=8)
            sq_s = bfv(A(oc_PT, 2048)).rearrange("p (a b) -> p a b", a=8)
            rl = A(oc_lnxbf, 4096).rearrange("p (a b) -> p a b", a=8)
            E.wait("sync", "V", E.tok("V"))
            E.wait("sync", "T", E.tok("T"))
            src_ap = bass.AP(strmspill.tensor, c4 * 512, [[16384, 128], [2048, 8], [1, 512]])
            dst_ap = bass.AP(strm2d.tensor, c4 * 512, [[16384, 128], [2048, 8], [1, 512]])
            tokd = E.emit("sync", lambda e, src_ap=src_ap: e.dma_start(rl, src_ap), inc="DS")
            E.wait("vector", "DS", tokd)
            lnret = layer_norm(lnxo[:, :, t0:t0 + 512], 512, 1e-8, xbf_s, sq_s,
                               gcol=b_gam, bcol=b_bet, add_to=rl, srow_t=srowC,
                               war_tok=E.tok("T"))
            E.wait("sync", "V", lnret)
            tokd = E.emit("sync", lambda e, dst_ap=dst_ap: e.dma_start(dst_ap, rl), inc="DS")
            E.wait("vector", "DS", tokd)

    PTb = [bfv(A(43136, 1024)), bfv(A(43136 + 1024, 1024))]
    bvrep_b = A(45184, 1024)

    if KPH >= 3:
        E.barrier()
        phaseC()
        E.barrier()

    def phaseD():
        od_lnxD = 0
        od_strm = 8192
        od_w13 = 24576
        od_w13b = 32768
        od_w2 = 40960
        od_w2b = 45056
        od_g = 49152
        od_cr = 50176
        od_sc = 51200
        lnxD = bfv(A(od_lnxD, 8192)).rearrange("p (a b) -> p a b", a=8)
        acc = A(od_strm, 16384).rearrange("p (a b) -> p a b", a=8)
        gbuf = bfv(A(od_g, 1024)).rearrange("p (a b) -> p a b", a=8)
        crep = bfv(A(od_cr, 1024))
        s1scr = bfv(A(od_sc, 128))
        h3scr = bfv(A(od_sc + 128, 128))
        lgD = A(od_sc + 256, 64)
        g8b = bfv(A(od_sc + 320, 32))
        gb8s = A(od_sc + 352, 8)
        cfm_st = bfv(A(od_g, 1024))[0:8, 0:2048]
        outst = A(od_cr, 1024)
        srowD = arena_t[0:1, od_w13 + 4096:od_w13 + 4096 + 2048]

        g8b3 = g8b[:, 0:64].rearrange("p (a b) -> p a b", a=8)
        tokg8 = E.emit("gpsimd", lambda e: e.dma_start(
            g8b3, g8T.rearrange("(a p) b -> p a b", p=128)), inc="DW")
        tokb8 = E.emit("sync", lambda e: e.dma_start(gb8s, gb8), inc="DS")
        tokrl = None
        for c4 in range(4):
            tokrl = E.emit("sync", lambda e, c4=c4: e.dma_start(
                A(od_strm + c4 * 4096, 4096), strm2d[:, c4 * 4096:(c4 + 1) * 4096]), inc="DS")
        E.wait("vector", "DS", tokrl)
        E.wait("tensor", "DS", tokrl)
        for c4 in range(4):
            t0 = c4 * 512
            xbf_s = bfv(A(od_w13, 2048)).rearrange("p (a b) -> p a b", a=8)
            sq_s = bfv(A(od_w13 + 2048, 2048)).rearrange("p (a b) -> p a b", a=8)
            layer_norm(acc[:, :, t0:t0 + 512], 512, 1e-5, xbf_s, sq_s,
                       out_bf=lnxD[:, :, t0:t0 + 512], srow_t=srowD)
        if DBG:
            E.barrier(("sync",))
            E.emit("sync", lambda e: e.dma_start(dbg["d_l"], bfv(A(od_lnxD, 8192))), inc="DS")
            E.barrier(("sync",))
        E.wait("tensor", "DW", tokg8)
        E.wait("vector", "DS", tokb8)
        g8v = g8b[:, 0:8].rearrange("p (a b) -> p a b", a=1) if False else None
        tokcf = None
        for tc in range(16):
            sl = P.acquire(1)
            psl = pt[:, sl[0] * 512: sl[0] * 512 + 8]
            E.wait("tensor", "V", E.tok("V"))
            tokt = None
            for dd in range(8):
                tokt = E.emit("tensor", lambda e, dd=dd, psl=psl, tc=tc: e.matmul(
                    psl[:, 0:8], lnxD[:, dd, tc * 128:(tc + 1) * 128],
                    g8b3[:, dd, :],
                    start=(dd == 0), stop=(dd == 7)), inc="T")
            E.wait("vector", "T", tokt)
            E.wait("scalar", "T", tokt)
            E.emit_s("vector", lambda e, psl=psl: e.reduce_max(
                lgD[:, 48:49], psl[:, 0:8], AX.X, negate=True))
            E.wait("scalar", "V", E.tok("V"))
            toke = E.emit("scalar", lambda e, psl=psl: e.activation(
                lgD[:, 0:8], psl[:, 0:8], AF.Exp, bias=lgD[:, 48:49],
                accum_out=lgD[:, 49:50]), inc="S")
            E.wait("vector", "S", toke)
            E.emit_s("vector", lambda e: e.reciprocal(lgD[:, 50:51], lgD[:, 49:50]))
            E.emit_s("vector", lambda e: e.tensor_scalar(
                lgD[:, 0:8], lgD[:, 0:8], lgD[:, 50:51], None, op0=ALU.mult))
            E.emit_s("vector", lambda e: e.tensor_add(lgD[:, 8:16], lgD[:, 0:8], gb8s))
            E.emit_s("vector", lambda e: e.reduce_max(lgD[:, 51:52], lgD[:, 8:16], AX.X))
            E.emit_s("vector", lambda e: e.tensor_tensor(
                lgD[:, 16:24], lgD[:, 8:16], lgD[:, 51:52].broadcast_to([128, 8]), ALU.is_ge))
            E.emit_s("vector", lambda e: e.tensor_tensor_scan(
                lgD[:, 24:32], c_onesf4[:, 0:1].broadcast_to([128, 8]), lgD[:, 16:24],
                0.0, ALU.mult, ALU.add))
            E.emit_s("vector", lambda e: e.tensor_scalar(
                lgD[:, 24:32], lgD[:, 24:32], 1.0, None, op0=ALU.is_le))
            E.emit_s("vector", lambda e: e.tensor_mul(lgD[:, 24:32], lgD[:, 24:32], lgD[:, 16:24]))
            E.emit_s("vector", lambda e: e.scalar_tensor_tensor(
                lgD[:, 8:16], lgD[:, 24:32], -1e30, lgD[:, 8:16], ALU.mult, ALU.add))
            E.emit_s("vector", lambda e: e.reduce_max(lgD[:, 51:52], lgD[:, 8:16], AX.X))
            E.emit_s("vector", lambda e: e.tensor_tensor(
                lgD[:, 16:24], lgD[:, 8:16], lgD[:, 51:52].broadcast_to([128, 8]), ALU.is_ge))
            E.emit_s("vector", lambda e: e.tensor_tensor_scan(
                lgD[:, 32:40], c_onesf4[:, 0:1].broadcast_to([128, 8]), lgD[:, 16:24],
                0.0, ALU.mult, ALU.add))
            E.emit_s("vector", lambda e: e.tensor_scalar(
                lgD[:, 32:40], lgD[:, 32:40], 1.0, None, op0=ALU.is_le))
            E.emit_s("vector", lambda e: e.tensor_mul(lgD[:, 32:40], lgD[:, 32:40], lgD[:, 16:24]))
            E.emit_s("vector", lambda e: e.tensor_mul(lgD[:, 40:48], lgD[:, 24:32], lgD[:, 0:8]))
            E.emit_s("vector", lambda e: e.reduce_sum(lgD[:, 52:53], lgD[:, 40:48], AX.X))
            E.emit_s("vector", lambda e: e.tensor_mul(lgD[:, 40:48], lgD[:, 32:40], lgD[:, 0:8]))
            E.emit_s("vector", lambda e: e.reduce_sum(lgD[:, 53:54], lgD[:, 40:48], AX.X))
            E.emit_s("vector", lambda e: e.tensor_scalar(
                lgD[:, 40:48], lgD[:, 24:32], lgD[:, 52:53], None, op0=ALU.mult))
            E.emit_s("vector", lambda e: e.scalar_tensor_tensor(
                lgD[:, 32:40], lgD[:, 32:40], lgD[:, 53:54], lgD[:, 40:48],
                ALU.mult, ALU.add))
            tokv = E.emit_s("vector", lambda e: e.engine_nop())
            P.release(sl, "V", tokv)
            sl2 = P.acquire(1)
            pst = pt[0:8, sl2[0] * 512: sl2[0] * 512 + 128]
            E.wait("tensor", "V", tokv)
            tokt2 = E.emit("tensor", lambda e, pst=pst: e.transpose(
                pst, lgD[:, 32:40], c_identf), inc="T")
            E.wait("vector", "T", tokt2)
            tokcf = E.emit("vector", lambda e, pst=pst, tc=tc: e.tensor_copy(
                cfm_st[0:8, tc * 128:(tc + 1) * 128], pst), inc="V")
            P.release(sl2, "V", tokcf)
        E.wait("sync", "V", tokcf)
        tokcd = E.emit("sync", lambda e: e.dma_start(cfmscr[:, :], cfm_st), inc="DS")
        E.wait("sync", "DS", tokcd)
        E.wait("vector", "DS", tokcd)
        E.wait("scalar", "DS", tokcd)
        if DBG:
            E.emit("sync", lambda e: e.dma_start(dbg["d_r"], cfm_st), inc="DS")
            E.barrier(("sync",))

        gbuf2 = bfv(A(od_g, 2048)).rearrange("p (a b) -> p a b", a=8)
        s1s = bfv(A(od_sc, 256))
        crepS = bfv(A(od_sc + 256, 256))

        def w13_dma(step):
            e8, half = step // 2, step % 2
            buf = od_w13 if step % 2 == 0 else od_w13b
            toks = []
            for m in range(2):
                dst = bfv(A(buf + m * 4096, 4096)).rearrange("p (a b) -> p a b", a=8)
                toks.append(E.emit("gpsimd", lambda e, m=m, e8=e8, half=half, dst=dst: e.dma_start(
                    dst, W13[m, e8, :, half * 1024:(half + 1) * 1024].rearrange(
                        "(a p) b -> p a b", p=128)), inc="DW"))
            buf2 = od_w2 if step % 2 == 0 else od_w2b
            dst2 = bfv(A(buf2, 4096)).rearrange("p (a b) -> p a b", a=8)
            tok2 = E.emit("gpsimd", lambda e, e8=e8, half=half, dst2=dst2: e.dma_start(
                dst2, W2[e8, half * 1024:(half + 1) * 1024, :].rearrange(
                    "(a p) b -> p a b", p=128)), inc="DW")
            return toks[-1], tok2

        E.wait("gpsimd", "T", E.tok("T"))
        E.wait("gpsimd", "V", E.tok("V"))
        wtoks = {0: w13_dma(0)}
        wfree = {}
        for e8 in range(9):
            shared = (e8 == 8)
            for half in range(2):
                step = e8 * 2 + half
                if step + 1 < 18:
                    if step - 1 in wfree:
                        E.wait("gpsimd", "T", wfree[step - 1])
                    wtoks[step + 1] = w13_dma(step + 1)
                tok13, tok2w = wtoks[step]
                buf = od_w13 if step % 2 == 0 else od_w13b
                buf2 = od_w2 if step % 2 == 0 else od_w2b
                w1v = bfv(A(buf, 4096)).rearrange("p (a b) -> p a b", a=8)
                w3v = bfv(A(buf + 4096, 4096)).rearrange("p (a b) -> p a b", a=8)
                w2v = bfv(A(buf2, 4096)).rearrange("p (a b) -> p a b", a=8)
                E.wait("tensor", "DW", tok13)
                lastT = None
                for sect in range(4):
                    s0 = sect * 512
                    if not shared:
                        E.wait("sync", "V", E.tok("V"))
                        tokcr = E.emit("sync", lambda e, e8=e8, s0=s0: e.dma_start(
                            crepS, bass.AP(cfmscr.tensor, e8 * 2048 + s0,
                                           [[0, 128], [1, 512]])), inc="DS")
                        E.wait("vector", "DS", tokcr)
                    gtokv = None
                    for ii in range(8):
                        sl1 = P.acquire(1)
                        ph1 = pt[:, sl1[0] * 512: sl1[0] * 512 + 512]
                        tokt = None
                        for kk in range(8):
                            tokt = E.emit("tensor", lambda e, ph1=ph1, w1v=w1v, kk=kk, ii=ii, s0=s0: e.matmul(
                                ph1, w1v[:, kk, ii * 128:(ii + 1) * 128],
                                lnxD[:, kk, s0:s0 + 512],
                                start=(kk == 0), stop=(kk == 7)), inc="T")
                        sl3 = P.acquire(1)
                        ph3 = pt[:, sl3[0] * 512: sl3[0] * 512 + 512]
                        tokt3 = None
                        for kk in range(8):
                            tokt3 = E.emit("tensor", lambda e, ph3=ph3, w3v=w3v, kk=kk, ii=ii, s0=s0: e.matmul(
                                ph3, w3v[:, kk, ii * 128:(ii + 1) * 128],
                                lnxD[:, kk, s0:s0 + 512],
                                start=(kk == 0), stop=(kk == 7)), inc="T")
                        E.wait("scalar", "T", tokt)
                        if gtokv is not None:
                            E.wait("scalar", "V", gtokv)
                        toks1 = E.emit("scalar", lambda e, ph1=ph1: e.activation(
                            s1s[:, 0:512], ph1, AF.Silu), inc="S")
                        P.release(sl1, "S", toks1)
                        E.wait("vector", "S", toks1)
                        if not shared:
                            E.emit("vector", lambda e: e.tensor_mul(
                                s1s[:, 0:512], s1s[:, 0:512], crepS[:, 0:512]))
                        E.wait("vector", "T", tokt3)
                        gtokv = E.emit("vector", lambda e, ph3=ph3, ii=ii: e.tensor_mul(
                            gbuf2[:, ii, :], s1s[:, 0:512], ph3), inc="V")
                        P.release(sl3, "V", gtokv)
                    if DBG and e8 == 0 and half == 0 and sect == 0:
                        E.barrier(("sync",))
                        E.emit("sync", lambda e: e.dma_start(dbg["d_g"], bfv(A(od_g, 2048))), inc="DS")
                        E.barrier(("sync",))
                    E.wait("tensor", "DW", tok2w)
                    E.wait("tensor", "V", gtokv)
                    for ddg in range(2):
                        osl = P.acquire(4)
                        P.reserve(osl)
                        ops = pt[:, osl[0] * 512: osl[0] * 512 + 2048]
                        tokw2 = None
                        for dd4 in range(4):
                            dd = ddg * 4 + dd4
                            for ii in range(8):
                                tokw2 = E.emit("tensor", lambda e, ops=ops, w2v=w2v, dd=dd, dd4=dd4, ii=ii: e.matmul(
                                    ops[:, dd4 * 512:dd4 * 512 + 512],
                                    w2v[:, ii, dd * 128:(dd + 1) * 128],
                                    gbuf2[:, ii, :],
                                    start=(ii == 0), stop=(ii == 7)), inc="T")
                        lastT = tokw2
                        E.wait("vector", "T", tokw2)
                        tokacc = E.emit("vector", lambda e, ops=ops, s0=s0, ddg=ddg: e.tensor_add(
                            acc[:, ddg * 4:(ddg + 1) * 4, s0:s0 + 512],
                            acc[:, ddg * 4:(ddg + 1) * 4, s0:s0 + 512],
                            ops.rearrange("p (a b) -> p a b", a=4)), inc="V")
                        P.unreserve(osl)
                        P.release(osl, "V", tokacc)
                wfree[step] = lastT
        E.wait("tensor", "V", E.tok("V"))
        outb2 = A(od_g, 1024)
        for tt_ in range(16):
            tokvo = None
            for dd in range(8):
                slo = P.acquire(1)
                apo = pt[:, slo[0] * 512: slo[0] * 512 + 128]
                tokto = E.emit("tensor", lambda e, apo=apo, dd=dd, tt_=tt_: e.transpose(
                    apo, acc[:, dd, tt_ * 128:(tt_ + 1) * 128], c_identf), inc="T")
                E.wait("vector", "T", tokto)
                tokvo = E.emit("vector", lambda e, apo=apo, dd=dd: e.tensor_copy(
                    outb2[:, dd * 128:(dd + 1) * 128], apo), inc="V")
                P.release(slo, "V", tokvo)
            E.wait("sync", "V", tokvo)
            tokdo = E.emit("sync", lambda e, tt_=tt_: e.dma_start(
                out[tt_ * 128:(tt_ + 1) * 128, :], outb2), inc="DS")
            E.wait("vector", "DS", tokdo)
            E.wait("tensor", "DS", tokdo)

    if KPH >= 4:
        E.barrier()
        phaseD()
        E.barrier()
    else:
        if KPH >= 3:
            for c4 in range(4):
                tokd = E.emit("sync", lambda e, c4=c4: e.dma_start(
                    A(c4 * 4096, 4096), strm2d[:, c4 * 4096:(c4 + 1) * 4096]), inc="DS")
            E.wait("sync", "DS", E.tok("DS"))
            E.wait("tensor", "DS", E.tok("DS"))
            E.wait("vector", "DS", E.tok("DS"))
            E.barrier()
        emit_out_from_strm(nc, E, P, pt, A, strm, c_identf, out)

    E.barrier()
    E.replay()
    stack.close()
    return nc


def emit_out_from_strm(nc, E, P, pt, A, strm, c_identf, out):
    o_ost = 16384
    ost = A(o_ost, 1024)
    for tt_ in range(16):
        tokv = None
        for dd in range(8):
            slots = P.acquire(1)
            apo = pt[:, slots[0] * 512: slots[0] * 512 + 128]
            tokt = E.emit("tensor", lambda e, apo=apo, dd=dd, tt_=tt_: e.transpose(
                apo, strm[:, dd, tt_ * 128:(tt_ + 1) * 128], c_identf), inc="T")
            E.wait("vector", "T", tokt)
            tokv = E.emit("vector", lambda e, apo=apo, dd=dd: e.tensor_copy(
                ost[:, dd * 128:(dd + 1) * 128], apo), inc="V")
            P.release(slots, "V", tokv)
        E.wait("sync", "V", tokv)
        tokd = E.emit("sync", lambda e, tt_=tt_: e.dma_start(
            out[tt_ * 128:(tt_ + 1) * 128, :], ost), inc="DS")
        E.wait("vector", "DS", tokd)
    E.wait("sync", "DS", E.tok("DS"))


def build_C(*a, **k):
    raise NotImplementedError


def build_D(*a, **k):
    raise NotImplementedError


_BF = ml_dtypes.bfloat16


def _fm(v):
    return np.ascontiguousarray(np.asarray(v, np.float32).reshape(8, 128).T)


_NC_CACHE = {}


def _get_nc():
    key = (KPH, DBG)
    if key not in _NC_CACHE:
        _NC_CACHE[key] = build_nc()
    return _NC_CACHE[key]


def kernel(**inp):
    f32 = np.float32
    g = {k: np.asarray(v) for k, v in inp.items()}
    im = {}
    im["cw"] = np.ascontiguousarray(
        g["L0_attn_conv_w"].reshape(1024, 4).reshape(8, 128, 4).transpose(1, 0, 2)).astype(f32)
    im["cb"] = _fm(g["L0_attn_conv_b"])
    im["Afm"] = np.ascontiguousarray(
        g["L0_attn_A"].reshape(8, 128, 16).transpose(1, 0, 2).reshape(128, 128)).astype(f32)
    im["Dfm"] = _fm(g["L0_attn_D"])
    im["bpwT"] = np.ascontiguousarray(g["L0_attn_Bp_w"].T).astype(f32)
    im["cpwT"] = np.ascontiguousarray(g["L0_attn_Cp_w"].T).astype(f32)
    im["bcb2"] = np.ascontiguousarray(
        np.stack([g["L0_attn_Bp_b"], g["L0_attn_Cp_b"]], axis=1)).astype(f32)
    im["gwT"] = np.ascontiguousarray(g["L0_attn_gate_w"].T).astype(f32)
    im["owT"] = np.ascontiguousarray(g["L0_attn_out_w"].T).astype(f32)
    im["gb"] = _fm(g["L0_attn_gate_b"])
    im["ob"] = _fm(g["L0_attn_out_b"])
    im["lng"] = _fm(g["L0_attn_ln_g"])
    im["lnb"] = _fm(g["L0_attn_ln_b"])
    im["wr"] = np.ascontiguousarray(g["L0_ffn_W_router"]).astype(f32)
    im["w1"] = np.ascontiguousarray(g["L0_ffn_W1"]).astype(f32)
    im["b1fm"] = np.ascontiguousarray(
        g["L0_ffn_b1"].reshape(4, 32, 128).transpose(2, 0, 1).reshape(128, 128)).astype(f32)
    im["woB"] = np.ascontiguousarray(
        g["L0_ffn_Wo"].reshape(4096, 8, 128).transpose(1, 0, 2)).astype(f32)
    im["bofm"] = _fm(g["L0_ffn_bo"])
    im["g2"] = _fm(g["L0_ffn_ln_gamma"])
    im["b2"] = _fm(g["L0_ffn_ln_beta"])
    im["wq"] = np.ascontiguousarray(g["L1_attn_Wq"]).astype(f32)
    im["wk"] = np.ascontiguousarray(g["L1_attn_Wk"]).astype(f32)
    im["wv"] = np.ascontiguousarray(g["L1_attn_Wv"]).astype(f32)
    im["bqfm"] = _fm(g["L1_attn_bq"] / 8.0) if False else _fm(np.asarray(g["L1_attn_bq"], f32) * 0.125)
    im["bkfm"] = _fm(g["L1_attn_bk"])
    im["bvfm"] = _fm(g["L1_attn_bv"])
    im["bvrep"] = np.ascontiguousarray(
        np.broadcast_to(np.asarray(g["L1_attn_bv"], f32), (128, 1024)))
    im["gamv"] = _fm(g["L1_attn_gamma"])
    im["betv"] = _fm(g["L1_attn_beta"])
    im["g8T"] = np.ascontiguousarray(g["L1_ffn_gate_w"].T).astype(f32)
    im["gb8"] = np.ascontiguousarray(
        np.broadcast_to(np.asarray(g["L1_ffn_gate_bias"], f32), (128, 8))).astype(f32)
    w1T = np.asarray(g["L1_ffn_w1"], f32).transpose(0, 2, 1)
    w3T = np.asarray(g["L1_ffn_w3"], f32).transpose(0, 2, 1)
    sw1T = np.asarray(g["L1_ffn_sw1"], f32).T[None]
    sw3T = np.asarray(g["L1_ffn_sw3"], f32).T[None]
    W13 = np.ascontiguousarray(np.stack([
        np.concatenate([w1T, sw1T], 0), np.concatenate([w3T, sw3T], 0)], 0))
    im["W13"] = W13
    w2T = np.asarray(g["L1_ffn_w2"], f32).transpose(0, 2, 1)
    sw2T = np.asarray(g["L1_ffn_sw2"], f32).T[None]
    im["W2"] = np.ascontiguousarray(np.concatenate([w2T, sw2T], 0))
    im["identf"] = np.eye(128, dtype=f32)
    tril = (np.arange(128)[None, :] >= np.arange(128)[:, None]).astype(_BF)
    im["trilT"] = np.ascontiguousarray(tril)
    im["ones1"] = np.ones((1, 128), f32)

    nc = _get_nc()
    xs = np.asarray(g["L0_attn_x"], f32)
    in_maps = [dict(im, x=np.ascontiguousarray(xs[b])) for b in range(8)]
    res = run_bass_kernel_spmd(nc, in_maps, core_ids=list(range(8)))
    outp = np.stack([np.asarray(r["out"]) for r in res.results], 0)
    if DBG:
        kernel._dbg = [{k2: np.asarray(v2) for k2, v2 in r.items() if k2 != "out"}
                       for r in res.results]
    return outp
